# version 11
# speedup vs baseline: 1.1759x; 1.0117x over previous
import numpy as np
import concourse.bass as bass
import concourse.mybir as mybir
from concourse.bass_utils import run_bass_kernel_spmd

F32 = mybir.dt.float32
BF16 = mybir.dt.bfloat16
AF = mybir.ActivationFunctionType
ALU = mybir.AluOpType
AX = mybir.AxisListType

D = 1024
OWN = 2048
T = 5120
CH = 512
HEADS = 8
HID = 2730
NHC = 22
HIDP = NHC * 128
QK_SCALE = 8.0
EPS_RET = 1e-6
BIG = 1.0e30

RET_LO, L1A_LO, L1F_LO, L2A_LO, L2F_LO = 512, 1536, 1920, 2560, 3072
T_TAGS = []
CUR_TAG = ["init"]
TAB_LO, TABW = 1024, 4096
AB = 1016
ARENA_W = 4112
VROW_W = 3456
VNEG_W = 3200
NPHASES = 5
SUBB = 3


def build_nc():
    nc = bass.Bass("TRN2", target_bir_lowering=False)

    dr = {}
    for n, shp in [
        ("xh", [8, 128, T]), ("wq", [D, D]), ("wk", [D, D]), ("wv", [D, D]),
        ("bqc", [128, 8]), ("bkc", [128, 8]), ("bvr", [1, D]),
        ("wqkv", [2, D, 1536]), ("wo", [2, 512, D]),
        ("w1", [2, D, 2 * HID]), ("w2p", [2, HIDP, D]),
        ("b1a", [2, 128, NHC]), ("b1g", [2, 128, NHC]), ("b2c", [2, 128, 8]),
        ("lgc", [2, 128, 8]), ("lbc", [2, 128, 8]),
        ("qsc", [2, 128, 1]), ("qspc", [2, 128, 1]),
        ("ksc", [2, 128, 1]), ("kspc", [2, 128, 1]),
        ("crow", [1, D]), ("wmod", [D, 6 * D]), ("gam", [1, 1]),
        ("tabs", [8, 128, TABW]),
        ("validr", [64, VROW_W]), ("validc", [128, T // 128]),
        ("ident", [128, 128]), ("rotm", [128, 128]),
        ("m_up", [128, 128]), ("m_lo", [128, 128]), ("tri01", [128, 128]),
        ("bd2", [128, 2]), ("sel2", [2, 128]),
    ]:
        dr[n] = nc.dram_tensor(n, shp, F32, kind="ExternalInput")
    out_d = nc.dram_tensor("out", [8, 128, OWN], F32, kind="ExternalOutput")
    xa_d = nc.dram_tensor("xa_scr", [8, 128, OWN], F32, kind="Internal")

    def ap(n):
        return dr[n].ap()

    SB_BASE = 17408

    def at(name, shape, dtype, off):
        return nc.alloc_sbuf_tensor_at(name, shape, dtype, offset=SB_BASE + off)

    arena = [at(f"arena{f}", [128, ARENA_W], BF16, f * ARENA_W * 2)
             for f in range(8)]
    B0 = 8 * ARENA_W * 2
    wqT = [at(f"wqT{i}", [128, 1024], BF16, B0 + i * 2048) for i in range(8)]
    wkT = [at(f"wkT{i}", [128, 1024], BF16, B0 + 16384 + i * 2048) for i in range(8)]
    wvT = [at(f"wvT{i}", [128, 1024], BF16, B0 + 32768 + i * 2048) for i in range(8)]
    kring = [[at(f"krk{c}_{i}", [128, 512], BF16, B0 + 49152 + (c * 8 + i) * 1024)
              for i in range(8)] for c in range(2)]
    vring = [[at(f"vrk{c}_{i}", [128, 1024], BF16, B0 + 65536 + (c * 4 + i) * 2048)
              for i in range(4)] for c in range(2)]
    bvb = [at(f"bvb{i}", [128, 512], BF16, B0 + 81920 + i * 1024) for i in range(2)]
    gpow = at("gpow", [1, 512], F32, B0 + 83968)
    tabs = [at(f"tab{i}", [128, TABW], BF16, B0 + i * TABW * 2) for i in range(4)]
    wqkv_sb = [at(f"wqkv{i}", [128, 1536], BF16, B0 + 32768 + i * 3072)
               for i in range(8)]
    wo_sb = [at(f"wo{i}", [128, 1024], BF16, B0 + 57344 + i * 2048) for i in range(4)]
    khat = [[at(f"khat{c}_{i}", [128, 512], BF16, B0 + 65536 + (c * 4 + i) * 1024)
             for i in range(4)] for c in range(2)]
    vtm = [[at(f"vtm{c}_{i}", [128, 512], BF16, B0 + 73728 + (c * 4 + i) * 1024)
            for i in range(4)] for c in range(2)]
    w1_sb = [at(f"w1_{i}", [128, 5460], BF16, B0 + i * 10944) for i in range(8)]

    C0 = B0 + 87552
    _cc = [C0]

    def cal(name, shape, dtype):
        t = at(name, shape, dtype, _cc[0])
        per = int(np.prod(shape[1:])) * (2 if dtype == BF16 else 4)
        _cc[0] += (per + 31) // 32 * 32
        return t

    tmpA = cal("tmpA", [128, 512], F32)
    tmpB = cal("tmpB", [128, 512], F32)
    tmpC = cal("tmpC", [128, 512], F32)
    tmpCb = at("tmpCb", [128, 512], BF16, 8 * ARENA_W * 2 + 87552 + 4096)
    rows4 = cal("rows4", [64, 512], F32)
    rowaux = cal("rowaux", [64, 512], F32)
    denrow = cal("denrow", [64, 512], F32)
    vrow = cal("vrow", [64, VROW_W], BF16)
    identsb = cal("identsb", [128, 128], F32)
    onesc = cal("onesc", [64, 128], F32)
    ones128 = cal("ones128", [128, 1], F32)
    tri01_sb = cal("tri01", [128, 128], BF16)
    modc = cal("modc", [128, 48], F32)
    gac = cal("gac", [128, 8], F32)
    gbc = cal("gbc", [128, 8], F32)
    ca1c = cal("ca1c", [128, 8], F32)
    cb1c = cal("cb1c", [128, 8], F32)
    tgat = cal("tgat", [128, 8], F32)
    sgam = cal("sgam", [1, 1], F32)
    lng = cal("lng", [1, 1], F32)
    lngc = cal("lngc", [128, 1], F32)
    gnegc = cal("gnegc", [128, 4], F32)
    gprevc = cal("gprevc", [128, 4], F32)
    iotac = cal("iotac", [128, 4], F32)
    gmk8 = cal("gmk8", [128, 8], F32)
    gmk = [gmk8[:, i:i + 1] for i in range(8)]
    vcol = cal("vcol", [128, T // 128], F32)
    bq_sb = cal("bq", [128, 8], F32)
    bk_sb = cal("bk", [128, 8], F32)
    crow_sb = cal("crowsb", [128, 8], F32)
    b1a_sb = cal("b1a", [128, NHC], F32)
    b1g_sb = cal("b1g", [128, NHC], F32)
    b2_sb = cal("b2", [128, 8], F32)
    lg_sb = cal("lgcs", [128, 8], F32)
    lb_sb = cal("lbcs", [128, 8], F32)
    qs8_sb = cal("qs8", [128, 1], F32)
    qs8p_sb = cal("qs8p", [128, 1], F32)
    ks_sb = cal("ks", [128, 1], F32)
    ksp_sb = cal("ksp", [128, 1], F32)
    bd2_sb = cal("bd2", [128, 2], F32)
    sel2_sb = cal("sel2", [2, 128], F32)
    outp = cal("outp", [128, 128], F32)
    eps5 = cal("eps5", [2, 1], F32)
    eps24 = cal("eps24", [2, 1], F32)
    onescb = cal("onescb", [64, 256], BF16)
    ones128b = cal("ones128b", [128, 1], BF16)
    crow_bf = cal("crowbf", [128, 8], BF16)
    PB = 176832
    assert _cc[0] <= PB, _cc[0]
    xfm = [at(f"xfm{i}", [128, 512], F32, PB + i * 2048) for i in range(8)]
    hfm = [at(f"hfm{i}", [128, 512], BF16, PB + 16384 + i * 1024) for i in range(8)]
    Abuf = [at(f"Ab{i}", [128, 512], BF16, PB + 16384 + i * 1024) for i in range(8)]
    qfm = [at(f"qfm{i}", [128, 512], BF16, PB + 24576 + i * 1024) for i in range(8)]
    xstage = at("xstage", [128, 1024], F32, PB + 16384)
    wstage = at("wstage", [128, 1024], F32, PB)
    qhat = [at(f"qhat{i}", [128, 512], BF16, PB + i * 1024) for i in range(4)]
    obuf = [at(f"obuf{i}", [128, 512], BF16, PB + 4096 + i * 1024) for i in range(4)]
    qsbuf = at("qsbuf", [128, 512], BF16, PB + 8192)
    pbuf = [at(f"pbuf{i}", [128, 1536], BF16, PB + 9216 + i * 3072)
            for i in range(2)]
    vnegrow = at("vnegrow", [1, VNEG_W], BF16, PB + 15360)
    m_up_sb = at("m_up", [128, 128], BF16, PB + 21760)
    m_lo_sb = at("m_lo", [128, 128], BF16, PB + 22016)
    rotm_sb = at("rotm", [128, 128], BF16, PB + 22272)
    nrow4 = at("nrow4", [1, 512], BF16, PB + 30720)
    nmax = at("nmax", [128, 16], F32, PB + 31744)
    tablo = at("tablo", [128, TABW], BF16, PB + 22528)
    gated = [at(f"gat{i}", [128, 512], BF16, PB + i * 1024) for i in range(NHC)]
    lnh = [at(f"lnh{i}", [128, 512], BF16, PB + 22528 + i * 1024) for i in range(8)]
    w2r = at("w2r", [128, 1024], BF16, PB + 30720)
    gelbuf = at("gelbuf", [128, 512], BF16, PB + 32768)
    validB = at("validB", [128, 512], BF16, PB + 33792)
    xaring = tmpA
    assert 17408 + PB + 34816 <= 229344, PB

    pb = [nc.alloc_psum_tensor(f"pb{i}", [128, 512], F32) for i in range(8)]

    ops = []
    counts = {"t": 0, "v": 0, "s": 0, "g": 0, "gd": 0, "y": 0}

    def _add(eng, sem, amt, emit, waits):
        prev = counts[sem]
        counts[sem] += amt
        ws = [w for w in waits if w is not None]
        if sem in ("gd", "y") and prev > 0:
            ws.append((sem, prev, "~"))
        ops.append((eng, emit, ws, (sem, counts[sem], amt)))
        return (sem, counts[sem], eng)

    def Tm(emit, waits=()):
        T_TAGS.append(CUR_TAG[0])
        return _add("T", "t", 1, emit, waits)

    def Vm(emit, waits=()):
        return _add("V", "v", 1, emit, waits)

    def Sm(emit, waits=()):
        return _add("S", "s", 1, emit, waits)

    def Gm(emit, waits=()):
        return _add("G", "g", 1, emit, waits)

    def GD(emit, waits=()):
        return _add("G", "gd", 16, emit, waits)

    def Ym(emit, waits=()):
        return _add("Y", "y", 16, emit, waits)

    def barrier():
        return [(s, counts[s], "~") for s in counts if counts[s] > 0]

    bankR = [[] for _ in range(8)]
    bankW = [None] * 8

    def bacq(bs):
        w = []
        for b in bs:
            w += bankR[b]
            bankR[b] = []
        return w

    def bwr(bs, tok):
        for b in bs:
            bankW[b] = tok

    def brd(bs, tok):
        for b in bs:
            bankR[b].append(tok)
        return tok

    tl = {}
    for n, dst in [("validc", vcol), ("ident", identsb),
                   ("bd2", bd2_sb), ("sel2", sel2_sb),
                   ("gam", sgam), ("bqc", bq_sb), ("bkc", bk_sb)]:
        tl[n] = Ym(lambda y, d=dst, n=n: y.dma_start(d[:, :], ap(n)))
    for n, dst in [("validr", vrow), ("tri01", tri01_sb)]:
        tl[n] = GD(lambda g, d=dst, n=n: g.dma_start(d[:, :], ap(n)))
    tl["crow"] = Ym(lambda y: y.dma_start(
        crow_sb[:, :], ap("crow").rearrange("o (j p) -> (o p) j", p=128),
        allow_slow_non_contiguous=True))
    t_ones = Vm(lambda v: v.memset(onesc[:, :], 1.0))
    t_ones128 = Vm(lambda v: v.memset(ones128[:, :], 1.0))
    t_onesb = Vm(lambda v: v.memset(onescb[:, :], 1.0))
    t_eps5 = Vm(lambda v: v.memset(eps5[:, :], 1e-5))
    t_eps24 = Vm(lambda v: v.memset(eps24[:, :], 1e-24))
    t_ones128b = Vm(lambda v: v.memset(ones128b[:, :], 1.0))
    t_iota = Gm(lambda g: g.iota(iotac[:, :], [[128, 4]], channel_multiplier=1,
                                 allow_small_or_imprecise_dtypes=True))
    t_iotar = Gm(lambda g: g.iota(gpow[:, :], [[1, 512]], channel_multiplier=0,
                                  allow_small_or_imprecise_dtypes=True))

    t_lng = Sm(lambda s: s.activation(lng[:, :], sgam[:, :], AF.Ln),
               waits=[tl["gam"]])
    tk_ = Tm(lambda t: t.matmul(pb[5][:, 0:1], onesc[0:1, :], lng[:, :],
                                start=True, stop=True),
             waits=[t_lng, t_ones] + bacq([5]))
    bwr([5], tk_)
    t_lngc = brd([5], Vm(lambda v: v.tensor_copy(lngc[:, :], pb[5][:, 0:1]),
                         waits=[tk_]))
    t1 = Vm(lambda v: v.tensor_scalar(gpow[:, :], gpow[:, :], lng[0:1, 0:1], None,
                                      ALU.mult), waits=[t_iotar, t_lng])
    t_gpow = Sm(lambda s: s.activation(gpow[:, :], gpow[:, :], AF.Exp), waits=[t1])
    t1 = Vm(lambda v: v.tensor_scalar(gnegc[:, :], iotac[:, :], lngc[:, :], -1.0,
                                      ALU.mult, ALU.mult), waits=[t_iota, t_lngc])
    t_gneg = Sm(lambda s: s.activation(gnegc[:, :], gnegc[:, :], AF.Exp), waits=[t1])
    t1 = Vm(lambda v: v.tensor_scalar(tmpA[:, 0:1], lngc[:, :], 512.0, None,
                                      ALU.mult), waits=[t_lngc])
    t2 = Vm(lambda v: v.tensor_scalar(gprevc[:, :], iotac[:, :], lngc[:, :], -1.0,
                                      ALU.mult, ALU.mult), waits=[t_iota, t_lngc])
    t3 = Vm(lambda v: v.tensor_scalar(gprevc[:, :], gprevc[:, :], tmpA[:, 0:1],
                                      None, ALU.add), waits=[t1, t2])
    t_gprev = Sm(lambda s: s.activation(gprevc[:, :], gprevc[:, :], AF.Exp),
                 waits=[t3])
    t_b1 = Ym(lambda y: y.dma_start(tmpA[0:1, 0:512], ap("bvr")[:, 0:512]),
              waits=[t1, t3])
    t_b2 = Ym(lambda y: y.dma_start(tmpB[0:1, 0:512], ap("bvr")[:, 512:1024]))
    tm1 = Tm(lambda t: t.matmul(pb[5][:, 0:512], onesc[0:1, :], tmpA[0:1, 0:512],
                                start=True, stop=True),
             waits=[t_b1, t_ones] + bacq([5]))
    bwr([5], tm1)
    t_bvb0 = brd([5], Vm(lambda v: v.tensor_copy(bvb[0][:, :], pb[5][:, 0:512]),
                         waits=[tm1]))
    tm2 = Tm(lambda t: t.matmul(pb[5][:, 0:512], onesc[0:1, :], tmpB[0:1, 0:512],
                                start=True, stop=True),
             waits=[t_b2] + bacq([5]))
    bwr([5], tm2)
    t_bvb1 = brd([5], Vm(lambda v: v.tensor_copy(bvb[1][:, :], pb[5][:, 0:512]),
                         waits=[tm2]))

    t_silu = Sm(lambda s: s.activation(crow_bf[:, :], crow_sb[:, :], AF.Silu),
                waits=[tl["crow"]])
    prev_ld = None
    mm_last = None
    first_mod = True
    for cg in range(12):
        lo = cg * 512
        lds = []
        for kc in range(8):
            lds.append(GD(lambda g, kc=kc, lo=lo: g.dma_start(
                kring[0][kc][:, :], ap("wmod")[kc * 128:(kc + 1) * 128, lo:lo + 512]),
                waits=[prev_ld]))
        for j in range(4):
            vf = cg * 4 + j
            for kc in range(8):
                w = [lds[kc], t_silu, mm_last]
                if first_mod:
                    w += bacq([4])
                    first_mod = False
                mm_last = Tm(lambda t, kc=kc, j=j, vf=vf: t.matmul(
                    pb[4][:, vf:vf + 1], kring[0][kc][:, j * 128:(j + 1) * 128],
                    crow_bf[:, kc:kc + 1], start=(kc == 0), stop=(kc == 7)),
                    waits=w)
        prev_ld = mm_last
    bwr([4], mm_last)
    t_modc = brd([4], Vm(lambda v: v.tensor_copy(modc[:, :], pb[4][:, 0:48]),
                         waits=[mm_last]))
    t_ca1 = Vm(lambda v: v.tensor_scalar(ca1c[:, :], modc[:, 8:16], 1.0, None,
                                         ALU.add), waits=[t_modc])
    t_cb1 = Vm(lambda v: v.tensor_scalar(cb1c[:, :], modc[:, 32:40], 1.0, None,
                                         ALU.add), waits=[t_modc])

    def gtrans(dst, c0, w0):
        a1 = Sm(lambda s: s.activation(tgat[:, :], modc[:, c0:c0 + 8], AF.Abs),
                waits=[t_modc, w0])
        a2 = Vm(lambda v: v.tensor_scalar(tgat[:, :], tgat[:, :], 0.25, None,
                                          ALU.max), waits=[a1])
        a3 = Sm(lambda s: s.activation(dst[:, :], modc[:, c0:c0 + 8], AF.Sign),
                waits=[a2])
        return Vm(lambda v: v.tensor_mul(dst[:, :], dst[:, :], tgat[:, :]),
                  waits=[a3])
    t_ga = gtrans(gac, 16, None)
    t_gb = gtrans(gbc, 40, t_ga)

    tok_wT = t_modc
    ti = 0
    for wname, wdst in [("wq", wqT), ("wk", wkT), ("wv", wvT)]:
        for r in range(8):
            tld = Ym(lambda y, wn=wname, r=r: y.dma_start(
                wstage[:, :], ap(wn)[r * 128:(r + 1) * 128, :]),
                waits=[tok_wT])
            for c in range(8):
                bk_ = 6 + ti % 2
                ti += 1
                ttr = Tm(lambda t, c=c, bk_=bk_: t.transpose(
                    pb[bk_][:, 0:128], wstage[:, c * 128:(c + 1) * 128],
                    identsb[:, :]),
                    waits=[tld, tl["ident"]] + bacq([bk_]))
                bwr([bk_], ttr)
                tok_wT = brd([bk_], Vm(lambda v, wd=wdst, r=r, c=c, bk_=bk_:
                                       v.tensor_copy(
                    wd[c][:, r * 128:(r + 1) * 128], pb[bk_][:, 0:128]),
                    waits=[ttr]))

    def ln_stats2(x_src, x_waits, ones_lhs, ones_tok, extra_first=()):
        w5 = bacq([5])
        w4 = bacq([4])
        tsum = None
        for f in range(8):
            w = [x_waits[f], ones_tok, tsum] + \
                (list(extra_first) + w5 + w4 if f == 0 else [])
            tsum = Tm(lambda t, f=f: t.matmul(
                pb[5][0:1, 0:512], ones_lhs[:, :], x_src[f],
                start=(f == 0), stop=(f == 7)), waits=w)
        bwr([5], tsum)
        tsq = None
        for f in range(8):
            s1 = Sm(lambda s, f=f: s.activation(tmpCb[:, :], x_src[f],
                                                AF.Square),
                    waits=[x_waits[f], tsq])
            tsq = Tm(lambda t, f=f: t.matmul(
                pb[4][0:1, 0:512], ones128b[:, :], tmpCb[:, :],
                start=(f == 0), stop=(f == 7)),
                waits=[s1, t_ones128b] + ([tsum] if f == 0 else []))
        bwr([4], tsq)
        r1 = brd([5], Vm(lambda v: v.tensor_scalar(
            rowaux[0:1, :], pb[5][0:1, 0:512], 1.0 / D, None, ALU.mult),
            waits=[tsum, tsq]))
        r2 = brd([4], Vm(lambda v: v.tensor_scalar(
            rows4[0:1, :], pb[4][0:1, 0:512], 1.0 / D, None, ALU.mult),
            waits=[tsq]))
        r3 = brd([4], Vm(lambda v: v.tensor_mul(
            pb[4][0:1, 0:512], rowaux[0:1, :], rowaux[0:1, :]),
            waits=[r1, r2]))
        r4 = brd([4], Vm(lambda v: v.tensor_sub(
            rows4[0:1, :], rows4[0:1, :], pb[4][0:1, 0:512]), waits=[r3]))
        r5 = Sm(lambda s: s.activation(rows4[0:1, :], rows4[0:1, :], AF.Ln,
                                       bias=eps5[0:1, :]), waits=[r4, t_eps5])
        return Sm(lambda s: s.activation(rows4[0:1, :], rows4[0:1, :], AF.Exp,
                                         scale=-0.5), waits=[r5])

    def ln_broadcasts(t_inv, u0, do_valid):
        tiv = t_inv
        if do_valid:
            tiv = Vm(lambda v, u0=u0: v.tensor_mul(
                rows4[0:1, :], rows4[0:1, :], vrow[0:1, u0:u0 + 512]),
                waits=[t_inv, tl["validr"]])
        tb_ = Vm(lambda v: v.tensor_scalar(rowaux[0:1, :], rowaux[0:1, :], -1.0,
                                           None, ALU.mult), waits=[tiv])
        tb_ = Vm(lambda v: v.tensor_mul(rowaux[0:1, :], rowaux[0:1, :],
                                        rows4[0:1, :]), waits=[tb_])
        tB1 = Tm(lambda t: t.matmul(pb[6][:, 0:512], onesc[0:1, :],
                                    rows4[0:1, :], start=True, stop=True),
                 waits=[tiv, t_ones] + bacq([6]))
        bwr([6], tB1)
        tB2 = Tm(lambda t: t.matmul(pb[7][:, 0:512], onesc[0:1, :],
                                    rowaux[0:1, :], start=True, stop=True),
                 waits=[tb_] + bacq([7]))
        bwr([7], tB2)
        tVB = None
        if do_valid:
            tVB = Tm(lambda t, u0=u0: t.matmul(
                pb[4][:, 0:512], onescb[0:1, 0:128], vrow[0:1, u0:u0 + 512],
                start=True, stop=True), waits=[tiv, t_onesb] + bacq([4]))
            bwr([4], tVB)
        return tB1, tB2, tVB

    tok_y0 = [None] * 8
    tok_x_free = None
    tok_kv_free = [None, None]
    tok_k_ready = [[None] * 8, [None] * 8]
    tok_v_ready = [[None] * 4, [None] * 4]
    tok_xa_st = None
    t_abuf_free = None

    for ci in range(10 if NPHASES >= 1 else 0):
        CUR_TAG[0] = 'p1'
        u0 = ci * CH
        par = ci % 2
        pp = 1 - par
        txf = [None] * 8
        for f in range(8):
            txf[f] = Ym(lambda y, u0=u0, f=f: y.dma_start(
                xfm[f][:, 0:512], ap("xh")[f, :, u0:u0 + 512]),
                waits=[tok_x_free, txf[f - 1] if f else None,
                       tok_wT if ci == 0 and f == 0 else None])
        xtoks = list(txf)

        xsrc = [xfm[f][:, 0:512] for f in range(8)]
        t_inv = ln_stats2(xsrc, xtoks, ones128, t_ones128)
        tB1, tB2, _ = ln_broadcasts(t_inv, u0, False)
        th = [None] * 8
        for f in range(8):
            a1 = brd([6], Vm(lambda v, f=f: v.tensor_mul(
                tmpB[:, :], xfm[f][:, :], pb[6][:, 0:512]),
                waits=[tB1, xtoks[f], th[f - 1] if f else None]))
            a2 = brd([7], Vm(lambda v: v.tensor_add(
                tmpB[:, :], tmpB[:, :], pb[7][:, 0:512]), waits=[a1, tB2]))
            th[f] = Vm(lambda v, f=f: v.tensor_scalar(
                hfm[f][:, :], tmpB[:, :], ca1c[:, f:f + 1], modc[:, f:f + 1],
                ALU.mult, ALU.add), waits=[a2, t_ca1, t_modc])

        tq = [None] * 8
        tkk = [None] * 8
        for e in range(8):
            bk_ = e % 2
            mm = None
            for dc in range(8):
                w = [th[dc], mm]
                if dc == 0:
                    w += [tok_wT] + bacq([bk_])
                mm = Tm(lambda t, e=e, dc=dc, bk_=bk_: t.matmul(
                    pb[bk_][:, 0:512], wqT[dc][:, e * 128:(e + 1) * 128],
                    hfm[dc][:, :], start=(dc == 0), stop=(dc == 7)), waits=w)
            bwr([bk_], mm)
            tq[e] = brd([bk_], Sm(lambda s, e=e, bk_=bk_: s.activation(
                qfm[e][:, :], pb[bk_][:, 0:512], AF.Relu,
                bias=bq_sb[:, e:e + 1]), waits=[mm, tl["bqc"]]))
        for e in range(8):
            bk_ = e % 2
            mm = None
            for dc in range(8):
                w = [th[dc], mm]
                if dc == 0:
                    w += bacq([bk_])
                mm = Tm(lambda t, e=e, dc=dc, bk_=bk_: t.matmul(
                    pb[bk_][:, 0:512], wkT[dc][:, e * 128:(e + 1) * 128],
                    hfm[dc][:, :], start=(dc == 0), stop=(dc == 7)), waits=w)
            bwr([bk_], mm)
            tkk[e] = brd([bk_], Sm(lambda s, e=e, bk_=bk_, par=par: s.activation(
                kring[par][e][:, :], pb[bk_][:, 0:512], AF.Relu,
                bias=bk_sb[:, e:e + 1]),
                waits=[mm, tl["bkc"], tok_kv_free[par]]))
            tok_k_ready[par][e] = tkk[e]
        tv = [None] * 4
        for st in range(4):
            for half in range(2):
                bk_ = half
                mm = None
                for dc in range(8):
                    w = [th[dc], mm]
                    if dc == 0:
                        w += bacq([bk_])
                    mm = Tm(lambda t, st=st, half=half, dc=dc, bk_=bk_: t.matmul(
                        pb[bk_][:, 0:512], hfm[dc][:, st * 128:(st + 1) * 128],
                        wvT[dc][:, half * 512:(half + 1) * 512],
                        start=(dc == 0), stop=(dc == 7)), waits=w)
                bwr([bk_], mm)
                tv[st] = brd([bk_], Vm(lambda v, st=st, half=half, bk_=bk_,
                                       par=par: v.tensor_add(
                    vring[par][st][:, half * 512:(half + 1) * 512],
                    pb[bk_][:, 0:512], bvb[half][:, :]),
                    waits=[mm, tok_kv_free[par], t_bvb0, t_bvb1, tv[st]]))
            tok_v_ready[par][st] = tv[st]

        if ci == 0:
            tok_x_free = th[7]
            t_abuf_free = tv[3]
            continue

        tgm = [None] * 8
        for jp in range(8):
            jt = jp % 4
            uj = (u0 - CH if jp < 4 else u0) + jt * 128
            gcol = gprevc if jp < 4 else gnegc
            tgm[jp] = Vm(lambda v, jp=jp, jt=jt, uj=uj, gcol=gcol: v.tensor_mul(
                gmk[jp], gcol[:, jt:jt + 1],
                vcol[:, uj // 128:uj // 128 + 1]),
                waits=[t_gneg, t_gprev, tl["validc"]])
        tA = [None] * 8
        for jp in range(8):
            sp = pp if jp < 4 else par
            jt = jp % 4
            bk_ = 2 + jp % 2
            mm = None
            for ec in range(8):
                w = [tq[ec], tok_k_ready[sp][ec], mm]
                if ec == 0:
                    w += bacq([bk_])
                mm = Tm(lambda t, jt=jt, ec=ec, bk_=bk_, sp=sp: t.matmul(
                    pb[bk_][:, 0:512], kring[sp][ec][:, jt * 128:(jt + 1) * 128],
                    qfm[ec][:, :], start=(ec == 0), stop=(ec == 7)), waits=w)
            bwr([bk_], mm)
            ilo = 0 if jp < 4 else jt * 128
            ev = brd([bk_], Vm(lambda v, jp=jp, bk_=bk_: v.tensor_scalar(
                Abuf[jp][:, 0:512], pb[bk_][:, 0:512], gmk[jp], None,
                ALU.mult), waits=[mm, tgm[jp]]))
            if jp >= 4:
                ev = Vm(lambda v, jp=jp, jt=jt: v.tensor_mul(
                    Abuf[jp][:, jt * 128:(jt + 1) * 128],
                    Abuf[jp][:, jt * 128:(jt + 1) * 128], tri01_sb[:, :]),
                    waits=[ev, tl["tri01"]])
            tA[jp] = ev
        tden = None
        w0 = bacq([4])
        for jp in range(8):
            ilo = 0 if jp < 4 else (jp % 4) * 128
            tden = Tm(lambda t, jp=jp, ilo=ilo: t.matmul(
                pb[4][0:1, ilo:512], ones128b[:, :], Abuf[jp][:, ilo:512],
                start=(jp == 0), stop=(jp == 7)),
                waits=[tA[jp], tden, t_ones128b] + (w0 if jp == 0 else []))
        bwr([4], tden)
        rr = brd([4], Vm(lambda v: v.tensor_mul(
            denrow[0:1, :], pb[4][0:1, 0:512], gpow[:, :]),
            waits=[tden, t_gpow]))
        rr = Vm(lambda v: v.tensor_scalar(denrow[0:1, :], denrow[0:1, :],
                                          EPS_RET, None, ALU.add), waits=[rr])
        rr = Vm(lambda v: v.reciprocal(denrow[0:1, :], denrow[0:1, :]), waits=[rr])
        rr = Vm(lambda v: v.tensor_mul(denrow[0:1, :], denrow[0:1, :],
                                       gpow[:, :]), waits=[rr])
        rbt = Tm(lambda t: t.matmul(pb[4][:, 0:512], onesc[0:1, :], denrow[0:1, :],
                                    start=True, stop=True),
                 waits=[rr] + bacq([4]))
        bwr([4], rbt)
        tAr = [None] * 8
        for jp in range(8):
            ilo = 0 if jp < 4 else (jp % 4) * 128
            tAr[jp] = brd([4], Vm(lambda v, jp=jp, ilo=ilo: v.tensor_mul(
                Abuf[jp][:, ilo:512], Abuf[jp][:, ilo:512], pb[4][:, ilo:512]),
                waits=[rbt, tA[jp]]))
        t_xa = [None] * 8
        for dv in range(8):
            bk_ = 2 + dv % 2
            mm = None
            for jp in range(8):
                sp = pp if jp < 4 else par
                jt = jp % 4
                ilo = 0 if jp < 4 else jt * 128
                w = [tAr[jp], tok_v_ready[sp][jt], mm]
                if jp == 0:
                    w += bacq([bk_])
                mm = Tm(lambda t, dv=dv, jt=jt, ilo=ilo, bk_=bk_, sp=sp, jp=jp:
                        t.matmul(
                    pb[bk_][:, ilo:512],
                    vring[sp][jt][:, dv * 128:(dv + 1) * 128],
                    Abuf[jp][:, ilo:512], start=(jp == 0), stop=(jp == 7)),
                    waits=w)
            bwr([bk_], mm)
            t_xa[dv] = brd([bk_], Vm(lambda v, dv=dv, bk_=bk_:
                                     v.scalar_tensor_tensor(
                xfm[dv][:, :], pb[bk_][:, 0:512], gac[:, dv:dv + 1],
                xfm[dv][:, :], ALU.mult, ALU.add),
                waits=[mm, t_ga, xtoks[dv]]))
        tok_kv_free[pp] = t_xa[7]
        t_abuf_free = t_xa[7]

        if u0 >= L2F_LO:
            for f in range(8):
                tok_xa_st = Ym(lambda y, f=f, u0=u0: y.dma_start(
                    xa_d.ap()[f, :, u0 - L2F_LO:u0 - L2F_LO + 512],
                    xfm[f][:, 0:512]), waits=[t_xa[f], tok_xa_st])

        t_inv2 = ln_stats2(xsrc, t_xa, ones128, t_ones128)
        do_v = u0 < L2F_LO
        tB1b, tB2b, tVBb = ln_broadcasts(t_inv2, u0, do_v)
        lo_w = 504 if ci == 1 else 0
        for f in range(8):
            a1 = brd([6], Vm(lambda v, f=f, lo_w=lo_w: v.tensor_mul(
                tmpB[:, lo_w:512], xfm[f][:, lo_w:512], pb[6][:, lo_w:512]),
                waits=[tB1b, t_xa[f], tok_y0[f - 1] if f else None]))
            a2 = brd([7], Vm(lambda v, lo_w=lo_w: v.tensor_add(
                tmpB[:, lo_w:512], tmpB[:, lo_w:512], pb[7][:, lo_w:512]),
                waits=[a1, tB2b]))
            a3 = Vm(lambda v, f=f, lo_w=lo_w: v.tensor_scalar(
                tmpB[:, lo_w:512], tmpB[:, lo_w:512], cb1c[:, f:f + 1], None,
                ALU.mult), waits=[a2, t_cb1])
            dst_c = u0 - AB + lo_w
            if do_v:
                tok_y0[f] = brd([4], Vm(lambda v, f=f, lo_w=lo_w, dst_c=dst_c:
                                        v.scalar_tensor_tensor(
                    arena[f][:, dst_c:dst_c + 512 - lo_w], pb[4][:, lo_w:512],
                    modc[:, 24 + f:24 + f + 1], tmpB[:, lo_w:512],
                    ALU.mult, ALU.add), waits=[a3, tVBb, t_modc]))
            else:
                tok_y0[f] = Vm(lambda v, f=f, dst_c=dst_c: v.tensor_scalar(
                    arena[f][:, dst_c:dst_c + 512], tmpB[:, 0:512],
                    modc[:, 24 + f:24 + f + 1], None, ALU.add),
                    waits=[a3, t_modc])
        tok_x_free = tok_y0[7]

    tok_p1 = barrier()

    def attn_phase(l, lo_keys, lo_q, in_base, out_base, prevb):
        CUR_TAG[0] = f'a{l}-pre'
        tw = {}
        for i in range(4):
            tw[("tab", i)] = GD(lambda g, i=i: g.dma_start(
                tabs[i][:, :], ap("tabs")[i, :, :]), waits=prevb)
        tadd = [None] * 4
        for i in range(4):
            tlo = GD(lambda g, i=i: g.dma_start(
                tablo[:, :], ap("tabs")[4 + i, :, :]),
                waits=prevb + [tadd[i - 1] if i else None])
            tadd[i] = Vm(lambda v, i=i: v.tensor_add(
                tabs[i][:, :], tabs[i][:, :], tablo[:, :]),
                waits=[tw[("tab", i)], tlo])
        for i in range(8):
            tw[("wqkv", i)] = GD(lambda g, i=i, l=l: g.dma_start(
                wqkv_sb[i][:, :], ap("wqkv")[l, i * 128:(i + 1) * 128, :]),
                waits=prevb)
        for i in range(4):
            tw[("wo", i)] = GD(lambda g, i=i, l=l: g.dma_start(
                wo_sb[i][:, :], ap("wo")[l, i * 128:(i + 1) * 128, :]),
                waits=prevb)
        for nm, dd in [("rotm", rotm_sb), ("m_up", m_up_sb), ("m_lo", m_lo_sb)]:
            tw[nm] = GD(lambda g, nm=nm, dd=dd: g.dma_start(dd[:, :], ap(nm)),
                        waits=prevb)
        for nm, dd in [("qsc", qs8_sb), ("qspc", qs8p_sb), ("ksc", ks_sb),
                       ("kspc", ksp_sb)]:
            tw[nm] = Ym(lambda y, nm=nm, dd=dd, l=l: y.dma_start(
                dd[:, :], ap(nm)[l]), waits=prevb)
        t_vneg = Vm(lambda v: v.tensor_scalar(
            vnegrow[:, :], vrow[0:1, 0:VNEG_W], -1.0, BIG, ALU.add, ALU.mult),
            waits=[tl["validr"]] + prevb)
        t8a = Vm(lambda v: v.tensor_scalar(qs8_sb[:, :], qs8_sb[:, :], QK_SCALE,
                                           None, ALU.mult), waits=[tw["qsc"]])
        t8b = Vm(lambda v: v.tensor_scalar(qs8p_sb[:, :], qs8p_sb[:, :],
                                           QK_SCALE, None, ALU.mult),
                 waits=[tw["qspc"]])
        ttab = [None] * 4
        for i, ctok in enumerate([t8a, t8b, tw["ksc"], tw["kspc"]]):
            col = [qs8_sb, qs8p_sb, ks_sb, ksp_sb][i]
            ttab[i] = Vm(lambda v, i=i, col=col: v.tensor_scalar(
                tabs[i][:, :], tabs[i][:, :], col[:, :], None, ALU.mult),
                waits=[tadd[i], ctok])

        tok_out = [None] * 8
        tok_kr = [[None] * 4, [None] * 4]
        tok_vr = [[None] * 4, [None] * 4]
        tok_kfree = [None, None]
        t_qhat = [None] * 4
        t_sB = None

        def h1s(dc, c0, w, u):
            off = (u - in_base) - (1 if dc >= 4 else 0) + c0
            return arena[dc][:, off:off + w]

        def stage_a(sb_u):
            CUR_TAG[0] = f'a{l}-sa'
            ps = (sb_u // 512) % 2
            tq_l = {}
            prev_x3 = [None]
            prev_sq = [None]
            prev_final = [None]
            for grp in range(2):
                for ept in range(4):
                    e = grp * 4 + ept
                    bk_ = e % 2
                    mm = None
                    for dc in range(8):
                        w = [tw[("wqkv", dc)], mm]
                        if dc == 0:
                            w += bacq([bk_]) + [t_sB]
                        mm = Tm(lambda t, e=e, dc=dc, bk_=bk_, sb_u=sb_u:
                                t.matmul(
                            pb[bk_][:, 0:512],
                            wqkv_sb[dc][:, e * 128:(e + 1) * 128],
                            h1s(dc, 0, 512, sb_u),
                            start=(dc == 0), stop=(dc == 7)), waits=w)
                    bwr([bk_], mm)
                    c1 = brd([bk_], Sm(lambda s, bk_=bk_: s.activation(
                        qsbuf[:, :], pb[bk_][:, 0:512], AF.Copy),
                        waits=[mm, prev_sq[0]]))
                    rmm = Tm(lambda t: t.matmul(
                        pb[4][:, 0:512], rotm_sb[:, :], qsbuf[:, :],
                        start=True, stop=True),
                        waits=[c1, tw["rotm"]] + bacq([4]))
                    bwr([4], rmm)
                    sq = Gm(lambda g: g.tensor_mul(tmpC[:, :], qsbuf[:, :],
                                                   qsbuf[:, :]), waits=[c1])
                    prev_sq[0] = sq
                    ssm = Tm(lambda t: t.matmul(
                        pb[5][0:2, 0:512], bd2_sb[:, :], tmpC[:, :],
                        start=True, stop=True),
                        waits=[sq, tl["bd2"]] + bacq([5]))
                    bwr([5], ssm)
                    iv = brd([5], Sm(lambda s: s.activation(
                        rows4[0:2, :], pb[5][0:2, 0:512], AF.Ln,
                        bias=eps24[:, :]), waits=[ssm, t_eps24]))
                    iv = Sm(lambda s: s.activation(rows4[0:2, :], rows4[0:2, :],
                                                   AF.Exp, scale=-0.5),
                            waits=[iv])
                    ivB = Tm(lambda t: t.matmul(
                        pb[5][:, 0:512], sel2_sb[:, :], rows4[0:2, :],
                        start=True, stop=True),
                        waits=[iv, tl["sel2"]] + bacq([5]))
                    bwr([5], ivB)
                    ct, st_ = 2 * grp, 2 * grp + 1
                    tc0 = sb_u - TAB_LO
                    x1 = brd([bk_], Vm(lambda v, bk_=bk_, ct=ct, tc0=tc0:
                                       v.tensor_mul(
                        tmpA[:, :], pb[bk_][:, 0:512],
                        tabs[ct][:, tc0:tc0 + 512]),
                        waits=[mm, ttab[ct], c1, prev_final[0], prev_x3[0]]))
                    x2 = brd([4], Vm(lambda v, st_=st_, tc0=tc0: v.tensor_mul(
                        tmpB[:, :], pb[4][:, 0:512],
                        tabs[st_][:, tc0:tc0 + 512]),
                        waits=[rmm, ttab[st_], prev_x3[0]]))
                    x3 = Gm(lambda g: g.tensor_add(tmpA[:, :], tmpA[:, :],
                                                   tmpB[:, :]), waits=[x1, x2])
                    prev_x3[0] = x3
                    if grp == 0:
                        dst, wfree = qhat[ept], t_sB
                    else:
                        dst, wfree = khat[ps][ept], tok_kfree[ps]
                    tqv = brd([5], Vm(lambda v, dst=dst: v.tensor_mul(
                        dst[:, :], tmpA[:, :], pb[5][:, 0:512]),
                        waits=[x3, ivB, wfree]))
                    tq_l[e] = tqv
                    prev_final[0] = tqv
                    if grp == 0:
                        t_qhat[ept] = tqv
                    else:
                        tok_kr[ps][ept] = tqv
            tvv = None
            for st in range(4):
                bk_ = st % 2
                mm = None
                for dc in range(8):
                    w = [tw[("wqkv", dc)], mm]
                    if dc == 0:
                        w += bacq([bk_]) + [tq_l[7], tvv]
                    mm = Tm(lambda t, st=st, dc=dc, bk_=bk_, sb_u=sb_u: t.matmul(
                        pb[bk_][:, 0:512], h1s(dc, st * 128, 128, sb_u),
                        wqkv_sb[dc][:, 1024:1536],
                        start=(dc == 0), stop=(dc == 7)), waits=w)
                bwr([bk_], mm)
                tvv = brd([bk_], Vm(lambda v, st=st, bk_=bk_, ps=ps:
                                    v.tensor_copy(
                    vtm[ps][st][:, :], pb[bk_][:, 0:512]),
                    waits=[mm, tok_kfree[ps]]))
                tok_vr[ps][st] = tvv
            return tvv

        def stage_b2(qp, t_prev_wo):
            CUR_TAG[0] = f'a{l}-sb'
            qc2 = qp % 512
            tev = [None] * 8
            tav = [None] * 8
            toev = t_prev_wo

            def pieces_for(qt):
                out = []
                j = 0
                while j < 640:
                    kabs = qt - 512 + j
                    slot = (kabs // 512) % 2
                    kc = kabs % 512
                    w = min(512 - kc, 640 - j)
                    if j < 512 and j + w > 512:
                        w = 512 - j
                    if kabs < 3072 < kabs + w:
                        w = 3072 - kabs
                    out.append((0 if j < 512 else 1, j if j < 512 else j - 512,
                                w, slot, kc, kabs))
                    j += w
                return out

            for h in range(HEADS):
                ept, half = h // 2, h % 2
                r0 = 64 * half
                for ti in range(2):
                    qt = qp + 128 * ti
                    qc = qt % 512
                    w01 = bacq([0, 1])
                    mm = None
                    for (bk_, c0, w, slot, kc, kabs) in pieces_for(qt):
                        mm = Tm(lambda t, ept=ept, r0=r0, bk_=bk_, c0=c0, w=w,
                                slot=slot, kc=kc, qc=qc, kabs=kabs: t.matmul(
                            pb[bk_][:, c0:c0 + w],
                            qhat[ept][r0:r0 + 64, qc:qc + 128],
                            khat[slot][ept][r0:r0 + 64, kc:kc + w],
                            start=True, stop=(kabs + w > 3072)),
                            waits=[tok_kr[slot][ept], t_qhat[ept], mm] +
                                  (w01 if mm is None else []))
                        if kabs + w <= 3072:
                            mm = Tm(lambda t, bk_=bk_, c0=c0, w=w, kabs=kabs:
                                    t.matmul(
                                pb[bk_][:, c0:c0 + w], onescb[0:1, 0:128],
                                vnegrow[0:1, kabs:kabs + w],
                                start=False, stop=True),
                                waits=[t_vneg, mm, t_onesb])
                    bwr([0], mm)
                    bwr([1], mm)
                    mq1 = brd([0], Vm(lambda v: v.tensor_add(
                        pb[0][:, 0:128], pb[0][:, 0:128], m_lo_sb[:, :]),
                        waits=[mm, tw["m_lo"]]))
                    mq2 = brd([1], Vm(lambda v: v.tensor_add(
                        pb[1][:, 0:128], pb[1][:, 0:128], m_up_sb[:, :]),
                        waits=[mm, tw["m_up"], mq1]))
                    nc_ = 2 * h + ti
                    n1 = brd([0], Vm(lambda v, nc_=nc_: v.tensor_reduce(
                        nmax[:, nc_:nc_ + 1], pb[0][:, 0:512], AX.X, ALU.max,
                        negate=True), waits=[mq1, mq2]))
                    n2 = brd([1], Vm(lambda v: v.tensor_reduce(
                        tmpA[:, 0:1], pb[1][:, 0:128], AX.X, ALU.max,
                        negate=True), waits=[mq2, n1, prev_nm[0]]))
                    prev_nm[0] = Vm(lambda v, nc_=nc_: v.tensor_scalar(
                        nmax[:, nc_:nc_ + 1], nmax[:, nc_:nc_ + 1],
                        tmpA[:, 0:1], None, ALU.min), waits=[n2])
                ttrA = Tm(lambda t, h=h: t.transpose(
                    pb[1][0:1, 0:128], nmax[:, 2 * h:2 * h + 1],
                    identsb[:, :]),
                    waits=[prev_nm[0], tl["ident"]] + bacq([1]))
                ttrB = Tm(lambda t, h=h: t.transpose(
                    pb[1][0:1, 128:256], nmax[:, 2 * h + 1:2 * h + 2],
                    identsb[:, :]), waits=[prev_nm[0], ttrA])
                bwr([1], ttrB)
                trw = brd([1], Vm(lambda v: v.tensor_copy(
                    nrow4[0:1, 0:256], pb[1][0:1, 0:256]),
                    waits=[ttrA, ttrB, prev_bc[0]]))
                trw = Vm(lambda v: v.tensor_copy(
                    nrow4[0:1, 256:512], nrow4[0:1, 0:256]), waits=[trw])
                pbh = pbuf[h % 2]
                smm = [None] * 3
                wst = bacq([5, 6, 7])
                for b in range(6):
                    kb = qp - 512 + b * 128
                    slot = (kb // 512) % 2
                    kc = kb % 512
                    bs = 5 + b // 2
                    c0 = 256 * (b % 2)
                    smm[b // 2] = Tm(lambda t, ept=ept, r0=r0, slot=slot,
                                     kc=kc, bs=bs, c0=c0, qc2=qc2, b=b:
                                     t.matmul(
                        pb[bs][:, c0:c0 + 256],
                        khat[slot][ept][r0:r0 + 64, kc:kc + 128],
                        qhat[ept][r0:r0 + 64, qc2:qc2 + 256],
                        start=(b % 2 == 0), stop=False, skip_group_check=True),
                        waits=[tok_kr[slot][ept], t_qhat[ept], smm[b // 2]] +
                              (wst if b == 0 else []))
                    if kb < 3072:
                        smm[b // 2] = Tm(lambda t, bs=bs, c0=c0, kb=kb:
                                         t.matmul(
                            pb[bs][:, c0:c0 + 256],
                            vnegrow[0:1, kb:kb + 128], onescb[0:1, 0:256],
                            start=False, stop=False, skip_group_check=True),
                            waits=[t_vneg, smm[b // 2], t_onesb])
                tbcm = [None] * 3
                for g2 in range(3):
                    tbcm[g2] = Tm(lambda t, g2=g2: t.matmul(
                        pb[5 + g2][:, 0:512], onescb[0:1, 0:128],
                        nrow4[0:1, 0:512], start=False, stop=True,
                        skip_group_check=True),
                        waits=[trw, smm[g2], t_onesb,
                               tbcm[g2 - 1] if g2 else None])
                    bwr([5 + g2], tbcm[g2])
                prev_bc[0] = tbcm[2]
                mk = brd([5], Vm(lambda v: v.tensor_add(
                    pb[5][:, 0:128], pb[5][:, 0:128], m_up_sb[:, :]),
                    waits=[tbcm[0], tw["m_up"]]))
                mk = brd([5], Vm(lambda v: v.tensor_scalar(
                    pb[5][:, 128:256], pb[5][:, 128:256], -BIG, None,
                    ALU.add), waits=[mk]))
                mk = brd([5], Vm(lambda v: v.tensor_add(
                    pb[5][:, 384:512], pb[5][:, 384:512], m_up_sb[:, :]),
                    waits=[mk]))
                mk2 = brd([7], Vm(lambda v: v.tensor_add(
                    pb[7][:, 0:128], pb[7][:, 0:128], m_lo_sb[:, :]),
                    waits=[tbcm[2], tw["m_lo"], mk]))
                mk2 = brd([7], Vm(lambda v: v.tensor_scalar(
                    pb[7][:, 256:384], pb[7][:, 256:384], -BIG, None,
                    ALU.add), waits=[mk2]))
                mk2 = brd([7], Vm(lambda v: v.tensor_add(
                    pb[7][:, 384:512], pb[7][:, 384:512], m_lo_sb[:, :]),
                    waits=[mk2]))
                we = [tav[h - 2] if h >= 2 else None]
                e0 = brd([5], Sm(lambda s, pbh=pbh: s.activation(
                    pbh[:, 0:512], pb[5][:, 0:512], AF.Exp),
                    waits=[mk] + we))
                e1 = brd([6], Sm(lambda s, pbh=pbh: s.activation(
                    pbh[:, 512:1024], pb[6][:, 0:512], AF.Exp),
                    waits=[tbcm[1], e0]))
                e2 = brd([7], Sm(lambda s, pbh=pbh: s.activation(
                    pbh[:, 1024:1536], pb[7][:, 0:512], AF.Exp),
                    waits=[mk2, e1]))
                tev[h] = e2
                eb = [e0, e1, e2]
                wd = bacq([4])
                tdn = None
                for b in range(6):
                    tdn = Tm(lambda t, b=b, pbh=pbh: t.matmul(
                        pb[4][0:1, 0:256], ones128b[:, :],
                        pbh[:, b * 256:(b + 1) * 256],
                        start=(b == 0), stop=(b == 5)),
                        waits=[eb[b // 2], tdn, t_ones128b] +
                              (wd if b == 0 else []))
                bwr([4], tdn)
                rc = brd([4], Vm(lambda v: v.reciprocal(
                    denrow[0:1, 0:256], pb[4][0:1, 0:256]),
                    waits=[tdn, prev_tbc[0]]))
                if qp < L2F_LO:
                    rc = Vm(lambda v, qp=qp: v.tensor_mul(
                        denrow[0:1, 0:256], denrow[0:1, 0:256],
                        vrow[0:1, qp:qp + 256]), waits=[rc, tl["validr"]])
                avb = 2 + (h % 2)
                avc = 256 * ((h // 2) % 2)
                tp = (0, 64) if half else None
                wa = bacq([avb])
                amm = None
                for b in range(6):
                    kb = qp - 512 + b * 128
                    slot = (kb // 512) % 2
                    stt_ = (kb % 512) // 128
                    amm = Tm(lambda t, slot=slot, stt_=stt_, h=h, avb=avb,
                             avc=avc, tp=tp, pbh=pbh, b=b, r0=r0: t.matmul(
                        pb[avb][r0:r0 + 64, avc:avc + 256],
                        vtm[slot][stt_][:, h * 64:h * 64 + 64],
                        pbh[:, b * 256:(b + 1) * 256],
                        start=(b == 0), stop=(b == 5), tile_position=tp),
                        waits=[tok_vr[slot][stt_], eb[b // 2], amm] +
                              (wa if b == 0 else []))
                tav[h] = amm
                bwr([avb], amm)
                tbc = Tm(lambda t: t.matmul(
                    pb[4][:, 256:512], onesc[0:1, :], denrow[0:1, 0:256],
                    start=True, stop=True), waits=[rc, tdn, amm])
                bwr([4], tbc)
                prev_tbc[0] = tbc
                toev = brd([avb], Vm(lambda v, ept=ept, r0=r0, avb=avb,
                                     avc=avc, qc2=qc2: v.tensor_copy(
                    obuf[ept][r0:r0 + 64, qc2:qc2 + 256],
                    pb[avb][r0:r0 + 64, avc:avc + 256]),
                    waits=[amm, toev]))
                toev = brd([4], Vm(lambda v, ept=ept, r0=r0, qc2=qc2:
                                   v.tensor_mul(
                    obuf[ept][r0:r0 + 64, qc2:qc2 + 256],
                    obuf[ept][r0:r0 + 64, qc2:qc2 + 256],
                    pb[4][r0:r0 + 64, 256:512]), waits=[tbc, toev]))
            return toev

        prev_nm = [None]
        prev_bc = [None]
        prev_tbc = [None]

        for sb_u in range(lo_keys, T, 512):
            ta = stage_a(sb_u)
            if sb_u < lo_q or SUBB == 0:
                t_sB = ta
                continue
            tos = None
            for k2 in range(2):
                tos = stage_b2(sb_u + k2 * 256,
                               tok_out[7] if k2 == 0 else None)
            t_sB = tos
            if SUBB < 3 or SUBB == 23:
                continue
            CUR_TAG[0] = f'a{l}-wo'
            for f in range(8):
                bk_ = 6 + f % 2
                mm = None
                for ic in range(4):
                    w = [tos, tw[("wo", ic)], mm]
                    if ic == 0:
                        w += bacq([bk_])
                    mm = Tm(lambda t, f=f, ic=ic, bk_=bk_: t.matmul(
                        pb[bk_][:, 0:512], wo_sb[ic][:, f * 128:(f + 1) * 128],
                        obuf[ic][:, 0:512], start=(ic == 0), stop=(ic == 3)),
                        waits=w)
                bwr([bk_], mm)
                oc = sb_u - out_base
                icc = sb_u - in_base
                tok_out[f] = brd([bk_], Vm(lambda v, f=f, oc=oc, icc=icc,
                                           bk_=bk_: v.tensor_add(
                    arena[f][:, oc:oc + 512], arena[f][:, icc:icc + 512],
                    pb[bk_][:, 0:512]), waits=[mm, tok_out[f]]))
            ps = (sb_u // 512) % 2
            tok_kfree[1 - ps] = tos
        return barrier()

    bar = tok_p1
    if NPHASES >= 2:
        bar = attn_phase(0, 1024, L1A_LO, AB, 1536, tok_p1)

    def ffn_phase(l, lo, in_base, out_base, prevb, final):
        tw1 = []
        for i in range(8):
            tw1.append(GD(lambda g, i=i, l=l: g.dma_start(
                w1_sb[i][:, :], ap("w1")[l, i * 128:(i + 1) * 128, :]),
                waits=prevb))
        tb = {}
        for n, dd in [("b1a", b1a_sb), ("b1g", b1g_sb), ("b2c", b2_sb),
                      ("lgc", lg_sb), ("lbc", lb_sb)]:
            tb[n] = Ym(lambda y, n=n, dd=dd, l=l: y.dma_start(
                dd[:, :], ap(n)[l]), waits=prevb)
        tz = Vm(lambda v: v.memset(gated[NHC - 1][:, :], 0.0), waits=prevb)

        blocks = []
        u = lo
        while u < T:
            w = min(512, T - u)
            blocks.append((u, w))
            u += w

        tok_y = [None] * 8
        tok_w2 = None
        tok_xa = None
        t_pass2_last = None
        def xs(dc, c0, w, u0):
            off = (u0 - in_base) - (1 if dc >= 4 else 0) + c0
            return arena[dc][:, off:off + w]

        for (u0, W) in blocks:
            CUR_TAG[0] = f'f{l}-st'
            w5 = bacq([5])
            w4f = bacq([4])
            tsum = None
            for f in range(8):
                tsum = Tm(lambda t, f=f, W=W, u0=u0: t.matmul(
                    pb[5][0:1, 0:W], ones128b[:, :], xs(f, 0, W, u0),
                    start=(f == 0), stop=(f == 7)),
                    waits=[t_ones128b, tsum, tok_y[7]] +
                          (w5 + w4f + (prevb if u0 == lo else [])
                           if f == 0 else []))
            bwr([5], tsum)
            tsq = None
            for f in range(8):
                s1 = Sm(lambda s, f=f, W=W, u0=u0: s.activation(
                    tmpCb[:, 0:W], xs(f, 0, W, u0), AF.Square),
                    waits=[tsq, tok_y[7]] +
                          (prevb if u0 == lo and f == 0 else []))
                tsq = Tm(lambda t, f=f, W=W: t.matmul(
                    pb[4][0:1, 0:W], ones128b[:, :], tmpCb[:, 0:W],
                    start=(f == 0), stop=(f == 7)),
                    waits=[s1, t_ones128b] + ([tsum] if f == 0 else []))
            bwr([4], tsq)
            r1 = brd([5], Vm(lambda v, W=W: v.tensor_scalar(
                rowaux[0:1, 0:W], pb[5][0:1, 0:W], 1.0 / D, None, ALU.mult),
                waits=[tsum, tsq]))
            r2 = brd([4], Vm(lambda v, W=W: v.tensor_scalar(
                rows4[0:1, 0:W], pb[4][0:1, 0:W], 1.0 / D, None, ALU.mult),
                waits=[tsq]))
            r3 = brd([4], Vm(lambda v, W=W: v.tensor_mul(
                pb[4][0:1, 0:W], rowaux[0:1, 0:W], rowaux[0:1, 0:W]),
                waits=[r1, r2]))
            r4 = brd([4], Vm(lambda v, W=W: v.tensor_sub(
                rows4[0:1, 0:W], rows4[0:1, 0:W], pb[4][0:1, 0:W]), waits=[r3]))
            r5 = Sm(lambda s, W=W: s.activation(rows4[0:1, 0:W], rows4[0:1, 0:W],
                                                AF.Ln, bias=eps5[0:1, :]),
                    waits=[r4, t_eps5])
            r5 = Sm(lambda s, W=W: s.activation(rows4[0:1, 0:W], rows4[0:1, 0:W],
                                                AF.Exp, scale=-0.5), waits=[r5])
            rb = Vm(lambda v, W=W: v.tensor_scalar(
                rowaux[0:1, 0:W], rowaux[0:1, 0:W], -1.0, None, ALU.mult),
                waits=[r5])
            rb = Vm(lambda v, W=W: v.tensor_mul(rowaux[0:1, 0:W],
                                                rowaux[0:1, 0:W],
                                                rows4[0:1, 0:W]), waits=[rb])
            tB1 = Tm(lambda t, W=W: t.matmul(
                pb[6][:, 0:W], onesc[0:1, :], rows4[0:1, 0:W],
                start=True, stop=True), waits=[r5, t_ones] + bacq([6]))
            bwr([6], tB1)
            tB2 = Tm(lambda t, W=W: t.matmul(
                pb[7][:, 0:W], onesc[0:1, :], rowaux[0:1, 0:W],
                start=True, stop=True), waits=[rb] + bacq([7]))
            bwr([7], tB2)
            tlnh = [None] * 8
            for f in range(8):
                a1 = brd([6], Vm(lambda v, f=f, W=W, u0=u0: v.tensor_mul(
                    tmpB[:, 0:W], xs(f, 0, W, u0), pb[6][:, 0:W]),
                    waits=[tB1, tlnh[f - 1] if f else None]))
                a2 = brd([7], Vm(lambda v, W=W: v.tensor_add(
                    tmpB[:, 0:W], tmpB[:, 0:W], pb[7][:, 0:W]),
                    waits=[a1, tB2]))
                tlnh[f] = Vm(lambda v, f=f, W=W: v.tensor_scalar(
                    lnh[f][:, 0:W], tmpB[:, 0:W], lg_sb[:, f:f + 1],
                    lb_sb[:, f:f + 1], ALU.mult, ALU.add),
                    waits=[a2, tb["lgc"], tb["lbc"], t_pass2_last])
            tvB = None
            if (not final) and u0 < L2F_LO:
                tvBm = Tm(lambda t, u0=u0, W=W: t.matmul(
                    pb[4][:, 0:W], onescb[0:1, 0:128], vrow[0:1, u0:u0 + W],
                    start=True, stop=True),
                    waits=[tlnh[7], tl["validr"], t_onesb] + bacq([4]))
                bwr([4], tvBm)
                tvB = brd([4], Vm(lambda v, W=W: v.tensor_copy(
                    validB[:, 0:W], pb[4][:, 0:W]), waits=[tvBm]))
            CUR_TAG[0] = f'f{l}-w1'
            tgd = [None] * NHC
            for c in range(NHC):
                aw = min(128, HID - c * 128)
                pa, pg = [(0, 1), (2, 3), (6, 7)][c % 3]
                mma = None
                wA = bacq([pa])
                for dc in range(8):
                    mma = Tm(lambda t, c=c, dc=dc, aw=aw, W=W, pa=pa: t.matmul(
                        pb[pa][0:aw, 0:W], w1_sb[dc][:, c * 128:c * 128 + aw],
                        lnh[dc][:, 0:W], start=(dc == 0), stop=(dc == 7)),
                        waits=[tlnh[dc], tw1[dc], mma] +
                              (wA if dc == 0 else []))
                bwr([pa], mma)
                mmg = None
                wG = bacq([pg])
                for dc in range(8):
                    mmg = Tm(lambda t, c=c, dc=dc, aw=aw, W=W, pg=pg: t.matmul(
                        pb[pg][0:aw, 0:W],
                        w1_sb[dc][:, HID + c * 128:HID + c * 128 + aw],
                        lnh[dc][:, 0:W], start=(dc == 0), stop=(dc == 7)),
                        waits=[tlnh[dc], mmg] + (wG if dc == 0 else []))
                bwr([pg], mmg)
                ge = brd([pg], Sm(lambda s, c=c, aw=aw, W=W, pg=pg: s.activation(
                    gated[c][0:aw, 0:W], pb[pg][0:aw, 0:W], AF.Gelu,
                    bias=b1g_sb[0:aw, c:c + 1]),
                    waits=[mmg, tb["b1g"], tz]))
                tgd[c] = brd([pa], Vm(lambda v, c=c, aw=aw, W=W, pa=pa:
                                      v.scalar_tensor_tensor(
                    gated[c][0:aw, 0:W], pb[pa][0:aw, 0:W],
                    b1a_sb[0:aw, c:c + 1], gated[c][0:aw, 0:W],
                    ALU.add, ALU.mult), waits=[ge, tb["b1a"]]))
            CUR_TAG[0] = f'f{l}-w2'
            mmo = [None] * 8
            wall = bacq(list(range(8)))
            for c in range(NHC):
                tok_w2A = GD(lambda g, c=c, l=l: g.dma_start(
                    w2r[:, 0:512], ap("w2p")[l, c * 128:(c + 1) * 128, 0:512]),
                    waits=[tok_w2, tgd[c], mmo[3]])
                tok_w2 = GD(lambda g, c=c, l=l: g.dma_start(
                    w2r[:, 512:1024],
                    ap("w2p")[l, c * 128:(c + 1) * 128, 512:1024]),
                    waits=[tok_w2A, mmo[7]])
                for f in range(8):
                    mmo[f] = Tm(lambda t, c=c, f=f, W=W: t.matmul(
                        pb[f][:, 0:W], w2r[:, f * 128:(f + 1) * 128],
                        gated[c][:, 0:W], start=(c == 0), stop=(c == NHC - 1)),
                        waits=[tgd[c], tok_w2A if f < 4 else tok_w2, mmo[f]] +
                              (wall if c == 0 and f == 0 else []))
            for f in range(8):
                bwr([f], mmo[f])
            CUR_TAG[0] = f'f{l}-ep'
            if not final:
                for f in range(8):
                    e1 = brd([f], Vm(lambda v, f=f, u0=u0, W=W:
                                     v.scalar_tensor_tensor(
                        tmpB[:, 0:W], pb[f][:, 0:W], b2_sb[:, f:f + 1],
                        arena[f][:, u0 - in_base:u0 - in_base + W],
                        ALU.add, ALU.add),
                        waits=[mmo[f], tb["b2c"], tok_y[f],
                               tok_y[f - 1] if f else tok_y[7]]))
                    if u0 < L2F_LO:
                        tok_y[f] = Vm(lambda v, f=f, u0=u0, W=W: v.tensor_mul(
                            arena[f][:, u0 - out_base:u0 - out_base + W],
                            tmpB[:, 0:W], validB[:, 0:W]), waits=[e1, tvB])
                    else:
                        tok_y[f] = Vm(lambda v, f=f, u0=u0, W=W: v.tensor_copy(
                            arena[f][:, u0 - out_base:u0 - out_base + W],
                            tmpB[:, 0:W]), waits=[e1])
                t_pass2_last = tok_y[7]
            else:
                tst = [None, None]
                for f in range(8):
                    buf = tmpB if f % 2 == 0 else tmpC
                    tok_xa = Ym(lambda y, f=f, u0=u0, W=W: y.dma_start(
                        xaring[:, 0:W],
                        xa_d.ap()[f, :, u0 - L2F_LO:u0 - L2F_LO + W]),
                        waits=[tok_xa, tok_y[f - 1] if f else tok_y[7]] +
                              (prevb if u0 == lo and f == 0 else []))
                    e1 = brd([f], Vm(lambda v, f=f, u0=u0, W=W, buf=buf:
                                     v.scalar_tensor_tensor(
                        buf[:, 0:W], pb[f][:, 0:W], b2_sb[:, f:f + 1],
                        arena[f][:, u0 - in_base:u0 - in_base + W],
                        ALU.add, ALU.add),
                        waits=[mmo[f], tb["b2c"], tst[f % 2]]))
                    ty2 = Vm(lambda v, f=f, W=W, buf=buf:
                             v.scalar_tensor_tensor(
                        buf[:, 0:W], buf[:, 0:W], gbc[:, f:f + 1],
                        xaring[:, 0:W], ALU.mult, ALU.add),
                        waits=[e1, t_gb, tok_xa])
                    tok_y[f] = ty2
                    tst[f % 2] = Ym(lambda y, f=f, u0=u0, W=W, buf=buf:
                                    y.dma_start(
                        out_d.ap()[f, :, u0 - L2F_LO:u0 - L2F_LO + W],
                        buf[:, 0:W]), waits=[ty2])
                    tok_y[f] = tst[f % 2]
                t_pass2_last = tok_y[7]
        return barrier()

    if NPHASES >= 3:
        bar = ffn_phase(0, L1F_LO, 1536, 1920, bar, False)
    if NPHASES >= 4:
        bar = attn_phase(1, 2048, L2A_LO, 1920, 2560, bar)
    if NPHASES >= 5:
        bar = ffn_phase(1, L2F_LO, 2560, 0, bar, True)
    final_waits = barrier()

    with (
        nc.semaphore("sem_t") as sem_t,
        nc.semaphore("sem_v") as sem_v,
        nc.semaphore("sem_s") as sem_s,
        nc.semaphore("sem_g") as sem_g,
        nc.semaphore("sem_gd") as sem_gd,
        nc.semaphore("sem_y") as sem_y,
        nc.Block() as block,
    ):
        sems = {"t": sem_t, "v": sem_v, "s": sem_s, "g": sem_g, "gd": sem_gd,
                "y": sem_y}

        def emit_stream(eng_obj, eng_key):
            last = {}
            for (ek, emit, waits, inc) in ops:
                if ek != eng_key:
                    continue
                for (sname, val, peng) in waits:
                    if last.get(sname, 0) >= val:
                        continue
                    eng_obj.wait_ge(sems[sname], val)
                    last[sname] = val
                ins = emit(eng_obj)
                if inc is not None:
                    ins.then_inc(sems[inc[0]], inc[2])
            if eng_key == "Y":
                for (sname, val, _) in final_waits:
                    if sname == "y" and last.get("y", 0) < val:
                        eng_obj.wait_ge(sems["y"], val)

        @block.tensor
        def _(t):
            emit_stream(t, "T")

        @block.vector
        def _(v):
            emit_stream(v, "V")

        @block.scalar
        def _(s):
            emit_stream(s, "S")

        @block.gpsimd
        def _(g):
            emit_stream(g, "G")

        @block.sync
        def _(y):
            emit_stream(y, "Y")

    return nc


_TABCACHE = {}


def _host_tables(lo):
    if lo in _TABCACHE:
        return _TABCACHE[lo]
    import jax
    import jax.numpy as jnp
    with jax.default_device(jax.devices("cpu")[0]):
        half = jnp.arange(0, 64, 2, dtype=jnp.float32)
        invf = 1.0 / 10000.0 ** (half / 64)
        t = jnp.arange(lo + TAB_LO, lo + TAB_LO + TABW, dtype=jnp.float32)
        fr = t[:, None] * invf[None, :]
        base = (half + 0.4 * 64) / (1.4 * 64)
        power = (t - 4096) / float(256)
        sc32 = base[None, :] ** power[:, None]
        cos32, sin32 = jnp.cos(fr), jnp.sin(fr)
    cos = np.tile(np.asarray(cos32, np.float64), (1, 2))
    sin = np.tile(np.asarray(sin32, np.float64), (1, 2))
    sc = np.tile(np.asarray(sc32, np.float64), (1, 2))
    prods = [(cos * sc).T, (sin * sc).T, (cos / sc).T, (sin / sc).T]
    tabs_ = np.zeros((8, 128, TABW), np.float32)
    for i, p in enumerate(prods):
        hi = p.astype(np.float32)
        lo_ = (p - hi.astype(np.float64)).astype(np.float32)
        tabs_[i] = np.tile(hi, (2, 1))
        tabs_[4 + i] = np.tile(lo_, (2, 1))
    _TABCACHE[lo] = tabs_
    return tabs_


def kernel(L0_attn_x_seq, L0_attn_Wq, L0_attn_bq, L0_attn_Wk, L0_attn_bk,
           L0_attn_Wv, L0_attn_bv, L0_attn_gamma, L0_ffn_wqkv, L0_ffn_q_scale,
           L0_ffn_k_scale, L0_ffn_wo, L0_ffn_ln_g, L0_ffn_ln_b, L0_ffn_w1,
           L0_ffn_b1, L0_ffn_w2, L0_ffn_b2, L0_c, L0_wmod):
    f32 = np.float32
    x = np.asarray(L0_attn_x_seq, f32)
    nc = build_nc()

    def colz(v):
        return np.ascontiguousarray(np.asarray(v, f32).reshape(8, 128).T)

    w2p = np.zeros((2, HIDP, D), f32)
    w2p[:, :HID] = np.asarray(L0_ffn_w2, f32)
    b1 = np.asarray(L0_ffn_b1, f32)
    b1a = np.zeros((2, 128, NHC), f32)
    b1g = np.zeros((2, 128, NHC), f32)
    for l in range(2):
        pa = np.zeros(HIDP, f32)
        pa[:HID] = b1[l, :HID]
        pg = np.zeros(HIDP, f32)
        pg[:HID] = b1[l, HID:]
        b1a[l] = pa.reshape(NHC, 128).T
        b1g[l] = pg.reshape(NHC, 128).T
    qs = np.asarray(L0_ffn_q_scale, f32)
    ks = np.asarray(L0_ffn_k_scale, f32)

    def tile2(v):
        return np.ascontiguousarray(np.tile(v, 2).reshape(128, 1))

    def perm(v):
        return np.concatenate([v[32:], v[:32]])

    rotm = np.zeros((128, 128), f32)
    for d in range(128):
        h0 = (d // 64) * 64
        dl = d % 64
        rotm[h0 + (dl + 32) % 64, d] = -1.0 if dl < 32 else 1.0
    jj, ii = np.meshgrid(np.arange(128), np.arange(128), indexing="ij")
    m_up = np.where(jj >= ii, 0.0, -BIG).astype(f32)
    m_lo = np.where(jj <= ii, 0.0, -BIG).astype(f32)
    tri01 = (jj <= ii).astype(f32)
    bd2 = np.zeros((128, 2), f32)
    bd2[:64, 0] = 1.0
    bd2[64:, 1] = 1.0
    sel2 = np.zeros((2, 128), f32)
    sel2[0, :64] = 1.0
    sel2[1, 64:] = 1.0

    shared = {
        "wq": np.ascontiguousarray(np.asarray(L0_attn_Wq, f32)),
        "wk": np.ascontiguousarray(np.asarray(L0_attn_Wk, f32)),
        "wv": np.ascontiguousarray(np.asarray(L0_attn_Wv, f32)),
        "bqc": colz(L0_attn_bq), "bkc": colz(L0_attn_bk),
        "bvr": np.asarray(L0_attn_bv, f32).reshape(1, D),
        "wqkv": np.ascontiguousarray(np.asarray(L0_ffn_wqkv, f32)),
        "wo": np.ascontiguousarray(np.asarray(L0_ffn_wo, f32)),
        "w1": np.ascontiguousarray(np.asarray(L0_ffn_w1, f32)),
        "w2p": w2p, "b1a": b1a, "b1g": b1g,
        "b2c": np.stack([colz(np.asarray(L0_ffn_b2, f32)[l]) for l in range(2)]),
        "lgc": np.stack([colz(np.asarray(L0_ffn_ln_g, f32)[l]) for l in range(2)]),
        "lbc": np.stack([colz(np.asarray(L0_ffn_ln_b, f32)[l]) for l in range(2)]),
        "qsc": np.stack([tile2(qs[l]) for l in range(2)]),
        "qspc": np.stack([tile2(perm(qs[l])) for l in range(2)]),
        "ksc": np.stack([tile2(ks[l]) for l in range(2)]),
        "kspc": np.stack([tile2(perm(ks[l])) for l in range(2)]),
        "wmod": np.ascontiguousarray(np.asarray(L0_wmod, f32)),
        "gam": np.asarray(L0_attn_gamma, f32).reshape(1, 1),
        "ident": np.eye(128, dtype=f32),
        "rotm": rotm, "m_up": m_up, "m_lo": m_lo, "tri01": tri01,
        "bd2": bd2, "sel2": sel2,
    }

    in_maps = []
    for c in range(8):
        b, qdx = c // 4, c % 4
        start = OWN * qdx
        lo = start - 3072
        xh = np.zeros((T, D), f32)
        s0 = max(0, lo)
        xh[s0 - lo:] = x[b, s0:start + OWN]
        upos = np.arange(lo, start + OWN)
        valid = (upos >= 0).astype(f32)
        m = dict(shared)
        m["xh"] = np.ascontiguousarray(xh.T.reshape(8, 128, T))
        m["crow"] = np.asarray(L0_c, f32)[b].reshape(1, D)
        m["tabs"] = _host_tables(lo)
        m["validr"] = np.ascontiguousarray(
            np.tile(valid[:VROW_W].reshape(1, VROW_W), (64, 1)))
        m["validc"] = np.ascontiguousarray(valid.reshape(T // 128, 128).T)
        in_maps.append(m)

    res = run_bass_kernel_spmd(nc, in_maps, core_ids=list(range(8)))
    out = np.zeros((2, 8192, D), f32)
    for c in range(8):
        b, qdx = c // 4, c % 4
        r = np.asarray(res.results[c]["out"]).reshape(D, OWN)
        out[b, OWN * qdx:OWN * (qdx + 1)] = r.T
    return out


# version 12
# speedup vs baseline: 1.1875x; 1.0099x over previous
import numpy as np
import concourse.bass as bass
import concourse.mybir as mybir
from concourse.bass_utils import run_bass_kernel_spmd

F32 = mybir.dt.float32
BF16 = mybir.dt.bfloat16
AF = mybir.ActivationFunctionType
ALU = mybir.AluOpType
AX = mybir.AxisListType

D = 1024
OWN = 2048
T = 5120
CH = 512
HEADS = 8
HID = 2730
NHC = 22
HIDP = NHC * 128
QK_SCALE = 8.0
EPS_RET = 1e-6
BIG = 1.0e30

RET_LO, L1A_LO, L1F_LO, L2A_LO, L2F_LO = 512, 1536, 1920, 2560, 3072
T_TAGS = []
CUR_TAG = ["init"]
TAB_LO, TABW = 1024, 4096
AB = 1016
ARENA_W = 4112
VROW_W = 3456
VNEG_W = 3200
NPHASES = 5
SUBB = 3


def build_nc():
    nc = bass.Bass("TRN2", target_bir_lowering=False)

    dr = {}
    for n, shp in [
        ("xh", [8, 128, T]), ("wq", [D, D]), ("wk", [D, D]), ("wv", [D, D]),
        ("bqc", [128, 8]), ("bkc", [128, 8]), ("bvr", [1, D]),
        ("wqkv", [2, D, 1536]), ("wo", [2, 512, D]),
        ("w1", [2, D, 2 * HID]), ("w2p", [2, HIDP, D]),
        ("b1a", [2, 128, NHC]), ("b1g", [2, 128, NHC]), ("b2c", [2, 128, 8]),
        ("lgc", [2, 128, 8]), ("lbc", [2, 128, 8]),
        ("qsc", [2, 128, 1]), ("qspc", [2, 128, 1]),
        ("ksc", [2, 128, 1]), ("kspc", [2, 128, 1]),
        ("crow", [1, D]), ("wmod", [D, 6 * D]), ("gam", [1, 1]),
        ("tabs", [8, 128, TABW]),
        ("validr", [64, VROW_W]), ("validc", [128, T // 128]),
        ("ident", [128, 128]), ("rotm", [128, 128]),
        ("m_up", [128, 128]), ("m_lo", [128, 128]), ("tri01", [128, 128]),
        ("bd2", [128, 2]), ("sel2", [2, 128]),
    ]:
        dr[n] = nc.dram_tensor(n, shp, F32, kind="ExternalInput")
    out_d = nc.dram_tensor("out", [8, 128, OWN], F32, kind="ExternalOutput")
    xa_d = nc.dram_tensor("xa_scr", [8, 128, OWN], F32, kind="Internal")

    def ap(n):
        return dr[n].ap()

    SB_BASE = 17408

    def at(name, shape, dtype, off):
        return nc.alloc_sbuf_tensor_at(name, shape, dtype, offset=SB_BASE + off)

    arena = [at(f"arena{f}", [128, ARENA_W], BF16, f * ARENA_W * 2)
             for f in range(8)]
    B0 = 8 * ARENA_W * 2
    wqT = [at(f"wqT{i}", [128, 1024], BF16, B0 + i * 2048) for i in range(8)]
    wkT = [at(f"wkT{i}", [128, 1024], BF16, B0 + 16384 + i * 2048) for i in range(8)]
    wvT = [at(f"wvT{i}", [128, 1024], BF16, B0 + 32768 + i * 2048) for i in range(8)]
    kring = [[at(f"krk{c}_{i}", [128, 512], BF16, B0 + 49152 + (c * 8 + i) * 1024)
              for i in range(8)] for c in range(2)]
    vring = [[at(f"vrk{c}_{i}", [128, 1024], BF16, B0 + 65536 + (c * 4 + i) * 2048)
              for i in range(4)] for c in range(2)]
    bvb = [at(f"bvb{i}", [128, 512], BF16, B0 + 81920 + i * 1024) for i in range(2)]
    gpow = at("gpow", [1, 512], F32, B0 + 83968)
    tabs = [at(f"tab{i}", [128, TABW], BF16, B0 + i * TABW * 2) for i in range(4)]
    wqkv_sb = [at(f"wqkv{i}", [128, 1536], BF16, B0 + 32768 + i * 3072)
               for i in range(8)]
    wo_sb = [at(f"wo{i}", [128, 1024], BF16, B0 + 57344 + i * 2048) for i in range(4)]
    khat = [[at(f"khat{c}_{i}", [128, 512], BF16, B0 + 65536 + (c * 4 + i) * 1024)
             for i in range(4)] for c in range(2)]
    vtm = [[at(f"vtm{c}_{i}", [128, 512], BF16, B0 + 73728 + (c * 4 + i) * 1024)
            for i in range(4)] for c in range(2)]
    w1_sb = [at(f"w1_{i}", [128, 5460], BF16, B0 + i * 10944) for i in range(8)]

    C0 = B0 + 87552
    _cc = [C0]

    def cal(name, shape, dtype):
        t = at(name, shape, dtype, _cc[0])
        per = int(np.prod(shape[1:])) * (2 if dtype == BF16 else 4)
        _cc[0] += (per + 31) // 32 * 32
        return t

    tmpA = cal("tmpA", [128, 512], F32)
    tmpB = cal("tmpB", [128, 512], F32)
    tmpC = cal("tmpC", [128, 512], F32)
    tmpCb = at("tmpCb", [128, 512], BF16, 8 * ARENA_W * 2 + 87552 + 4096)
    rows4 = cal("rows4", [64, 512], F32)
    rowaux = cal("rowaux", [64, 512], F32)
    denrow = cal("denrow", [64, 512], F32)
    vrow = cal("vrow", [64, VROW_W], BF16)
    identsb = cal("identsb", [128, 128], F32)
    onesc = cal("onesc", [64, 128], F32)
    ones128 = cal("ones128", [128, 1], F32)
    tri01_sb = cal("tri01", [128, 128], BF16)
    modc = cal("modc", [128, 48], F32)
    gac = cal("gac", [128, 8], F32)
    gbc = cal("gbc", [128, 8], F32)
    ca1c = cal("ca1c", [128, 8], F32)
    cb1c = cal("cb1c", [128, 8], F32)
    tgat = cal("tgat", [128, 8], F32)
    sgam = cal("sgam", [1, 1], F32)
    lng = cal("lng", [1, 1], F32)
    lngc = cal("lngc", [128, 1], F32)
    gnegc = cal("gnegc", [128, 4], F32)
    gprevc = cal("gprevc", [128, 4], F32)
    iotac = cal("iotac", [128, 4], F32)
    gmk8 = cal("gmk8", [128, 8], F32)
    gmk = [gmk8[:, i:i + 1] for i in range(8)]
    vcol = cal("vcol", [128, T // 128], F32)
    bq_sb = cal("bq", [128, 8], F32)
    bk_sb = cal("bk", [128, 8], F32)
    crow_sb = cal("crowsb", [128, 8], F32)
    b1a_sb = cal("b1a", [128, NHC], F32)
    b1g_sb = cal("b1g", [128, NHC], F32)
    b2_sb = cal("b2", [128, 8], F32)
    lg_sb = cal("lgcs", [128, 8], F32)
    lb_sb = cal("lbcs", [128, 8], F32)
    qs8_sb = cal("qs8", [128, 1], F32)
    qs8p_sb = cal("qs8p", [128, 1], F32)
    ks_sb = cal("ks", [128, 1], F32)
    ksp_sb = cal("ksp", [128, 1], F32)
    bd2_sb = cal("bd2", [128, 2], F32)
    sel2_sb = cal("sel2", [2, 128], F32)
    outp = cal("outp", [128, 128], F32)
    eps5 = cal("eps5", [2, 1], F32)
    eps24 = cal("eps24", [2, 1], F32)
    onescb = cal("onescb", [64, 256], BF16)
    ones128b = cal("ones128b", [128, 1], BF16)
    crow_bf = cal("crowbf", [128, 8], BF16)
    PB = 176832
    assert _cc[0] <= PB, _cc[0]
    xfm = [at(f"xfm{i}", [128, 512], F32, PB + i * 2048) for i in range(8)]
    hfm = [at(f"hfm{i}", [128, 512], BF16, PB + 16384 + i * 1024) for i in range(8)]
    Abuf = [at(f"Ab{i}", [128, 512], BF16, PB + 16384 + i * 1024) for i in range(8)]
    qfm = [at(f"qfm{i}", [128, 512], BF16, PB + 24576 + i * 1024) for i in range(8)]
    xstage = at("xstage", [128, 1024], F32, PB + 16384)
    wstage = at("wstage", [128, 1024], F32, PB)
    qhat = [at(f"qhat{i}", [128, 512], BF16, PB + i * 1024) for i in range(4)]
    obuf = [at(f"obuf{i}", [128, 512], BF16, PB + 4096 + i * 1024) for i in range(4)]
    qsbuf = at("qsbuf", [128, 512], BF16, PB + 8192)
    pbuf = [at(f"pbuf{i}", [128, 1536], BF16, PB + 9216 + i * 3072)
            for i in range(2)]
    vnegrow = at("vnegrow", [1, VNEG_W], BF16, PB + 15360)
    m_up_sb = at("m_up", [128, 128], BF16, PB + 21760)
    m_lo_sb = at("m_lo", [128, 128], BF16, PB + 22016)
    rotm_sb = at("rotm", [128, 128], BF16, PB + 22272)
    nrow4 = at("nrow4", [1, 512], BF16, PB + 30720)
    nmax = at("nmax", [128, 16], F32, PB + 31744)
    vnegc = at("vnegc", [128, 40], F32, PB + 31808)
    tablo = at("tablo", [128, TABW], BF16, PB + 22528)
    gated = [at(f"gat{i}", [128, 512], BF16, PB + i * 1024) for i in range(NHC)]
    lnh = [at(f"lnh{i}", [128, 512], BF16, PB + 22528 + i * 1024) for i in range(8)]
    w2r = at("w2r", [128, 1024], BF16, PB + 30720)
    gelbuf = at("gelbuf", [128, 512], BF16, PB + 32768)
    validB = at("validB", [128, 512], BF16, PB + 33792)
    xaring = tmpA
    assert 17408 + PB + 34816 <= 229344, PB

    pb = [nc.alloc_psum_tensor(f"pb{i}", [128, 512], F32) for i in range(8)]

    ops = []
    counts = {"t": 0, "v": 0, "s": 0, "g": 0, "gd": 0, "y": 0}

    def _add(eng, sem, amt, emit, waits):
        prev = counts[sem]
        counts[sem] += amt
        ws = [w for w in waits if w is not None]
        if sem in ("gd", "y") and prev > 0:
            ws.append((sem, prev, "~"))
        ops.append((eng, emit, ws, (sem, counts[sem], amt)))
        return (sem, counts[sem], eng)

    def Tm(emit, waits=()):
        T_TAGS.append(CUR_TAG[0])
        return _add("T", "t", 1, emit, waits)

    def Vm(emit, waits=()):
        return _add("V", "v", 1, emit, waits)

    def Sm(emit, waits=()):
        return _add("S", "s", 1, emit, waits)

    def Gm(emit, waits=()):
        return _add("G", "g", 1, emit, waits)

    def GD(emit, waits=()):
        return _add("G", "gd", 16, emit, waits)

    def Ym(emit, waits=()):
        return _add("Y", "y", 16, emit, waits)

    def barrier():
        return [(s, counts[s], "~") for s in counts if counts[s] > 0]

    bankR = [[] for _ in range(8)]
    bankW = [None] * 8

    def bacq(bs):
        w = []
        for b in bs:
            w += bankR[b]
            bankR[b] = []
        return w

    def bwr(bs, tok):
        for b in bs:
            bankW[b] = tok

    def brd(bs, tok):
        for b in bs:
            bankR[b].append(tok)
        return tok

    tl = {}
    for n, dst in [("validc", vcol), ("ident", identsb),
                   ("bd2", bd2_sb), ("sel2", sel2_sb),
                   ("gam", sgam), ("bqc", bq_sb), ("bkc", bk_sb)]:
        tl[n] = Ym(lambda y, d=dst, n=n: y.dma_start(d[:, :], ap(n)))
    for n, dst in [("validr", vrow), ("tri01", tri01_sb)]:
        tl[n] = GD(lambda g, d=dst, n=n: g.dma_start(d[:, :], ap(n)))
    tl["crow"] = Ym(lambda y: y.dma_start(
        crow_sb[:, :], ap("crow").rearrange("o (j p) -> (o p) j", p=128),
        allow_slow_non_contiguous=True))
    t_ones = Vm(lambda v: v.memset(onesc[:, :], 1.0))
    t_ones128 = Vm(lambda v: v.memset(ones128[:, :], 1.0))
    t_onesb = Vm(lambda v: v.memset(onescb[:, :], 1.0))
    t_eps5 = Vm(lambda v: v.memset(eps5[:, :], 1e-5))
    t_eps24 = Vm(lambda v: v.memset(eps24[:, :], 1e-24))
    t_ones128b = Vm(lambda v: v.memset(ones128b[:, :], 1.0))
    t_iota = Gm(lambda g: g.iota(iotac[:, :], [[128, 4]], channel_multiplier=1,
                                 allow_small_or_imprecise_dtypes=True))
    t_iotar = Gm(lambda g: g.iota(gpow[:, :], [[1, 512]], channel_multiplier=0,
                                  allow_small_or_imprecise_dtypes=True))

    t_lng = Sm(lambda s: s.activation(lng[:, :], sgam[:, :], AF.Ln),
               waits=[tl["gam"]])
    tk_ = Tm(lambda t: t.matmul(pb[5][:, 0:1], onesc[0:1, :], lng[:, :],
                                start=True, stop=True),
             waits=[t_lng, t_ones] + bacq([5]))
    bwr([5], tk_)
    t_lngc = brd([5], Vm(lambda v: v.tensor_copy(lngc[:, :], pb[5][:, 0:1]),
                         waits=[tk_]))
    t1 = Vm(lambda v: v.tensor_scalar(gpow[:, :], gpow[:, :], lng[0:1, 0:1], None,
                                      ALU.mult), waits=[t_iotar, t_lng])
    t_gpow = Sm(lambda s: s.activation(gpow[:, :], gpow[:, :], AF.Exp), waits=[t1])
    t1 = Vm(lambda v: v.tensor_scalar(gnegc[:, :], iotac[:, :], lngc[:, :], -1.0,
                                      ALU.mult, ALU.mult), waits=[t_iota, t_lngc])
    t_gneg = Sm(lambda s: s.activation(gnegc[:, :], gnegc[:, :], AF.Exp), waits=[t1])
    t1 = Vm(lambda v: v.tensor_scalar(tmpA[:, 0:1], lngc[:, :], 512.0, None,
                                      ALU.mult), waits=[t_lngc])
    t2 = Vm(lambda v: v.tensor_scalar(gprevc[:, :], iotac[:, :], lngc[:, :], -1.0,
                                      ALU.mult, ALU.mult), waits=[t_iota, t_lngc])
    t3 = Vm(lambda v: v.tensor_scalar(gprevc[:, :], gprevc[:, :], tmpA[:, 0:1],
                                      None, ALU.add), waits=[t1, t2])
    t_gprev = Sm(lambda s: s.activation(gprevc[:, :], gprevc[:, :], AF.Exp),
                 waits=[t3])
    t_b1 = Ym(lambda y: y.dma_start(tmpA[0:1, 0:512], ap("bvr")[:, 0:512]),
              waits=[t1, t3])
    t_b2 = Ym(lambda y: y.dma_start(tmpB[0:1, 0:512], ap("bvr")[:, 512:1024]))
    tm1 = Tm(lambda t: t.matmul(pb[5][:, 0:512], onesc[0:1, :], tmpA[0:1, 0:512],
                                start=True, stop=True),
             waits=[t_b1, t_ones] + bacq([5]))
    bwr([5], tm1)
    t_bvb0 = brd([5], Vm(lambda v: v.tensor_copy(bvb[0][:, :], pb[5][:, 0:512]),
                         waits=[tm1]))
    tm2 = Tm(lambda t: t.matmul(pb[5][:, 0:512], onesc[0:1, :], tmpB[0:1, 0:512],
                                start=True, stop=True),
             waits=[t_b2] + bacq([5]))
    bwr([5], tm2)
    t_bvb1 = brd([5], Vm(lambda v: v.tensor_copy(bvb[1][:, :], pb[5][:, 0:512]),
                         waits=[tm2]))

    t_silu = Sm(lambda s: s.activation(crow_bf[:, :], crow_sb[:, :], AF.Silu),
                waits=[tl["crow"]])
    prev_ld = None
    mm_last = None
    first_mod = True
    for cg in range(12):
        lo = cg * 512
        lds = []
        for kc in range(8):
            lds.append(GD(lambda g, kc=kc, lo=lo: g.dma_start(
                kring[0][kc][:, :], ap("wmod")[kc * 128:(kc + 1) * 128, lo:lo + 512]),
                waits=[prev_ld]))
        for j in range(4):
            vf = cg * 4 + j
            for kc in range(8):
                w = [lds[kc], t_silu, mm_last]
                if first_mod:
                    w += bacq([4])
                    first_mod = False
                mm_last = Tm(lambda t, kc=kc, j=j, vf=vf: t.matmul(
                    pb[4][:, vf:vf + 1], kring[0][kc][:, j * 128:(j + 1) * 128],
                    crow_bf[:, kc:kc + 1], start=(kc == 0), stop=(kc == 7)),
                    waits=w)
        prev_ld = mm_last
    bwr([4], mm_last)
    t_modc = brd([4], Vm(lambda v: v.tensor_copy(modc[:, :], pb[4][:, 0:48]),
                         waits=[mm_last]))
    t_ca1 = Vm(lambda v: v.tensor_scalar(ca1c[:, :], modc[:, 8:16], 1.0, None,
                                         ALU.add), waits=[t_modc])
    t_cb1 = Vm(lambda v: v.tensor_scalar(cb1c[:, :], modc[:, 32:40], 1.0, None,
                                         ALU.add), waits=[t_modc])

    def gtrans(dst, c0, w0):
        a1 = Sm(lambda s: s.activation(tgat[:, :], modc[:, c0:c0 + 8], AF.Abs),
                waits=[t_modc, w0])
        a2 = Vm(lambda v: v.tensor_scalar(tgat[:, :], tgat[:, :], 0.25, None,
                                          ALU.max), waits=[a1])
        a3 = Sm(lambda s: s.activation(dst[:, :], modc[:, c0:c0 + 8], AF.Sign),
                waits=[a2])
        return Vm(lambda v: v.tensor_mul(dst[:, :], dst[:, :], tgat[:, :]),
                  waits=[a3])
    t_ga = gtrans(gac, 16, None)
    t_gb = gtrans(gbc, 40, t_ga)

    tok_wT = t_modc
    ti = 0
    for wname, wdst in [("wq", wqT), ("wk", wkT), ("wv", wvT)]:
        for r in range(8):
            tld = Ym(lambda y, wn=wname, r=r: y.dma_start(
                wstage[:, :], ap(wn)[r * 128:(r + 1) * 128, :]),
                waits=[tok_wT])
            for c in range(8):
                bk_ = 6 + ti % 2
                ti += 1
                ttr = Tm(lambda t, c=c, bk_=bk_: t.transpose(
                    pb[bk_][:, 0:128], wstage[:, c * 128:(c + 1) * 128],
                    identsb[:, :]),
                    waits=[tld, tl["ident"]] + bacq([bk_]))
                bwr([bk_], ttr)
                tok_wT = brd([bk_], Vm(lambda v, wd=wdst, r=r, c=c, bk_=bk_:
                                       v.tensor_copy(
                    wd[c][:, r * 128:(r + 1) * 128], pb[bk_][:, 0:128]),
                    waits=[ttr]))

    def ln_stats2(x_src, x_waits, ones_lhs, ones_tok, extra_first=()):
        w5 = bacq([5])
        w4 = bacq([4])
        tsum = None
        for f in range(8):
            w = [x_waits[f], ones_tok, tsum] + \
                (list(extra_first) + w5 + w4 if f == 0 else [])
            tsum = Tm(lambda t, f=f: t.matmul(
                pb[5][0:1, 0:512], ones_lhs[:, :], x_src[f],
                start=(f == 0), stop=(f == 7)), waits=w)
        bwr([5], tsum)
        tsq = None
        for f in range(8):
            s1 = Sm(lambda s, f=f: s.activation(tmpCb[:, :], x_src[f],
                                                AF.Square),
                    waits=[x_waits[f], tsq])
            tsq = Tm(lambda t, f=f: t.matmul(
                pb[4][0:1, 0:512], ones128b[:, :], tmpCb[:, :],
                start=(f == 0), stop=(f == 7)),
                waits=[s1, t_ones128b] + ([tsum] if f == 0 else []))
        bwr([4], tsq)
        r1 = brd([5], Vm(lambda v: v.tensor_scalar(
            rowaux[0:1, :], pb[5][0:1, 0:512], 1.0 / D, None, ALU.mult),
            waits=[tsum, tsq]))
        r2 = brd([4], Vm(lambda v: v.tensor_scalar(
            rows4[0:1, :], pb[4][0:1, 0:512], 1.0 / D, None, ALU.mult),
            waits=[tsq]))
        r3 = brd([4], Vm(lambda v: v.tensor_mul(
            pb[4][0:1, 0:512], rowaux[0:1, :], rowaux[0:1, :]),
            waits=[r1, r2]))
        r4 = brd([4], Vm(lambda v: v.tensor_sub(
            rows4[0:1, :], rows4[0:1, :], pb[4][0:1, 0:512]), waits=[r3]))
        r5 = Sm(lambda s: s.activation(rows4[0:1, :], rows4[0:1, :], AF.Ln,
                                       bias=eps5[0:1, :]), waits=[r4, t_eps5])
        return Sm(lambda s: s.activation(rows4[0:1, :], rows4[0:1, :], AF.Exp,
                                         scale=-0.5), waits=[r5])

    def ln_broadcasts(t_inv, u0, do_valid):
        tiv = t_inv
        if do_valid:
            tiv = Vm(lambda v, u0=u0: v.tensor_mul(
                rows4[0:1, :], rows4[0:1, :], vrow[0:1, u0:u0 + 512]),
                waits=[t_inv, tl["validr"]])
        tb_ = Vm(lambda v: v.tensor_scalar(rowaux[0:1, :], rowaux[0:1, :], -1.0,
                                           None, ALU.mult), waits=[tiv])
        tb_ = Vm(lambda v: v.tensor_mul(rowaux[0:1, :], rowaux[0:1, :],
                                        rows4[0:1, :]), waits=[tb_])
        tB1 = Tm(lambda t: t.matmul(pb[6][:, 0:512], onesc[0:1, :],
                                    rows4[0:1, :], start=True, stop=True),
                 waits=[tiv, t_ones] + bacq([6]))
        bwr([6], tB1)
        tB2 = Tm(lambda t: t.matmul(pb[7][:, 0:512], onesc[0:1, :],
                                    rowaux[0:1, :], start=True, stop=True),
                 waits=[tb_] + bacq([7]))
        bwr([7], tB2)
        tVB = None
        if do_valid:
            tVB = Tm(lambda t, u0=u0: t.matmul(
                pb[4][:, 0:512], onescb[0:1, 0:128], vrow[0:1, u0:u0 + 512],
                start=True, stop=True), waits=[tiv, t_onesb] + bacq([4]))
            bwr([4], tVB)
        return tB1, tB2, tVB

    tok_y0 = [None] * 8
    tok_x_free = None
    tok_kv_free = [None, None]
    tok_k_ready = [[None] * 8, [None] * 8]
    tok_v_ready = [[None] * 4, [None] * 4]
    tok_xa_st = None
    t_abuf_free = None

    for ci in range(10 if NPHASES >= 1 else 0):
        CUR_TAG[0] = 'p1'
        u0 = ci * CH
        par = ci % 2
        pp = 1 - par
        txf = [None] * 8
        for f in range(8):
            txf[f] = Ym(lambda y, u0=u0, f=f: y.dma_start(
                xfm[f][:, 0:512], ap("xh")[f, :, u0:u0 + 512]),
                waits=[tok_x_free, txf[f - 1] if f else None,
                       tok_wT if ci == 0 and f == 0 else None])
        xtoks = list(txf)

        xsrc = [xfm[f][:, 0:512] for f in range(8)]
        t_inv = ln_stats2(xsrc, xtoks, ones128, t_ones128)
        tB1, tB2, _ = ln_broadcasts(t_inv, u0, False)
        th = [None] * 8
        for f in range(8):
            a1 = brd([6], Vm(lambda v, f=f: v.tensor_mul(
                tmpB[:, :], xfm[f][:, :], pb[6][:, 0:512]),
                waits=[tB1, xtoks[f], th[f - 1] if f else None]))
            a2 = brd([7], Vm(lambda v: v.tensor_add(
                tmpB[:, :], tmpB[:, :], pb[7][:, 0:512]), waits=[a1, tB2]))
            th[f] = Vm(lambda v, f=f: v.tensor_scalar(
                hfm[f][:, :], tmpB[:, :], ca1c[:, f:f + 1], modc[:, f:f + 1],
                ALU.mult, ALU.add), waits=[a2, t_ca1, t_modc])

        tq = [None] * 8
        tkk = [None] * 8
        for e in range(8):
            bk_ = e % 2
            mm = None
            for dc in range(8):
                w = [th[dc], mm]
                if dc == 0:
                    w += [tok_wT] + bacq([bk_])
                mm = Tm(lambda t, e=e, dc=dc, bk_=bk_: t.matmul(
                    pb[bk_][:, 0:512], wqT[dc][:, e * 128:(e + 1) * 128],
                    hfm[dc][:, :], start=(dc == 0), stop=(dc == 7)), waits=w)
            bwr([bk_], mm)
            tq[e] = brd([bk_], Sm(lambda s, e=e, bk_=bk_: s.activation(
                qfm[e][:, :], pb[bk_][:, 0:512], AF.Relu,
                bias=bq_sb[:, e:e + 1]), waits=[mm, tl["bqc"]]))
        for e in range(8):
            bk_ = e % 2
            mm = None
            for dc in range(8):
                w = [th[dc], mm]
                if dc == 0:
                    w += bacq([bk_])
                mm = Tm(lambda t, e=e, dc=dc, bk_=bk_: t.matmul(
                    pb[bk_][:, 0:512], wkT[dc][:, e * 128:(e + 1) * 128],
                    hfm[dc][:, :], start=(dc == 0), stop=(dc == 7)), waits=w)
            bwr([bk_], mm)
            tkk[e] = brd([bk_], Sm(lambda s, e=e, bk_=bk_, par=par: s.activation(
                kring[par][e][:, :], pb[bk_][:, 0:512], AF.Relu,
                bias=bk_sb[:, e:e + 1]),
                waits=[mm, tl["bkc"], tok_kv_free[par]]))
            tok_k_ready[par][e] = tkk[e]
        tv = [None] * 4
        for st in range(4):
            for half in range(2):
                bk_ = half
                mm = None
                for dc in range(8):
                    w = [th[dc], mm]
                    if dc == 0:
                        w += bacq([bk_])
                    mm = Tm(lambda t, st=st, half=half, dc=dc, bk_=bk_: t.matmul(
                        pb[bk_][:, 0:512], hfm[dc][:, st * 128:(st + 1) * 128],
                        wvT[dc][:, half * 512:(half + 1) * 512],
                        start=(dc == 0), stop=(dc == 7)), waits=w)
                bwr([bk_], mm)
                tv[st] = brd([bk_], Vm(lambda v, st=st, half=half, bk_=bk_,
                                       par=par: v.tensor_add(
                    vring[par][st][:, half * 512:(half + 1) * 512],
                    pb[bk_][:, 0:512], bvb[half][:, :]),
                    waits=[mm, tok_kv_free[par], t_bvb0, t_bvb1, tv[st]]))
            tok_v_ready[par][st] = tv[st]

        if ci == 0:
            tok_x_free = th[7]
            t_abuf_free = tv[3]
            continue

        tgm = [None] * 8
        for jp in range(8):
            jt = jp % 4
            uj = (u0 - CH if jp < 4 else u0) + jt * 128
            gcol = gprevc if jp < 4 else gnegc
            tgm[jp] = Vm(lambda v, jp=jp, jt=jt, uj=uj, gcol=gcol: v.tensor_mul(
                gmk[jp], gcol[:, jt:jt + 1],
                vcol[:, uj // 128:uj // 128 + 1]),
                waits=[t_gneg, t_gprev, tl["validc"]])
        tA = [None] * 8
        for jp in range(8):
            sp = pp if jp < 4 else par
            jt = jp % 4
            bk_ = 2 + jp % 2
            mm = None
            for ec in range(8):
                w = [tq[ec], tok_k_ready[sp][ec], mm]
                if ec == 0:
                    w += bacq([bk_])
                mm = Tm(lambda t, jt=jt, ec=ec, bk_=bk_, sp=sp: t.matmul(
                    pb[bk_][:, 0:512], kring[sp][ec][:, jt * 128:(jt + 1) * 128],
                    qfm[ec][:, :], start=(ec == 0), stop=(ec == 7)), waits=w)
            bwr([bk_], mm)
            ilo = 0 if jp < 4 else jt * 128
            ev = brd([bk_], Vm(lambda v, jp=jp, bk_=bk_: v.tensor_scalar(
                Abuf[jp][:, 0:512], pb[bk_][:, 0:512], gmk[jp], None,
                ALU.mult), waits=[mm, tgm[jp]]))
            if jp >= 4:
                ev = Vm(lambda v, jp=jp, jt=jt: v.tensor_mul(
                    Abuf[jp][:, jt * 128:(jt + 1) * 128],
                    Abuf[jp][:, jt * 128:(jt + 1) * 128], tri01_sb[:, :]),
                    waits=[ev, tl["tri01"]])
            tA[jp] = ev
        tden = None
        w0 = bacq([4])
        for jp in range(8):
            ilo = 0 if jp < 4 else (jp % 4) * 128
            tden = Tm(lambda t, jp=jp, ilo=ilo: t.matmul(
                pb[4][0:1, ilo:512], ones128b[:, :], Abuf[jp][:, ilo:512],
                start=(jp == 0), stop=(jp == 7)),
                waits=[tA[jp], tden, t_ones128b] + (w0 if jp == 0 else []))
        bwr([4], tden)
        rr = brd([4], Vm(lambda v: v.tensor_mul(
            denrow[0:1, :], pb[4][0:1, 0:512], gpow[:, :]),
            waits=[tden, t_gpow]))
        rr = Vm(lambda v: v.tensor_scalar(denrow[0:1, :], denrow[0:1, :],
                                          EPS_RET, None, ALU.add), waits=[rr])
        rr = Vm(lambda v: v.reciprocal(denrow[0:1, :], denrow[0:1, :]), waits=[rr])
        rr = Vm(lambda v: v.tensor_mul(denrow[0:1, :], denrow[0:1, :],
                                       gpow[:, :]), waits=[rr])
        rbt = Tm(lambda t: t.matmul(pb[4][:, 0:512], onesc[0:1, :], denrow[0:1, :],
                                    start=True, stop=True),
                 waits=[rr] + bacq([4]))
        bwr([4], rbt)
        tAr = [None] * 8
        for jp in range(8):
            ilo = 0 if jp < 4 else (jp % 4) * 128
            tAr[jp] = brd([4], Vm(lambda v, jp=jp, ilo=ilo: v.tensor_mul(
                Abuf[jp][:, ilo:512], Abuf[jp][:, ilo:512], pb[4][:, ilo:512]),
                waits=[rbt, tA[jp]]))
        t_xa = [None] * 8
        for dv in range(8):
            bk_ = 2 + dv % 2
            mm = None
            for jp in range(8):
                sp = pp if jp < 4 else par
                jt = jp % 4
                ilo = 0 if jp < 4 else jt * 128
                w = [tAr[jp], tok_v_ready[sp][jt], mm]
                if jp == 0:
                    w += bacq([bk_])
                mm = Tm(lambda t, dv=dv, jt=jt, ilo=ilo, bk_=bk_, sp=sp, jp=jp:
                        t.matmul(
                    pb[bk_][:, ilo:512],
                    vring[sp][jt][:, dv * 128:(dv + 1) * 128],
                    Abuf[jp][:, ilo:512], start=(jp == 0), stop=(jp == 7)),
                    waits=w)
            bwr([bk_], mm)
            t_xa[dv] = brd([bk_], Vm(lambda v, dv=dv, bk_=bk_:
                                     v.scalar_tensor_tensor(
                xfm[dv][:, :], pb[bk_][:, 0:512], gac[:, dv:dv + 1],
                xfm[dv][:, :], ALU.mult, ALU.add),
                waits=[mm, t_ga, xtoks[dv]]))
        tok_kv_free[pp] = t_xa[7]
        t_abuf_free = t_xa[7]

        if u0 >= L2F_LO:
            for f in range(8):
                tok_xa_st = Ym(lambda y, f=f, u0=u0: y.dma_start(
                    xa_d.ap()[f, :, u0 - L2F_LO:u0 - L2F_LO + 512],
                    xfm[f][:, 0:512]), waits=[t_xa[f], tok_xa_st])

        t_inv2 = ln_stats2(xsrc, t_xa, ones128, t_ones128)
        do_v = u0 < L2F_LO
        tB1b, tB2b, tVBb = ln_broadcasts(t_inv2, u0, do_v)
        lo_w = 504 if ci == 1 else 0
        for f in range(8):
            a1 = brd([6], Vm(lambda v, f=f, lo_w=lo_w: v.tensor_mul(
                tmpB[:, lo_w:512], xfm[f][:, lo_w:512], pb[6][:, lo_w:512]),
                waits=[tB1b, t_xa[f], tok_y0[f - 1] if f else None]))
            a2 = brd([7], Vm(lambda v, lo_w=lo_w: v.tensor_add(
                tmpB[:, lo_w:512], tmpB[:, lo_w:512], pb[7][:, lo_w:512]),
                waits=[a1, tB2b]))
            a3 = Vm(lambda v, f=f, lo_w=lo_w: v.tensor_scalar(
                tmpB[:, lo_w:512], tmpB[:, lo_w:512], cb1c[:, f:f + 1], None,
                ALU.mult), waits=[a2, t_cb1])
            dst_c = u0 - AB + lo_w
            if do_v:
                tok_y0[f] = brd([4], Vm(lambda v, f=f, lo_w=lo_w, dst_c=dst_c:
                                        v.scalar_tensor_tensor(
                    arena[f][:, dst_c:dst_c + 512 - lo_w], pb[4][:, lo_w:512],
                    modc[:, 24 + f:24 + f + 1], tmpB[:, lo_w:512],
                    ALU.mult, ALU.add), waits=[a3, tVBb, t_modc]))
            else:
                tok_y0[f] = Vm(lambda v, f=f, dst_c=dst_c: v.tensor_scalar(
                    arena[f][:, dst_c:dst_c + 512], tmpB[:, 0:512],
                    modc[:, 24 + f:24 + f + 1], None, ALU.add),
                    waits=[a3, t_modc])
        tok_x_free = tok_y0[7]

    tok_p1 = barrier()

    def attn_phase(l, lo_keys, lo_q, in_base, out_base, prevb):
        CUR_TAG[0] = f'a{l}-pre'
        tw = {}
        for i in range(4):
            tw[("tab", i)] = GD(lambda g, i=i: g.dma_start(
                tabs[i][:, :], ap("tabs")[i, :, :]), waits=prevb)
        tadd = [None] * 4
        for i in range(4):
            tlo = GD(lambda g, i=i: g.dma_start(
                tablo[:, :], ap("tabs")[4 + i, :, :]),
                waits=prevb + [tadd[i - 1] if i else None])
            tadd[i] = Vm(lambda v, i=i: v.tensor_add(
                tabs[i][:, :], tabs[i][:, :], tablo[:, :]),
                waits=[tw[("tab", i)], tlo])
        for i in range(8):
            tw[("wqkv", i)] = GD(lambda g, i=i, l=l: g.dma_start(
                wqkv_sb[i][:, :], ap("wqkv")[l, i * 128:(i + 1) * 128, :]),
                waits=prevb)
        for i in range(4):
            tw[("wo", i)] = GD(lambda g, i=i, l=l: g.dma_start(
                wo_sb[i][:, :], ap("wo")[l, i * 128:(i + 1) * 128, :]),
                waits=prevb)
        for nm, dd in [("rotm", rotm_sb), ("m_up", m_up_sb), ("m_lo", m_lo_sb)]:
            tw[nm] = GD(lambda g, nm=nm, dd=dd: g.dma_start(dd[:, :], ap(nm)),
                        waits=prevb)
        for nm, dd in [("qsc", qs8_sb), ("qspc", qs8p_sb), ("ksc", ks_sb),
                       ("kspc", ksp_sb)]:
            tw[nm] = Ym(lambda y, nm=nm, dd=dd, l=l: y.dma_start(
                dd[:, :], ap(nm)[l]), waits=prevb)
        t_vneg = Vm(lambda v: v.tensor_scalar(
            vnegrow[:, :], vrow[0:1, 0:VNEG_W], -1.0, BIG, ALU.add, ALU.mult),
            waits=[tl["validr"]] + prevb)
        t_vnegc = Vm(lambda v: v.tensor_scalar(
            vnegc[:, :], vcol[:, :], -1.0, BIG, ALU.add, ALU.mult),
            waits=[tl["validc"]] + prevb)
        t8a = Vm(lambda v: v.tensor_scalar(qs8_sb[:, :], qs8_sb[:, :], QK_SCALE,
                                           None, ALU.mult), waits=[tw["qsc"]])
        t8b = Vm(lambda v: v.tensor_scalar(qs8p_sb[:, :], qs8p_sb[:, :],
                                           QK_SCALE, None, ALU.mult),
                 waits=[tw["qspc"]])
        ttab = [None] * 4
        for i, ctok in enumerate([t8a, t8b, tw["ksc"], tw["kspc"]]):
            col = [qs8_sb, qs8p_sb, ks_sb, ksp_sb][i]
            ttab[i] = Vm(lambda v, i=i, col=col: v.tensor_scalar(
                tabs[i][:, :], tabs[i][:, :], col[:, :], None, ALU.mult),
                waits=[tadd[i], ctok])

        tok_out = [None] * 8
        tok_kr = [[None] * 4, [None] * 4]
        tok_vr = [[None] * 4, [None] * 4]
        tok_kfree = [None, None]
        t_qhat = [None] * 4
        t_sB = None

        def h1s(dc, c0, w, u):
            off = (u - in_base) - (1 if dc >= 4 else 0) + c0
            return arena[dc][:, off:off + w]

        def stage_a(sb_u):
            CUR_TAG[0] = f'a{l}-sa'
            ps = (sb_u // 512) % 2
            tq_l = {}
            prev_x3 = [None]
            prev_sq = [None]
            prev_final = [None]
            for grp in range(2):
                for ept in range(4):
                    e = grp * 4 + ept
                    bk_ = e % 2
                    mm = None
                    for dc in range(8):
                        w = [tw[("wqkv", dc)], mm]
                        if dc == 0:
                            w += bacq([bk_]) + [t_sB]
                        mm = Tm(lambda t, e=e, dc=dc, bk_=bk_, sb_u=sb_u:
                                t.matmul(
                            pb[bk_][:, 0:512],
                            wqkv_sb[dc][:, e * 128:(e + 1) * 128],
                            h1s(dc, 0, 512, sb_u),
                            start=(dc == 0), stop=(dc == 7)), waits=w)
                    bwr([bk_], mm)
                    c1 = brd([bk_], Sm(lambda s, bk_=bk_: s.activation(
                        qsbuf[:, :], pb[bk_][:, 0:512], AF.Copy),
                        waits=[mm, prev_sq[0]]))
                    rmm = Tm(lambda t: t.matmul(
                        pb[4][:, 0:512], rotm_sb[:, :], qsbuf[:, :],
                        start=True, stop=True),
                        waits=[c1, tw["rotm"]] + bacq([4]))
                    bwr([4], rmm)
                    sq = Gm(lambda g: g.tensor_mul(tmpC[:, :], qsbuf[:, :],
                                                   qsbuf[:, :]), waits=[c1])
                    prev_sq[0] = sq
                    ssm = Tm(lambda t: t.matmul(
                        pb[5][0:2, 0:512], bd2_sb[:, :], tmpC[:, :],
                        start=True, stop=True),
                        waits=[sq, tl["bd2"]] + bacq([5]))
                    bwr([5], ssm)
                    iv = brd([5], Sm(lambda s: s.activation(
                        rows4[0:2, :], pb[5][0:2, 0:512], AF.Ln,
                        bias=eps24[:, :]), waits=[ssm, t_eps24]))
                    iv = Sm(lambda s: s.activation(rows4[0:2, :], rows4[0:2, :],
                                                   AF.Exp, scale=-0.5),
                            waits=[iv])
                    ivB = Tm(lambda t: t.matmul(
                        pb[5][:, 0:512], sel2_sb[:, :], rows4[0:2, :],
                        start=True, stop=True),
                        waits=[iv, tl["sel2"]] + bacq([5]))
                    bwr([5], ivB)
                    ct, st_ = 2 * grp, 2 * grp + 1
                    tc0 = sb_u - TAB_LO
                    x1 = brd([bk_], Vm(lambda v, bk_=bk_, ct=ct, tc0=tc0:
                                       v.tensor_mul(
                        tmpA[:, :], pb[bk_][:, 0:512],
                        tabs[ct][:, tc0:tc0 + 512]),
                        waits=[mm, ttab[ct], c1, prev_final[0], prev_x3[0]]))
                    x2 = brd([4], Vm(lambda v, st_=st_, tc0=tc0: v.tensor_mul(
                        tmpB[:, :], pb[4][:, 0:512],
                        tabs[st_][:, tc0:tc0 + 512]),
                        waits=[rmm, ttab[st_], prev_x3[0]]))
                    x3 = Gm(lambda g: g.tensor_add(tmpA[:, :], tmpA[:, :],
                                                   tmpB[:, :]), waits=[x1, x2])
                    prev_x3[0] = x3
                    if grp == 0:
                        dst, wfree = qhat[ept], t_sB
                    else:
                        dst, wfree = khat[ps][ept], tok_kfree[ps]
                    tqv = brd([5], Vm(lambda v, dst=dst: v.tensor_mul(
                        dst[:, :], tmpA[:, :], pb[5][:, 0:512]),
                        waits=[x3, ivB, wfree]))
                    tq_l[e] = tqv
                    prev_final[0] = tqv
                    if grp == 0:
                        t_qhat[ept] = tqv
                    else:
                        tok_kr[ps][ept] = tqv
            tvv = None
            for st in range(4):
                bk_ = st % 2
                mm = None
                for dc in range(8):
                    w = [tw[("wqkv", dc)], mm]
                    if dc == 0:
                        w += bacq([bk_]) + [tq_l[7], tvv]
                    mm = Tm(lambda t, st=st, dc=dc, bk_=bk_, sb_u=sb_u: t.matmul(
                        pb[bk_][:, 0:512], h1s(dc, st * 128, 128, sb_u),
                        wqkv_sb[dc][:, 1024:1536],
                        start=(dc == 0), stop=(dc == 7)), waits=w)
                bwr([bk_], mm)
                tvv = brd([bk_], Vm(lambda v, st=st, bk_=bk_, ps=ps:
                                    v.tensor_copy(
                    vtm[ps][st][:, :], pb[bk_][:, 0:512]),
                    waits=[mm, tok_kfree[ps]]))
                tok_vr[ps][st] = tvv
            return tvv

        def stage_b2(qp, t_prev_wo):
            CUR_TAG[0] = f'a{l}-sb'
            qc2 = qp % 512
            tev = [None] * 8
            tav = [None] * 8
            toev = t_prev_wo

            def pieces_for(qt):
                out = []
                j = 0
                while j < 640:
                    kabs = qt - 512 + j
                    slot = (kabs // 512) % 2
                    kc = kabs % 512
                    w = min(512 - kc, 640 - j)
                    if j < 512 and j + w > 512:
                        w = 512 - j
                    if kabs < 3072 < kabs + w:
                        w = 3072 - kabs
                    out.append((0 if j < 512 else 1, j if j < 512 else j - 512,
                                w, slot, kc, kabs))
                    j += w
                return out

            for h in range(HEADS):
                ept, half = h // 2, h % 2
                r0 = 64 * half
                for ti in range(2):
                    qt = qp + 128 * ti
                    qc = qt % 512
                    w01 = bacq([0, 1])
                    mm = None
                    for (bk_, c0, w, slot, kc, kabs) in pieces_for(qt):
                        mm = Tm(lambda t, ept=ept, r0=r0, bk_=bk_, c0=c0, w=w,
                                slot=slot, kc=kc, qc=qc, kabs=kabs: t.matmul(
                            pb[bk_][:, c0:c0 + w],
                            qhat[ept][r0:r0 + 64, qc:qc + 128],
                            khat[slot][ept][r0:r0 + 64, kc:kc + w],
                            start=True, stop=(kabs + w > 3072)),
                            waits=[tok_kr[slot][ept], t_qhat[ept], mm] +
                                  (w01 if mm is None else []))
                        if kabs + w <= 3072:
                            mm = Tm(lambda t, bk_=bk_, c0=c0, w=w, kabs=kabs:
                                    t.matmul(
                                pb[bk_][:, c0:c0 + w], onescb[0:1, 0:128],
                                vnegrow[0:1, kabs:kabs + w],
                                start=False, stop=True),
                                waits=[t_vneg, mm, t_onesb])
                    bwr([0], mm)
                    bwr([1], mm)
                    mq1 = brd([0], Vm(lambda v: v.tensor_add(
                        pb[0][:, 0:128], pb[0][:, 0:128], m_lo_sb[:, :]),
                        waits=[mm, tw["m_lo"]]))
                    mq2 = brd([1], Vm(lambda v: v.tensor_add(
                        pb[1][:, 0:128], pb[1][:, 0:128], m_up_sb[:, :]),
                        waits=[mm, tw["m_up"], mq1]))
                    nc_ = 2 * h + ti
                    n1 = brd([0], Vm(lambda v, nc_=nc_: v.tensor_reduce(
                        nmax[:, nc_:nc_ + 1], pb[0][:, 0:512], AX.X, ALU.max,
                        negate=True), waits=[mq1, mq2]))
                    n2 = brd([1], Vm(lambda v: v.tensor_reduce(
                        tmpA[:, 0:1], pb[1][:, 0:128], AX.X, ALU.max,
                        negate=True), waits=[mq2, n1, prev_nm[0]]))
                    prev_nm[0] = Vm(lambda v, nc_=nc_: v.tensor_scalar(
                        nmax[:, nc_:nc_ + 1], nmax[:, nc_:nc_ + 1],
                        tmpA[:, 0:1], None, ALU.min), waits=[n2])
                    if qt < L2F_LO:
                        prev_nm[0] = Vm(lambda v, nc_=nc_, qt=qt:
                                        v.tensor_mul(
                            nmax[:, nc_:nc_ + 1], nmax[:, nc_:nc_ + 1],
                            vcol[:, qt // 128:qt // 128 + 1]),
                            waits=[prev_nm[0], tl["validc"]])
                ttrA = Tm(lambda t, h=h: t.transpose(
                    pb[1][0:1, 0:128], nmax[:, 2 * h:2 * h + 1],
                    identsb[:, :]),
                    waits=[prev_nm[0], tl["ident"]] + bacq([1]))
                ttrB = Tm(lambda t, h=h: t.transpose(
                    pb[1][0:1, 128:256], nmax[:, 2 * h + 1:2 * h + 2],
                    identsb[:, :]), waits=[prev_nm[0], ttrA])
                bwr([1], ttrB)
                trw = brd([1], Vm(lambda v: v.tensor_copy(
                    nrow4[0:1, 0:256], pb[1][0:1, 0:256]),
                    waits=[ttrA, ttrB, prev_bc[0]]))
                trw = Vm(lambda v: v.tensor_copy(
                    nrow4[0:1, 256:512], nrow4[0:1, 0:256]), waits=[trw])
                pbh = pbuf[h % 2]
                smm = [None] * 3
                wst = bacq([5, 6, 7])
                for b in range(6):
                    kb = qp - 512 + b * 128
                    slot = (kb // 512) % 2
                    kc = kb % 512
                    bs = 5 + b // 2
                    c0 = 256 * (b % 2)
                    smm[b // 2] = Tm(lambda t, ept=ept, r0=r0, slot=slot,
                                     kc=kc, bs=bs, c0=c0, qc2=qc2, b=b:
                                     t.matmul(
                        pb[bs][:, c0:c0 + 256],
                        khat[slot][ept][r0:r0 + 64, kc:kc + 128],
                        qhat[ept][r0:r0 + 64, qc2:qc2 + 256],
                        start=(b % 2 == 0), stop=False, skip_group_check=True),
                        waits=[tok_kr[slot][ept], t_qhat[ept], smm[b // 2]] +
                              (wst if b == 0 else []))
                tbcm = [None] * 3
                for g2 in range(3):
                    tbcm[g2] = Tm(lambda t, g2=g2: t.matmul(
                        pb[5 + g2][:, 0:512], onescb[0:1, 0:128],
                        nrow4[0:1, 0:512], start=False, stop=True,
                        skip_group_check=True),
                        waits=[trw, smm[g2], t_onesb,
                               tbcm[g2 - 1] if g2 else None])
                    bwr([5 + g2], tbcm[g2])
                prev_bc[0] = tbcm[2]
                mk = brd([5], Vm(lambda v: v.tensor_add(
                    pb[5][:, 0:128], pb[5][:, 0:128], m_up_sb[:, :]),
                    waits=[tbcm[0], tw["m_up"]]))
                mk = brd([5], Vm(lambda v: v.tensor_scalar(
                    pb[5][:, 128:256], pb[5][:, 128:256], -BIG, None,
                    ALU.add), waits=[mk]))
                mk = brd([5], Vm(lambda v: v.tensor_add(
                    pb[5][:, 384:512], pb[5][:, 384:512], m_up_sb[:, :]),
                    waits=[mk]))
                mk2 = brd([7], Vm(lambda v: v.tensor_add(
                    pb[7][:, 0:128], pb[7][:, 0:128], m_lo_sb[:, :]),
                    waits=[tbcm[2], tw["m_lo"], mk]))
                mk2 = brd([7], Vm(lambda v: v.tensor_scalar(
                    pb[7][:, 256:384], pb[7][:, 256:384], -BIG, None,
                    ALU.add), waits=[mk2]))
                mk2 = brd([7], Vm(lambda v: v.tensor_add(
                    pb[7][:, 384:512], pb[7][:, 384:512], m_lo_sb[:, :]),
                    waits=[mk2]))
                we = [tav[h - 2] if h >= 2 else None]
                bank_ready = [mk, mk, tbcm[1], tbcm[1], mk2, mk2]
                eb = [None] * 6
                ep = None
                for b in range(6):
                    kb = qp - 512 + b * 128
                    bs = 5 + b // 2
                    c0 = 256 * (b % 2)
                    eb[b] = brd([bs], Sm(lambda s, pbh=pbh, b=b, bs=bs, c0=c0,
                                         kb=kb: s.activation(
                        pbh[:, b * 256:(b + 1) * 256], pb[bs][:, c0:c0 + 256],
                        AF.Exp, bias=vnegc[:, kb // 128:kb // 128 + 1]),
                        waits=[bank_ready[b], t_vnegc, ep] + we))
                    ep = eb[b]
                tev[h] = eb[5]
                wd = bacq([4])
                tdn = None
                for b in range(6):
                    tdn = Tm(lambda t, b=b, pbh=pbh: t.matmul(
                        pb[4][0:1, 0:256], ones128b[:, :],
                        pbh[:, b * 256:(b + 1) * 256],
                        start=(b == 0), stop=(b == 5)),
                        waits=[eb[b], tdn, t_ones128b] +
                              (wd if b == 0 else []))
                bwr([4], tdn)
                rc = brd([4], Vm(lambda v: v.tensor_scalar(
                    denrow[0:1, 0:256], pb[4][0:1, 0:256], 1e-20, None,
                    ALU.add), waits=[tdn, prev_tbc[0]]))
                rc = Vm(lambda v: v.reciprocal(
                    denrow[0:1, 0:256], denrow[0:1, 0:256]), waits=[rc])
                if qp < L2F_LO:
                    rc = Vm(lambda v, qp=qp: v.tensor_mul(
                        denrow[0:1, 0:256], denrow[0:1, 0:256],
                        vrow[0:1, qp:qp + 256]), waits=[rc, tl["validr"]])
                avb = 2 + (h % 2)
                avc = 256 * ((h // 2) % 2)
                tp = (0, 64) if half else None
                wa = bacq([avb])
                amm = None
                for b in range(6):
                    kb = qp - 512 + b * 128
                    slot = (kb // 512) % 2
                    stt_ = (kb % 512) // 128
                    amm = Tm(lambda t, slot=slot, stt_=stt_, h=h, avb=avb,
                             avc=avc, tp=tp, pbh=pbh, b=b, r0=r0: t.matmul(
                        pb[avb][r0:r0 + 64, avc:avc + 256],
                        vtm[slot][stt_][:, h * 64:h * 64 + 64],
                        pbh[:, b * 256:(b + 1) * 256],
                        start=(b == 0), stop=(b == 5), tile_position=tp),
                        waits=[tok_vr[slot][stt_], eb[b], amm] +
                              (wa if b == 0 else []))
                tav[h] = amm
                bwr([avb], amm)
                tbc = Tm(lambda t: t.matmul(
                    pb[4][:, 256:512], onesc[0:1, :], denrow[0:1, 0:256],
                    start=True, stop=True), waits=[rc, tdn, amm])
                bwr([4], tbc)
                prev_tbc[0] = tbc
                toev = brd([avb], Vm(lambda v, ept=ept, r0=r0, avb=avb,
                                     avc=avc, qc2=qc2: v.tensor_copy(
                    obuf[ept][r0:r0 + 64, qc2:qc2 + 256],
                    pb[avb][r0:r0 + 64, avc:avc + 256]),
                    waits=[amm, toev]))
                toev = brd([4], Vm(lambda v, ept=ept, r0=r0, qc2=qc2:
                                   v.tensor_mul(
                    obuf[ept][r0:r0 + 64, qc2:qc2 + 256],
                    obuf[ept][r0:r0 + 64, qc2:qc2 + 256],
                    pb[4][r0:r0 + 64, 256:512]), waits=[tbc, toev]))
            return toev

        prev_nm = [None]
        prev_bc = [None]
        prev_tbc = [None]

        for sb_u in range(lo_keys, T, 512):
            ta = stage_a(sb_u)
            if sb_u < lo_q or SUBB == 0:
                t_sB = ta
                continue
            tos = None
            for k2 in range(2):
                tos = stage_b2(sb_u + k2 * 256,
                               tok_out[7] if k2 == 0 else None)
            t_sB = tos
            if SUBB < 3 or SUBB == 23:
                continue
            CUR_TAG[0] = f'a{l}-wo'
            for f in range(8):
                bk_ = 6 + f % 2
                mm = None
                for ic in range(4):
                    w = [tos, tw[("wo", ic)], mm]
                    if ic == 0:
                        w += bacq([bk_])
                    mm = Tm(lambda t, f=f, ic=ic, bk_=bk_: t.matmul(
                        pb[bk_][:, 0:512], wo_sb[ic][:, f * 128:(f + 1) * 128],
                        obuf[ic][:, 0:512], start=(ic == 0), stop=(ic == 3)),
                        waits=w)
                bwr([bk_], mm)
                oc = sb_u - out_base
                icc = sb_u - in_base
                tok_out[f] = brd([bk_], Vm(lambda v, f=f, oc=oc, icc=icc,
                                           bk_=bk_: v.tensor_add(
                    arena[f][:, oc:oc + 512], arena[f][:, icc:icc + 512],
                    pb[bk_][:, 0:512]), waits=[mm, tok_out[f]]))
            ps = (sb_u // 512) % 2
            tok_kfree[1 - ps] = tos
        return barrier()

    bar = tok_p1
    if NPHASES >= 2:
        bar = attn_phase(0, 1024, L1A_LO, AB, 1536, tok_p1)

    def ffn_phase(l, lo, in_base, out_base, prevb, final):
        tw1 = []
        for i in range(8):
            tw1.append(GD(lambda g, i=i, l=l: g.dma_start(
                w1_sb[i][:, :], ap("w1")[l, i * 128:(i + 1) * 128, :]),
                waits=prevb))
        tb = {}
        for n, dd in [("b1a", b1a_sb), ("b1g", b1g_sb), ("b2c", b2_sb),
                      ("lgc", lg_sb), ("lbc", lb_sb)]:
            tb[n] = Ym(lambda y, n=n, dd=dd, l=l: y.dma_start(
                dd[:, :], ap(n)[l]), waits=prevb)
        tz = Vm(lambda v: v.memset(gated[NHC - 1][:, :], 0.0), waits=prevb)

        blocks = []
        u = lo
        while u < T:
            w = min(512, T - u)
            blocks.append((u, w))
            u += w

        tok_y = [None] * 8
        tok_w2 = None
        tok_xa = None
        t_pass2_last = None
        def xs(dc, c0, w, u0):
            off = (u0 - in_base) - (1 if dc >= 4 else 0) + c0
            return arena[dc][:, off:off + w]

        for (u0, W) in blocks:
            CUR_TAG[0] = f'f{l}-st'
            w5 = bacq([5])
            w4f = bacq([4])
            tsum = None
            for f in range(8):
                tsum = Tm(lambda t, f=f, W=W, u0=u0: t.matmul(
                    pb[5][0:1, 0:W], ones128b[:, :], xs(f, 0, W, u0),
                    start=(f == 0), stop=(f == 7)),
                    waits=[t_ones128b, tsum, tok_y[7]] +
                          (w5 + w4f + (prevb if u0 == lo else [])
                           if f == 0 else []))
            bwr([5], tsum)
            tsq = None
            for f in range(8):
                s1 = Sm(lambda s, f=f, W=W, u0=u0: s.activation(
                    tmpCb[:, 0:W], xs(f, 0, W, u0), AF.Square),
                    waits=[tsq, tok_y[7]] +
                          (prevb if u0 == lo and f == 0 else []))
                tsq = Tm(lambda t, f=f, W=W: t.matmul(
                    pb[4][0:1, 0:W], ones128b[:, :], tmpCb[:, 0:W],
                    start=(f == 0), stop=(f == 7)),
                    waits=[s1, t_ones128b] + ([tsum] if f == 0 else []))
            bwr([4], tsq)
            r1 = brd([5], Vm(lambda v, W=W: v.tensor_scalar(
                rowaux[0:1, 0:W], pb[5][0:1, 0:W], 1.0 / D, None, ALU.mult),
                waits=[tsum, tsq]))
            r2 = brd([4], Vm(lambda v, W=W: v.tensor_scalar(
                rows4[0:1, 0:W], pb[4][0:1, 0:W], 1.0 / D, None, ALU.mult),
                waits=[tsq]))
            r3 = brd([4], Vm(lambda v, W=W: v.tensor_mul(
                pb[4][0:1, 0:W], rowaux[0:1, 0:W], rowaux[0:1, 0:W]),
                waits=[r1, r2]))
            r4 = brd([4], Vm(lambda v, W=W: v.tensor_sub(
                rows4[0:1, 0:W], rows4[0:1, 0:W], pb[4][0:1, 0:W]), waits=[r3]))
            r5 = Sm(lambda s, W=W: s.activation(rows4[0:1, 0:W], rows4[0:1, 0:W],
                                                AF.Ln, bias=eps5[0:1, :]),
                    waits=[r4, t_eps5])
            r5 = Sm(lambda s, W=W: s.activation(rows4[0:1, 0:W], rows4[0:1, 0:W],
                                                AF.Exp, scale=-0.5), waits=[r5])
            rb = Vm(lambda v, W=W: v.tensor_scalar(
                rowaux[0:1, 0:W], rowaux[0:1, 0:W], -1.0, None, ALU.mult),
                waits=[r5])
            rb = Vm(lambda v, W=W: v.tensor_mul(rowaux[0:1, 0:W],
                                                rowaux[0:1, 0:W],
                                                rows4[0:1, 0:W]), waits=[rb])
            tB1 = Tm(lambda t, W=W: t.matmul(
                pb[6][:, 0:W], onesc[0:1, :], rows4[0:1, 0:W],
                start=True, stop=True), waits=[r5, t_ones] + bacq([6]))
            bwr([6], tB1)
            tB2 = Tm(lambda t, W=W: t.matmul(
                pb[7][:, 0:W], onesc[0:1, :], rowaux[0:1, 0:W],
                start=True, stop=True), waits=[rb] + bacq([7]))
            bwr([7], tB2)
            tlnh = [None] * 8
            for f in range(8):
                a1 = brd([6], Vm(lambda v, f=f, W=W, u0=u0: v.tensor_mul(
                    tmpB[:, 0:W], xs(f, 0, W, u0), pb[6][:, 0:W]),
                    waits=[tB1, tlnh[f - 1] if f else None]))
                a2 = brd([7], Vm(lambda v, W=W: v.tensor_add(
                    tmpB[:, 0:W], tmpB[:, 0:W], pb[7][:, 0:W]),
                    waits=[a1, tB2]))
                tlnh[f] = Vm(lambda v, f=f, W=W: v.tensor_scalar(
                    lnh[f][:, 0:W], tmpB[:, 0:W], lg_sb[:, f:f + 1],
                    lb_sb[:, f:f + 1], ALU.mult, ALU.add),
                    waits=[a2, tb["lgc"], tb["lbc"], t_pass2_last])
            tvB = None
            if (not final) and u0 < L2F_LO:
                tvBm = Tm(lambda t, u0=u0, W=W: t.matmul(
                    pb[4][:, 0:W], onescb[0:1, 0:128], vrow[0:1, u0:u0 + W],
                    start=True, stop=True),
                    waits=[tlnh[7], tl["validr"], t_onesb] + bacq([4]))
                bwr([4], tvBm)
                tvB = brd([4], Vm(lambda v, W=W: v.tensor_copy(
                    validB[:, 0:W], pb[4][:, 0:W]), waits=[tvBm]))
            CUR_TAG[0] = f'f{l}-w1'
            tgd = [None] * NHC
            for c in range(NHC):
                aw = min(128, HID - c * 128)
                pa, pg = [(0, 1), (2, 3), (6, 7)][c % 3]
                mma = None
                wA = bacq([pa])
                for dc in range(8):
                    mma = Tm(lambda t, c=c, dc=dc, aw=aw, W=W, pa=pa: t.matmul(
                        pb[pa][0:aw, 0:W], w1_sb[dc][:, c * 128:c * 128 + aw],
                        lnh[dc][:, 0:W], start=(dc == 0), stop=(dc == 7)),
                        waits=[tlnh[dc], tw1[dc], mma] +
                              (wA if dc == 0 else []))
                bwr([pa], mma)
                mmg = None
                wG = bacq([pg])
                for dc in range(8):
                    mmg = Tm(lambda t, c=c, dc=dc, aw=aw, W=W, pg=pg: t.matmul(
                        pb[pg][0:aw, 0:W],
                        w1_sb[dc][:, HID + c * 128:HID + c * 128 + aw],
                        lnh[dc][:, 0:W], start=(dc == 0), stop=(dc == 7)),
                        waits=[tlnh[dc], mmg] + (wG if dc == 0 else []))
                bwr([pg], mmg)
                ge = brd([pg], Sm(lambda s, c=c, aw=aw, W=W, pg=pg: s.activation(
                    gated[c][0:aw, 0:W], pb[pg][0:aw, 0:W], AF.Gelu,
                    bias=b1g_sb[0:aw, c:c + 1]),
                    waits=[mmg, tb["b1g"], tz]))
                tgd[c] = brd([pa], Vm(lambda v, c=c, aw=aw, W=W, pa=pa:
                                      v.scalar_tensor_tensor(
                    gated[c][0:aw, 0:W], pb[pa][0:aw, 0:W],
                    b1a_sb[0:aw, c:c + 1], gated[c][0:aw, 0:W],
                    ALU.add, ALU.mult), waits=[ge, tb["b1a"]]))
            CUR_TAG[0] = f'f{l}-w2'
            mmo = [None] * 8
            wall = bacq(list(range(8)))
            for c in range(NHC):
                tok_w2A = GD(lambda g, c=c, l=l: g.dma_start(
                    w2r[:, 0:512], ap("w2p")[l, c * 128:(c + 1) * 128, 0:512]),
                    waits=[tok_w2, tgd[c], mmo[3]])
                tok_w2 = GD(lambda g, c=c, l=l: g.dma_start(
                    w2r[:, 512:1024],
                    ap("w2p")[l, c * 128:(c + 1) * 128, 512:1024]),
                    waits=[tok_w2A, mmo[7]])
                for f in range(8):
                    mmo[f] = Tm(lambda t, c=c, f=f, W=W: t.matmul(
                        pb[f][:, 0:W], w2r[:, f * 128:(f + 1) * 128],
                        gated[c][:, 0:W], start=(c == 0), stop=(c == NHC - 1)),
                        waits=[tgd[c], tok_w2A if f < 4 else tok_w2, mmo[f]] +
                              (wall if c == 0 and f == 0 else []))
            for f in range(8):
                bwr([f], mmo[f])
            CUR_TAG[0] = f'f{l}-ep'
            if not final:
                for f in range(8):
                    e1 = brd([f], Vm(lambda v, f=f, u0=u0, W=W:
                                     v.scalar_tensor_tensor(
                        tmpB[:, 0:W], pb[f][:, 0:W], b2_sb[:, f:f + 1],
                        arena[f][:, u0 - in_base:u0 - in_base + W],
                        ALU.add, ALU.add),
                        waits=[mmo[f], tb["b2c"], tok_y[f],
                               tok_y[f - 1] if f else tok_y[7]]))
                    if u0 < L2F_LO:
                        tok_y[f] = Vm(lambda v, f=f, u0=u0, W=W: v.tensor_mul(
                            arena[f][:, u0 - out_base:u0 - out_base + W],
                            tmpB[:, 0:W], validB[:, 0:W]), waits=[e1, tvB])
                    else:
                        tok_y[f] = Vm(lambda v, f=f, u0=u0, W=W: v.tensor_copy(
                            arena[f][:, u0 - out_base:u0 - out_base + W],
                            tmpB[:, 0:W]), waits=[e1])
                t_pass2_last = tok_y[7]
            else:
                tst = [None, None]
                for f in range(8):
                    buf = tmpB if f % 2 == 0 else tmpC
                    tok_xa = Ym(lambda y, f=f, u0=u0, W=W: y.dma_start(
                        xaring[:, 0:W],
                        xa_d.ap()[f, :, u0 - L2F_LO:u0 - L2F_LO + W]),
                        waits=[tok_xa, tok_y[f - 1] if f else tok_y[7]] +
                              (prevb if u0 == lo and f == 0 else []))
                    e1 = brd([f], Vm(lambda v, f=f, u0=u0, W=W, buf=buf:
                                     v.scalar_tensor_tensor(
                        buf[:, 0:W], pb[f][:, 0:W], b2_sb[:, f:f + 1],
                        arena[f][:, u0 - in_base:u0 - in_base + W],
                        ALU.add, ALU.add),
                        waits=[mmo[f], tb["b2c"], tst[f % 2]]))
                    ty2 = Vm(lambda v, f=f, W=W, buf=buf:
                             v.scalar_tensor_tensor(
                        buf[:, 0:W], buf[:, 0:W], gbc[:, f:f + 1],
                        xaring[:, 0:W], ALU.mult, ALU.add),
                        waits=[e1, t_gb, tok_xa])
                    tok_y[f] = ty2
                    tst[f % 2] = Ym(lambda y, f=f, u0=u0, W=W, buf=buf:
                                    y.dma_start(
                        out_d.ap()[f, :, u0 - L2F_LO:u0 - L2F_LO + W],
                        buf[:, 0:W]), waits=[ty2])
                    tok_y[f] = tst[f % 2]
                t_pass2_last = tok_y[7]
        return barrier()

    if NPHASES >= 3:
        bar = ffn_phase(0, L1F_LO, 1536, 1920, bar, False)
    if NPHASES >= 4:
        bar = attn_phase(1, 2048, L2A_LO, 1920, 2560, bar)
    if NPHASES >= 5:
        bar = ffn_phase(1, L2F_LO, 2560, 0, bar, True)
    final_waits = barrier()

    with (
        nc.semaphore("sem_t") as sem_t,
        nc.semaphore("sem_v") as sem_v,
        nc.semaphore("sem_s") as sem_s,
        nc.semaphore("sem_g") as sem_g,
        nc.semaphore("sem_gd") as sem_gd,
        nc.semaphore("sem_y") as sem_y,
        nc.Block() as block,
    ):
        sems = {"t": sem_t, "v": sem_v, "s": sem_s, "g": sem_g, "gd": sem_gd,
                "y": sem_y}

        def emit_stream(eng_obj, eng_key):
            last = {}
            for (ek, emit, waits, inc) in ops:
                if ek != eng_key:
                    continue
                for (sname, val, peng) in waits:
                    if last.get(sname, 0) >= val:
                        continue
                    eng_obj.wait_ge(sems[sname], val)
                    last[sname] = val
                ins = emit(eng_obj)
                if inc is not None:
                    ins.then_inc(sems[inc[0]], inc[2])
            if eng_key == "Y":
                for (sname, val, _) in final_waits:
                    if sname == "y" and last.get("y", 0) < val:
                        eng_obj.wait_ge(sems["y"], val)

        @block.tensor
        def _(t):
            emit_stream(t, "T")

        @block.vector
        def _(v):
            emit_stream(v, "V")

        @block.scalar
        def _(s):
            emit_stream(s, "S")

        @block.gpsimd
        def _(g):
            emit_stream(g, "G")

        @block.sync
        def _(y):
            emit_stream(y, "Y")

    return nc


_TABCACHE = {}


def _host_tables(lo):
    if lo in _TABCACHE:
        return _TABCACHE[lo]
    import jax
    import jax.numpy as jnp
    with jax.default_device(jax.devices("cpu")[0]):
        half = jnp.arange(0, 64, 2, dtype=jnp.float32)
        invf = 1.0 / 10000.0 ** (half / 64)
        t = jnp.arange(lo + TAB_LO, lo + TAB_LO + TABW, dtype=jnp.float32)
        fr = t[:, None] * invf[None, :]
        base = (half + 0.4 * 64) / (1.4 * 64)
        power = (t - 4096) / float(256)
        sc32 = base[None, :] ** power[:, None]
        cos32, sin32 = jnp.cos(fr), jnp.sin(fr)
    cos = np.tile(np.asarray(cos32, np.float64), (1, 2))
    sin = np.tile(np.asarray(sin32, np.float64), (1, 2))
    sc = np.tile(np.asarray(sc32, np.float64), (1, 2))
    prods = [(cos * sc).T, (sin * sc).T, (cos / sc).T, (sin / sc).T]
    tabs_ = np.zeros((8, 128, TABW), np.float32)
    for i, p in enumerate(prods):
        hi = p.astype(np.float32)
        lo_ = (p - hi.astype(np.float64)).astype(np.float32)
        tabs_[i] = np.tile(hi, (2, 1))
        tabs_[4 + i] = np.tile(lo_, (2, 1))
    _TABCACHE[lo] = tabs_
    return tabs_


def kernel(L0_attn_x_seq, L0_attn_Wq, L0_attn_bq, L0_attn_Wk, L0_attn_bk,
           L0_attn_Wv, L0_attn_bv, L0_attn_gamma, L0_ffn_wqkv, L0_ffn_q_scale,
           L0_ffn_k_scale, L0_ffn_wo, L0_ffn_ln_g, L0_ffn_ln_b, L0_ffn_w1,
           L0_ffn_b1, L0_ffn_w2, L0_ffn_b2, L0_c, L0_wmod):
    f32 = np.float32
    x = np.asarray(L0_attn_x_seq, f32)
    nc = build_nc()

    def colz(v):
        return np.ascontiguousarray(np.asarray(v, f32).reshape(8, 128).T)

    w2p = np.zeros((2, HIDP, D), f32)
    w2p[:, :HID] = np.asarray(L0_ffn_w2, f32)
    b1 = np.asarray(L0_ffn_b1, f32)
    b1a = np.zeros((2, 128, NHC), f32)
    b1g = np.zeros((2, 128, NHC), f32)
    for l in range(2):
        pa = np.zeros(HIDP, f32)
        pa[:HID] = b1[l, :HID]
        pg = np.zeros(HIDP, f32)
        pg[:HID] = b1[l, HID:]
        b1a[l] = pa.reshape(NHC, 128).T
        b1g[l] = pg.reshape(NHC, 128).T
    qs = np.asarray(L0_ffn_q_scale, f32)
    ks = np.asarray(L0_ffn_k_scale, f32)

    def tile2(v):
        return np.ascontiguousarray(np.tile(v, 2).reshape(128, 1))

    def perm(v):
        return np.concatenate([v[32:], v[:32]])

    rotm = np.zeros((128, 128), f32)
    for d in range(128):
        h0 = (d // 64) * 64
        dl = d % 64
        rotm[h0 + (dl + 32) % 64, d] = -1.0 if dl < 32 else 1.0
    jj, ii = np.meshgrid(np.arange(128), np.arange(128), indexing="ij")
    m_up = np.where(jj >= ii, 0.0, -BIG).astype(f32)
    m_lo = np.where(jj <= ii, 0.0, -BIG).astype(f32)
    tri01 = (jj <= ii).astype(f32)
    bd2 = np.zeros((128, 2), f32)
    bd2[:64, 0] = 1.0
    bd2[64:, 1] = 1.0
    sel2 = np.zeros((2, 128), f32)
    sel2[0, :64] = 1.0
    sel2[1, 64:] = 1.0

    shared = {
        "wq": np.ascontiguousarray(np.asarray(L0_attn_Wq, f32)),
        "wk": np.ascontiguousarray(np.asarray(L0_attn_Wk, f32)),
        "wv": np.ascontiguousarray(np.asarray(L0_attn_Wv, f32)),
        "bqc": colz(L0_attn_bq), "bkc": colz(L0_attn_bk),
        "bvr": np.asarray(L0_attn_bv, f32).reshape(1, D),
        "wqkv": np.ascontiguousarray(np.asarray(L0_ffn_wqkv, f32)),
        "wo": np.ascontiguousarray(np.asarray(L0_ffn_wo, f32)),
        "w1": np.ascontiguousarray(np.asarray(L0_ffn_w1, f32)),
        "w2p": w2p, "b1a": b1a, "b1g": b1g,
        "b2c": np.stack([colz(np.asarray(L0_ffn_b2, f32)[l]) for l in range(2)]),
        "lgc": np.stack([colz(np.asarray(L0_ffn_ln_g, f32)[l]) for l in range(2)]),
        "lbc": np.stack([colz(np.asarray(L0_ffn_ln_b, f32)[l]) for l in range(2)]),
        "qsc": np.stack([tile2(qs[l]) for l in range(2)]),
        "qspc": np.stack([tile2(perm(qs[l])) for l in range(2)]),
        "ksc": np.stack([tile2(ks[l]) for l in range(2)]),
        "kspc": np.stack([tile2(perm(ks[l])) for l in range(2)]),
        "wmod": np.ascontiguousarray(np.asarray(L0_wmod, f32)),
        "gam": np.asarray(L0_attn_gamma, f32).reshape(1, 1),
        "ident": np.eye(128, dtype=f32),
        "rotm": rotm, "m_up": m_up, "m_lo": m_lo, "tri01": tri01,
        "bd2": bd2, "sel2": sel2,
    }

    in_maps = []
    for c in range(8):
        b, qdx = c // 4, c % 4
        start = OWN * qdx
        lo = start - 3072
        xh = np.zeros((T, D), f32)
        s0 = max(0, lo)
        xh[s0 - lo:] = x[b, s0:start + OWN]
        upos = np.arange(lo, start + OWN)
        valid = (upos >= 0).astype(f32)
        m = dict(shared)
        m["xh"] = np.ascontiguousarray(xh.T.reshape(8, 128, T))
        m["crow"] = np.asarray(L0_c, f32)[b].reshape(1, D)
        m["tabs"] = _host_tables(lo)
        m["validr"] = np.ascontiguousarray(
            np.tile(valid[:VROW_W].reshape(1, VROW_W), (64, 1)))
        m["validc"] = np.ascontiguousarray(valid.reshape(T // 128, 128).T)
        in_maps.append(m)

    res = run_bass_kernel_spmd(nc, in_maps, core_ids=list(range(8)))
    out = np.zeros((2, 8192, D), f32)
    for c in range(8):
        b, qdx = c // 4, c % 4
        r = np.asarray(res.results[c]["out"]).reshape(D, OWN)
        out[b, OWN * qdx:OWN * (qdx + 1)] = r.T
    return out


# version 13
# speedup vs baseline: 1.1974x; 1.0083x over previous
import numpy as np
import concourse.bass as bass
import concourse.mybir as mybir
from concourse.bass_utils import run_bass_kernel_spmd

F32 = mybir.dt.float32
BF16 = mybir.dt.bfloat16
AF = mybir.ActivationFunctionType
ALU = mybir.AluOpType
AX = mybir.AxisListType

D = 1024
OWN = 2048
T = 5120
CH = 512
HEADS = 8
HID = 2730
NHC = 22
HIDP = NHC * 128
QK_SCALE = 8.0
EPS_RET = 1e-6
BIG = 1.0e30

RET_LO, L1A_LO, L1F_LO, L2A_LO, L2F_LO = 512, 1536, 1920, 2560, 3072
T_TAGS = []
CUR_TAG = ["init"]
TAB_LO, TABW = 1024, 4096
AB = 1016
ARENA_W = 4112
VROW_W = 3456
VNEG_W = 3200
NPHASES = 5
SUBB = 3


def build_nc():
    nc = bass.Bass("TRN2", target_bir_lowering=False)

    dr = {}
    for n, shp in [
        ("xh", [8, 128, T]), ("wq", [D, D]), ("wk", [D, D]), ("wv", [D, D]),
        ("bqc", [128, 8]), ("bkc", [128, 8]), ("bvr", [1, D]),
        ("wqkv", [2, D, 1536]), ("wo", [2, 512, D]),
        ("w1", [2, D, 2 * HID]), ("w2p", [2, HIDP, D]),
        ("b1a", [2, 128, NHC]), ("b1g", [2, 128, NHC]), ("b2c", [2, 128, 8]),
        ("lgc", [2, 128, 8]), ("lbc", [2, 128, 8]),
        ("qsc", [2, 128, 1]), ("qspc", [2, 128, 1]),
        ("ksc", [2, 128, 1]), ("kspc", [2, 128, 1]),
        ("crow", [1, D]), ("wmod", [D, 6 * D]), ("gam", [1, 1]),
        ("tabs", [8, 128, TABW]),
        ("validr", [64, VROW_W]), ("validc", [128, T // 128]),
        ("ident", [128, 128]), ("rotm", [128, 128]),
        ("m_up", [128, 128]), ("m_lo", [128, 128]), ("tri01", [128, 128]),
        ("bd2", [128, 2]), ("sel2", [2, 128]),
    ]:
        dr[n] = nc.dram_tensor(n, shp, F32, kind="ExternalInput")
    out_d = nc.dram_tensor("out", [8, 128, OWN], F32, kind="ExternalOutput")
    xa_d = nc.dram_tensor("xa_scr", [8, 128, OWN], F32, kind="Internal")

    def ap(n):
        return dr[n].ap()

    SB_BASE = 17408

    def at(name, shape, dtype, off):
        return nc.alloc_sbuf_tensor_at(name, shape, dtype, offset=SB_BASE + off)

    arena = [at(f"arena{f}", [128, ARENA_W], BF16, f * ARENA_W * 2)
             for f in range(8)]
    B0 = 8 * ARENA_W * 2
    wqT = [at(f"wqT{i}", [128, 1024], BF16, B0 + i * 2048) for i in range(8)]
    wkT = [at(f"wkT{i}", [128, 1024], BF16, B0 + 16384 + i * 2048) for i in range(8)]
    wvT = [at(f"wvT{i}", [128, 1024], BF16, B0 + 32768 + i * 2048) for i in range(8)]
    kring = [[at(f"krk{c}_{i}", [128, 512], BF16, B0 + 49152 + (c * 8 + i) * 1024)
              for i in range(8)] for c in range(2)]
    vring = [[at(f"vrk{c}_{i}", [128, 1024], BF16, B0 + 65536 + (c * 4 + i) * 2048)
              for i in range(4)] for c in range(2)]
    bvb = [at(f"bvb{i}", [128, 512], BF16, B0 + 81920 + i * 1024) for i in range(2)]
    gpow = at("gpow", [1, 512], F32, B0 + 83968)
    tabs = [at(f"tab{i}", [128, TABW], BF16, B0 + i * TABW * 2) for i in range(4)]
    wqkv_sb = [at(f"wqkv{i}", [128, 1536], BF16, B0 + 32768 + i * 3072)
               for i in range(8)]
    wo_sb = [at(f"wo{i}", [128, 1024], BF16, B0 + 57344 + i * 2048) for i in range(4)]
    khat = [[at(f"khat{c}_{i}", [128, 512], BF16, B0 + 65536 + (c * 4 + i) * 1024)
             for i in range(4)] for c in range(2)]
    vtm = [[at(f"vtm{c}_{i}", [128, 512], BF16, B0 + 73728 + (c * 4 + i) * 1024)
            for i in range(4)] for c in range(2)]
    w1_sb = [at(f"w1_{i}", [128, 5460], BF16, B0 + i * 10944) for i in range(8)]

    C0 = B0 + 87552
    _cc = [C0]

    def cal(name, shape, dtype):
        t = at(name, shape, dtype, _cc[0])
        per = int(np.prod(shape[1:])) * (2 if dtype == BF16 else 4)
        _cc[0] += (per + 31) // 32 * 32
        return t

    tmpA = cal("tmpA", [128, 512], F32)
    tmpB = cal("tmpB", [128, 512], F32)
    tmpC = cal("tmpC", [128, 512], F32)
    tmpCb = at("tmpCb", [128, 512], BF16, 8 * ARENA_W * 2 + 87552 + 4096)
    rows4 = cal("rows4", [64, 512], F32)
    rowaux = cal("rowaux", [64, 512], F32)
    denrow = cal("denrow", [64, 512], F32)
    vrow = cal("vrow", [64, VROW_W], BF16)
    identsb = cal("identsb", [128, 128], F32)
    onesc = cal("onesc", [64, 128], F32)
    ones128 = cal("ones128", [128, 1], F32)
    tri01_sb = cal("tri01", [128, 128], BF16)
    modc = cal("modc", [128, 48], F32)
    gac = cal("gac", [128, 8], F32)
    gbc = cal("gbc", [128, 8], F32)
    ca1c = cal("ca1c", [128, 8], F32)
    cb1c = cal("cb1c", [128, 8], F32)
    tgat = cal("tgat", [128, 8], F32)
    sgam = cal("sgam", [1, 1], F32)
    lng = cal("lng", [1, 1], F32)
    lngc = cal("lngc", [128, 1], F32)
    gnegc = cal("gnegc", [128, 4], F32)
    gprevc = cal("gprevc", [128, 4], F32)
    iotac = cal("iotac", [128, 4], F32)
    gmk8 = cal("gmk8", [128, 8], F32)
    gmk = [gmk8[:, i:i + 1] for i in range(8)]
    vcol = cal("vcol", [128, T // 128], F32)
    bq_sb = cal("bq", [128, 8], F32)
    bk_sb = cal("bk", [128, 8], F32)
    crow_sb = cal("crowsb", [128, 8], F32)
    b1a_sb = cal("b1a", [128, NHC], F32)
    b1g_sb = cal("b1g", [128, NHC], F32)
    b2_sb = cal("b2", [128, 8], F32)
    lg_sb = cal("lgcs", [128, 8], F32)
    lb_sb = cal("lbcs", [128, 8], F32)
    qs8_sb = cal("qs8", [128, 1], F32)
    qs8p_sb = cal("qs8p", [128, 1], F32)
    ks_sb = cal("ks", [128, 1], F32)
    ksp_sb = cal("ksp", [128, 1], F32)
    bd2_sb = cal("bd2", [128, 2], F32)
    sel2_sb = cal("sel2", [2, 128], F32)
    outp = cal("outp", [128, 128], F32)
    eps5 = cal("eps5", [2, 1], F32)
    eps24 = cal("eps24", [2, 1], F32)
    onescb = cal("onescb", [64, 256], BF16)
    ones128b = cal("ones128b", [128, 1], BF16)
    crow_bf = cal("crowbf", [128, 8], BF16)
    PB = 176832
    assert _cc[0] <= PB, _cc[0]
    xfm = [at(f"xfm{i}", [128, 512], F32, PB + i * 2048) for i in range(8)]
    hfm = [at(f"hfm{i}", [128, 512], BF16, PB + 16384 + i * 1024) for i in range(8)]
    Abuf = [at(f"Ab{i}", [128, 512], BF16, PB + 16384 + i * 1024) for i in range(8)]
    qfm = [at(f"qfm{i}", [128, 512], BF16, PB + 24576 + i * 1024) for i in range(8)]
    xstage = at("xstage", [128, 1024], F32, PB + 16384)
    wstage = at("wstage", [128, 1024], F32, PB)
    qhat = [at(f"qhat{i}", [128, 512], BF16, PB + i * 1024) for i in range(4)]
    obuf = [at(f"obuf{i}", [128, 512], BF16, PB + 4096 + i * 1024) for i in range(4)]
    qsbuf = at("qsbuf", [128, 512], BF16, PB + 8192)
    pbuf = [at(f"pbuf{i}", [128, 1536], BF16, PB + 9216 + i * 3072)
            for i in range(2)]
    vnegrow = at("vnegrow", [1, VNEG_W], BF16, PB + 15360)
    m_up_sb = at("m_up", [128, 128], BF16, PB + 21760)
    m_lo_sb = at("m_lo", [128, 128], BF16, PB + 22016)
    rotm_sb = at("rotm", [128, 128], BF16, PB + 22272)
    nrow4 = at("nrow4", [1, 512], BF16, PB + 30720)
    nmax = at("nmax", [128, 16], F32, PB + 31744)
    vnegc = at("vnegc", [128, 40], F32, PB + 31808)
    tablo = at("tablo", [128, TABW], BF16, PB + 22528)
    gated = [at(f"gat{i}", [128, 512], BF16, PB + i * 1024) for i in range(NHC)]
    lnh = [at(f"lnh{i}", [128, 512], BF16, PB + 22528 + i * 1024) for i in range(8)]
    w2r = at("w2r", [128, 1024], BF16, PB + 30720)
    gelbuf = at("gelbuf", [128, 512], BF16, PB + 32768)
    validB = at("validB", [128, 512], BF16, PB + 33792)
    xaring = tmpA
    assert 17408 + PB + 34816 <= 229344, PB

    pb = [nc.alloc_psum_tensor(f"pb{i}", [128, 512], F32) for i in range(8)]

    ops = []
    counts = {"t": 0, "v": 0, "s": 0, "g": 0, "gd": 0, "y": 0}

    def _add(eng, sem, amt, emit, waits):
        prev = counts[sem]
        counts[sem] += amt
        ws = [w for w in waits if w is not None]
        if sem in ("gd", "y") and prev > 0:
            ws.append((sem, prev, "~"))
        ops.append((eng, emit, ws, (sem, counts[sem], amt)))
        return (sem, counts[sem], eng)

    def Tm(emit, waits=()):
        T_TAGS.append(CUR_TAG[0])
        return _add("T", "t", 1, emit, waits)

    def Vm(emit, waits=()):
        return _add("V", "v", 1, emit, waits)

    def Sm(emit, waits=()):
        return _add("S", "s", 1, emit, waits)

    def Gm(emit, waits=()):
        return _add("G", "g", 1, emit, waits)

    def GD(emit, waits=()):
        return _add("G", "gd", 16, emit, waits)

    def Ym(emit, waits=()):
        return _add("Y", "y", 16, emit, waits)

    def barrier():
        return [(s, counts[s], "~") for s in counts if counts[s] > 0]

    bankR = [[] for _ in range(8)]
    bankW = [None] * 8

    def bacq(bs):
        w = []
        for b in bs:
            w += bankR[b]
            bankR[b] = []
        return w

    def bwr(bs, tok):
        for b in bs:
            bankW[b] = tok

    def brd(bs, tok):
        for b in bs:
            bankR[b].append(tok)
        return tok

    tl = {}
    for n, dst in [("validc", vcol), ("ident", identsb),
                   ("bd2", bd2_sb), ("sel2", sel2_sb),
                   ("gam", sgam), ("bqc", bq_sb), ("bkc", bk_sb)]:
        tl[n] = Ym(lambda y, d=dst, n=n: y.dma_start(d[:, :], ap(n)))
    for n, dst in [("validr", vrow), ("tri01", tri01_sb)]:
        tl[n] = GD(lambda g, d=dst, n=n: g.dma_start(d[:, :], ap(n)))
    tl["crow"] = Ym(lambda y: y.dma_start(
        crow_sb[:, :], ap("crow").rearrange("o (j p) -> (o p) j", p=128),
        allow_slow_non_contiguous=True))
    t_ones = Vm(lambda v: v.memset(onesc[:, :], 1.0))
    t_ones128 = Vm(lambda v: v.memset(ones128[:, :], 1.0))
    t_onesb = Vm(lambda v: v.memset(onescb[:, :], 1.0))
    t_eps5 = Vm(lambda v: v.memset(eps5[:, :], 1e-5))
    t_eps24 = Vm(lambda v: v.memset(eps24[:, :], 1e-24))
    t_ones128b = Vm(lambda v: v.memset(ones128b[:, :], 1.0))
    t_iota = Gm(lambda g: g.iota(iotac[:, :], [[128, 4]], channel_multiplier=1,
                                 allow_small_or_imprecise_dtypes=True))
    t_iotar = Gm(lambda g: g.iota(gpow[:, :], [[1, 512]], channel_multiplier=0,
                                  allow_small_or_imprecise_dtypes=True))

    t_lng = Sm(lambda s: s.activation(lng[:, :], sgam[:, :], AF.Ln),
               waits=[tl["gam"]])
    tk_ = Tm(lambda t: t.matmul(pb[5][:, 0:1], onesc[0:1, :], lng[:, :],
                                start=True, stop=True),
             waits=[t_lng, t_ones] + bacq([5]))
    bwr([5], tk_)
    t_lngc = brd([5], Vm(lambda v: v.tensor_copy(lngc[:, :], pb[5][:, 0:1]),
                         waits=[tk_]))
    t1 = Vm(lambda v: v.tensor_scalar(gpow[:, :], gpow[:, :], lng[0:1, 0:1], None,
                                      ALU.mult), waits=[t_iotar, t_lng])
    t_gpow = Sm(lambda s: s.activation(gpow[:, :], gpow[:, :], AF.Exp), waits=[t1])
    t1 = Vm(lambda v: v.tensor_scalar(gnegc[:, :], iotac[:, :], lngc[:, :], -1.0,
                                      ALU.mult, ALU.mult), waits=[t_iota, t_lngc])
    t_gneg = Sm(lambda s: s.activation(gnegc[:, :], gnegc[:, :], AF.Exp), waits=[t1])
    t1 = Vm(lambda v: v.tensor_scalar(tmpA[:, 0:1], lngc[:, :], 512.0, None,
                                      ALU.mult), waits=[t_lngc])
    t2 = Vm(lambda v: v.tensor_scalar(gprevc[:, :], iotac[:, :], lngc[:, :], -1.0,
                                      ALU.mult, ALU.mult), waits=[t_iota, t_lngc])
    t3 = Vm(lambda v: v.tensor_scalar(gprevc[:, :], gprevc[:, :], tmpA[:, 0:1],
                                      None, ALU.add), waits=[t1, t2])
    t_gprev = Sm(lambda s: s.activation(gprevc[:, :], gprevc[:, :], AF.Exp),
                 waits=[t3])
    t_b1 = Ym(lambda y: y.dma_start(tmpA[0:1, 0:512], ap("bvr")[:, 0:512]),
              waits=[t1, t3])
    t_b2 = Ym(lambda y: y.dma_start(tmpB[0:1, 0:512], ap("bvr")[:, 512:1024]))
    tm1 = Tm(lambda t: t.matmul(pb[5][:, 0:512], onesc[0:1, :], tmpA[0:1, 0:512],
                                start=True, stop=True),
             waits=[t_b1, t_ones] + bacq([5]))
    bwr([5], tm1)
    t_bvb0 = brd([5], Vm(lambda v: v.tensor_copy(bvb[0][:, :], pb[5][:, 0:512]),
                         waits=[tm1]))
    tm2 = Tm(lambda t: t.matmul(pb[5][:, 0:512], onesc[0:1, :], tmpB[0:1, 0:512],
                                start=True, stop=True),
             waits=[t_b2] + bacq([5]))
    bwr([5], tm2)
    t_bvb1 = brd([5], Vm(lambda v: v.tensor_copy(bvb[1][:, :], pb[5][:, 0:512]),
                         waits=[tm2]))

    t_silu = Sm(lambda s: s.activation(crow_bf[:, :], crow_sb[:, :], AF.Silu),
                waits=[tl["crow"]])
    prev_ld = None
    mm_last = None
    first_mod = True
    for cg in range(12):
        lo = cg * 512
        lds = []
        for kc in range(8):
            lds.append(GD(lambda g, kc=kc, lo=lo: g.dma_start(
                kring[0][kc][:, :], ap("wmod")[kc * 128:(kc + 1) * 128, lo:lo + 512]),
                waits=[prev_ld]))
        for j in range(4):
            vf = cg * 4 + j
            for kc in range(8):
                w = [lds[kc], t_silu, mm_last]
                if first_mod:
                    w += bacq([4])
                    first_mod = False
                mm_last = Tm(lambda t, kc=kc, j=j, vf=vf: t.matmul(
                    pb[4][:, vf:vf + 1], kring[0][kc][:, j * 128:(j + 1) * 128],
                    crow_bf[:, kc:kc + 1], start=(kc == 0), stop=(kc == 7)),
                    waits=w)
        prev_ld = mm_last
    bwr([4], mm_last)
    t_modc = brd([4], Vm(lambda v: v.tensor_copy(modc[:, :], pb[4][:, 0:48]),
                         waits=[mm_last]))
    t_ca1 = Vm(lambda v: v.tensor_scalar(ca1c[:, :], modc[:, 8:16], 1.0, None,
                                         ALU.add), waits=[t_modc])
    t_cb1 = Vm(lambda v: v.tensor_scalar(cb1c[:, :], modc[:, 32:40], 1.0, None,
                                         ALU.add), waits=[t_modc])

    def gtrans(dst, c0, w0):
        a1 = Sm(lambda s: s.activation(tgat[:, :], modc[:, c0:c0 + 8], AF.Abs),
                waits=[t_modc, w0])
        a2 = Vm(lambda v: v.tensor_scalar(tgat[:, :], tgat[:, :], 0.25, None,
                                          ALU.max), waits=[a1])
        a3 = Sm(lambda s: s.activation(dst[:, :], modc[:, c0:c0 + 8], AF.Sign),
                waits=[a2])
        return Vm(lambda v: v.tensor_mul(dst[:, :], dst[:, :], tgat[:, :]),
                  waits=[a3])
    t_ga = gtrans(gac, 16, None)
    t_gb = gtrans(gbc, 40, t_ga)

    tok_wT = t_modc
    for wname, wdst in [("wq", wqT), ("wk", wkT), ("wv", wvT)]:
        for c in range(8):
            tok_wT = GD(lambda g, wn=wname, c=c, wd=wdst: g.dma_start(
                wd[c][:, :], ap(wn)[c * 128:(c + 1) * 128, :]),
                waits=[tok_wT])

    def ln_stats2(x_src, x_waits, ones_lhs, ones_tok, extra_first=()):
        w5 = bacq([5])
        w4 = bacq([4])
        tsum = None
        for f in range(8):
            w = [x_waits[f], ones_tok, tsum] + \
                (list(extra_first) + w5 + w4 if f == 0 else [])
            tsum = Tm(lambda t, f=f: t.matmul(
                pb[5][0:1, 0:512], ones_lhs[:, :], x_src[f],
                start=(f == 0), stop=(f == 7)), waits=w)
        bwr([5], tsum)
        tsq = None
        for f in range(8):
            s1 = Sm(lambda s, f=f: s.activation(tmpCb[:, :], x_src[f],
                                                AF.Square),
                    waits=[x_waits[f], tsq])
            tsq = Tm(lambda t, f=f: t.matmul(
                pb[4][0:1, 0:512], ones128b[:, :], tmpCb[:, :],
                start=(f == 0), stop=(f == 7)),
                waits=[s1, t_ones128b] + ([tsum] if f == 0 else []))
        bwr([4], tsq)
        r1 = brd([5], Vm(lambda v: v.tensor_scalar(
            rowaux[0:1, :], pb[5][0:1, 0:512], 1.0 / D, None, ALU.mult),
            waits=[tsum, tsq]))
        r2 = brd([4], Vm(lambda v: v.tensor_scalar(
            rows4[0:1, :], pb[4][0:1, 0:512], 1.0 / D, None, ALU.mult),
            waits=[tsq]))
        r3 = brd([4], Vm(lambda v: v.tensor_mul(
            pb[4][0:1, 0:512], rowaux[0:1, :], rowaux[0:1, :]),
            waits=[r1, r2]))
        r4 = brd([4], Vm(lambda v: v.tensor_sub(
            rows4[0:1, :], rows4[0:1, :], pb[4][0:1, 0:512]), waits=[r3]))
        r5 = Sm(lambda s: s.activation(rows4[0:1, :], rows4[0:1, :], AF.Ln,
                                       bias=eps5[0:1, :]), waits=[r4, t_eps5])
        return Sm(lambda s: s.activation(rows4[0:1, :], rows4[0:1, :], AF.Exp,
                                         scale=-0.5), waits=[r5])

    def ln_broadcasts(t_inv, u0, do_valid):
        tiv = t_inv
        if do_valid:
            tiv = Vm(lambda v, u0=u0: v.tensor_mul(
                rows4[0:1, :], rows4[0:1, :], vrow[0:1, u0:u0 + 512]),
                waits=[t_inv, tl["validr"]])
        tb_ = Vm(lambda v: v.tensor_scalar(rowaux[0:1, :], rowaux[0:1, :], -1.0,
                                           None, ALU.mult), waits=[tiv])
        tb_ = Vm(lambda v: v.tensor_mul(rowaux[0:1, :], rowaux[0:1, :],
                                        rows4[0:1, :]), waits=[tb_])
        tB1 = Tm(lambda t: t.matmul(pb[6][:, 0:512], onesc[0:1, :],
                                    rows4[0:1, :], start=True, stop=True),
                 waits=[tiv, t_ones] + bacq([6]))
        bwr([6], tB1)
        tB2 = Tm(lambda t: t.matmul(pb[7][:, 0:512], onesc[0:1, :],
                                    rowaux[0:1, :], start=True, stop=True),
                 waits=[tb_] + bacq([7]))
        bwr([7], tB2)
        tVB = None
        if do_valid:
            tVB = Tm(lambda t, u0=u0: t.matmul(
                pb[4][:, 0:512], onescb[0:1, 0:128], vrow[0:1, u0:u0 + 512],
                start=True, stop=True), waits=[tiv, t_onesb] + bacq([4]))
            bwr([4], tVB)
        return tB1, tB2, tVB

    tok_y0 = [None] * 8
    tok_x_free = None
    tok_kv_free = [None, None]
    tok_k_ready = [[None] * 8, [None] * 8]
    tok_v_ready = [[None] * 4, [None] * 4]
    tok_xa_st = None
    t_abuf_free = None

    for ci in range(10 if NPHASES >= 1 else 0):
        CUR_TAG[0] = 'p1'
        u0 = ci * CH
        par = ci % 2
        pp = 1 - par
        txf = [None] * 8
        for f in range(8):
            txf[f] = Ym(lambda y, u0=u0, f=f: y.dma_start(
                xfm[f][:, 0:512], ap("xh")[f, :, u0:u0 + 512]),
                waits=[tok_x_free, txf[f - 1] if f else None,
                       tok_wT if ci == 0 and f == 0 else None])
        xtoks = list(txf)

        xsrc = [xfm[f][:, 0:512] for f in range(8)]
        t_inv = ln_stats2(xsrc, xtoks, ones128, t_ones128)
        tB1, tB2, _ = ln_broadcasts(t_inv, u0, False)
        th = [None] * 8
        for f in range(8):
            a1 = brd([6], Vm(lambda v, f=f: v.tensor_mul(
                tmpB[:, :], xfm[f][:, :], pb[6][:, 0:512]),
                waits=[tB1, xtoks[f], th[f - 1] if f else None]))
            a2 = brd([7], Vm(lambda v: v.tensor_add(
                tmpB[:, :], tmpB[:, :], pb[7][:, 0:512]), waits=[a1, tB2]))
            th[f] = Vm(lambda v, f=f: v.tensor_scalar(
                hfm[f][:, :], tmpB[:, :], ca1c[:, f:f + 1], modc[:, f:f + 1],
                ALU.mult, ALU.add), waits=[a2, t_ca1, t_modc])

        tq = [None] * 8
        tkk = [None] * 8
        for e in range(8):
            bk_ = e % 2
            mm = None
            for dc in range(8):
                w = [th[dc], mm]
                if dc == 0:
                    w += [tok_wT] + bacq([bk_])
                mm = Tm(lambda t, e=e, dc=dc, bk_=bk_: t.matmul(
                    pb[bk_][:, 0:512], wqT[dc][:, e * 128:(e + 1) * 128],
                    hfm[dc][:, :], start=(dc == 0), stop=(dc == 7)), waits=w)
            bwr([bk_], mm)
            tq[e] = brd([bk_], Sm(lambda s, e=e, bk_=bk_: s.activation(
                qfm[e][:, :], pb[bk_][:, 0:512], AF.Relu,
                bias=bq_sb[:, e:e + 1]), waits=[mm, tl["bqc"]]))
        for e in range(8):
            bk_ = e % 2
            mm = None
            for dc in range(8):
                w = [th[dc], mm]
                if dc == 0:
                    w += bacq([bk_])
                mm = Tm(lambda t, e=e, dc=dc, bk_=bk_: t.matmul(
                    pb[bk_][:, 0:512], wkT[dc][:, e * 128:(e + 1) * 128],
                    hfm[dc][:, :], start=(dc == 0), stop=(dc == 7)), waits=w)
            bwr([bk_], mm)
            tkk[e] = brd([bk_], Sm(lambda s, e=e, bk_=bk_, par=par: s.activation(
                kring[par][e][:, :], pb[bk_][:, 0:512], AF.Relu,
                bias=bk_sb[:, e:e + 1]),
                waits=[mm, tl["bkc"], tok_kv_free[par]]))
            tok_k_ready[par][e] = tkk[e]
        tv = [None] * 4
        for st in range(4):
            for half in range(2):
                bk_ = half
                mm = None
                for dc in range(8):
                    w = [th[dc], mm]
                    if dc == 0:
                        w += bacq([bk_])
                    mm = Tm(lambda t, st=st, half=half, dc=dc, bk_=bk_: t.matmul(
                        pb[bk_][:, 0:512], hfm[dc][:, st * 128:(st + 1) * 128],
                        wvT[dc][:, half * 512:(half + 1) * 512],
                        start=(dc == 0), stop=(dc == 7)), waits=w)
                bwr([bk_], mm)
                tv[st] = brd([bk_], Vm(lambda v, st=st, half=half, bk_=bk_,
                                       par=par: v.tensor_add(
                    vring[par][st][:, half * 512:(half + 1) * 512],
                    pb[bk_][:, 0:512], bvb[half][:, :]),
                    waits=[mm, tok_kv_free[par], t_bvb0, t_bvb1, tv[st]]))
            tok_v_ready[par][st] = tv[st]

        if ci == 0:
            tok_x_free = th[7]
            t_abuf_free = tv[3]
            continue

        tgm = [None] * 8
        for jp in range(8):
            jt = jp % 4
            uj = (u0 - CH if jp < 4 else u0) + jt * 128
            gcol = gprevc if jp < 4 else gnegc
            tgm[jp] = Vm(lambda v, jp=jp, jt=jt, uj=uj, gcol=gcol: v.tensor_mul(
                gmk[jp], gcol[:, jt:jt + 1],
                vcol[:, uj // 128:uj // 128 + 1]),
                waits=[t_gneg, t_gprev, tl["validc"]])
        tA = [None] * 8
        for jp in range(8):
            sp = pp if jp < 4 else par
            jt = jp % 4
            bk_ = 2 + jp % 2
            mm = None
            for ec in range(8):
                w = [tq[ec], tok_k_ready[sp][ec], mm]
                if ec == 0:
                    w += bacq([bk_])
                mm = Tm(lambda t, jt=jt, ec=ec, bk_=bk_, sp=sp: t.matmul(
                    pb[bk_][:, 0:512], kring[sp][ec][:, jt * 128:(jt + 1) * 128],
                    qfm[ec][:, :], start=(ec == 0), stop=(ec == 7)), waits=w)
            bwr([bk_], mm)
            ilo = 0 if jp < 4 else jt * 128
            ev = brd([bk_], Vm(lambda v, jp=jp, bk_=bk_: v.tensor_scalar(
                Abuf[jp][:, 0:512], pb[bk_][:, 0:512], gmk[jp], None,
                ALU.mult), waits=[mm, tgm[jp]]))
            if jp >= 4:
                ev = Vm(lambda v, jp=jp, jt=jt: v.tensor_mul(
                    Abuf[jp][:, jt * 128:(jt + 1) * 128],
                    Abuf[jp][:, jt * 128:(jt + 1) * 128], tri01_sb[:, :]),
                    waits=[ev, tl["tri01"]])
            tA[jp] = ev
        tden = None
        w0 = bacq([4])
        for jp in range(8):
            ilo = 0 if jp < 4 else (jp % 4) * 128
            tden = Tm(lambda t, jp=jp, ilo=ilo: t.matmul(
                pb[4][0:1, ilo:512], ones128b[:, :], Abuf[jp][:, ilo:512],
                start=(jp == 0), stop=(jp == 7)),
                waits=[tA[jp], tden, t_ones128b] + (w0 if jp == 0 else []))
        bwr([4], tden)
        rr = brd([4], Vm(lambda v: v.tensor_mul(
            denrow[0:1, :], pb[4][0:1, 0:512], gpow[:, :]),
            waits=[tden, t_gpow]))
        rr = Vm(lambda v: v.tensor_scalar(denrow[0:1, :], denrow[0:1, :],
                                          EPS_RET, None, ALU.add), waits=[rr])
        rr = Vm(lambda v: v.reciprocal(denrow[0:1, :], denrow[0:1, :]), waits=[rr])
        rr = Vm(lambda v: v.tensor_mul(denrow[0:1, :], denrow[0:1, :],
                                       gpow[:, :]), waits=[rr])
        rbt = Tm(lambda t: t.matmul(pb[4][:, 0:512], onesc[0:1, :], denrow[0:1, :],
                                    start=True, stop=True),
                 waits=[rr] + bacq([4]))
        bwr([4], rbt)
        tAr = [None] * 8
        for jp in range(8):
            ilo = 0 if jp < 4 else (jp % 4) * 128
            tAr[jp] = brd([4], Vm(lambda v, jp=jp, ilo=ilo: v.tensor_mul(
                Abuf[jp][:, ilo:512], Abuf[jp][:, ilo:512], pb[4][:, ilo:512]),
                waits=[rbt, tA[jp]]))
        t_xa = [None] * 8
        for dv in range(8):
            bk_ = 2 + dv % 2
            mm = None
            for jp in range(8):
                sp = pp if jp < 4 else par
                jt = jp % 4
                ilo = 0 if jp < 4 else jt * 128
                w = [tAr[jp], tok_v_ready[sp][jt], mm]
                if jp == 0:
                    w += bacq([bk_])
                mm = Tm(lambda t, dv=dv, jt=jt, ilo=ilo, bk_=bk_, sp=sp, jp=jp:
                        t.matmul(
                    pb[bk_][:, ilo:512],
                    vring[sp][jt][:, dv * 128:(dv + 1) * 128],
                    Abuf[jp][:, ilo:512], start=(jp == 0), stop=(jp == 7)),
                    waits=w)
            bwr([bk_], mm)
            t_xa[dv] = brd([bk_], Vm(lambda v, dv=dv, bk_=bk_:
                                     v.scalar_tensor_tensor(
                xfm[dv][:, :], pb[bk_][:, 0:512], gac[:, dv:dv + 1],
                xfm[dv][:, :], ALU.mult, ALU.add),
                waits=[mm, t_ga, xtoks[dv]]))
        tok_kv_free[pp] = t_xa[7]
        t_abuf_free = t_xa[7]

        if u0 >= L2F_LO:
            for f in range(8):
                tok_xa_st = Ym(lambda y, f=f, u0=u0: y.dma_start(
                    xa_d.ap()[f, :, u0 - L2F_LO:u0 - L2F_LO + 512],
                    xfm[f][:, 0:512]), waits=[t_xa[f], tok_xa_st])

        t_inv2 = ln_stats2(xsrc, t_xa, ones128, t_ones128)
        do_v = u0 < L2F_LO
        tB1b, tB2b, tVBb = ln_broadcasts(t_inv2, u0, do_v)
        lo_w = 504 if ci == 1 else 0
        for f in range(8):
            a1 = brd([6], Vm(lambda v, f=f, lo_w=lo_w: v.tensor_mul(
                tmpB[:, lo_w:512], xfm[f][:, lo_w:512], pb[6][:, lo_w:512]),
                waits=[tB1b, t_xa[f], tok_y0[f - 1] if f else None]))
            a2 = brd([7], Vm(lambda v, lo_w=lo_w: v.tensor_add(
                tmpB[:, lo_w:512], tmpB[:, lo_w:512], pb[7][:, lo_w:512]),
                waits=[a1, tB2b]))
            a3 = Vm(lambda v, f=f, lo_w=lo_w: v.tensor_scalar(
                tmpB[:, lo_w:512], tmpB[:, lo_w:512], cb1c[:, f:f + 1], None,
                ALU.mult), waits=[a2, t_cb1])
            dst_c = u0 - AB + lo_w
            if do_v:
                tok_y0[f] = brd([4], Vm(lambda v, f=f, lo_w=lo_w, dst_c=dst_c:
                                        v.scalar_tensor_tensor(
                    arena[f][:, dst_c:dst_c + 512 - lo_w], pb[4][:, lo_w:512],
                    modc[:, 24 + f:24 + f + 1], tmpB[:, lo_w:512],
                    ALU.mult, ALU.add), waits=[a3, tVBb, t_modc]))
            else:
                tok_y0[f] = Vm(lambda v, f=f, dst_c=dst_c: v.tensor_scalar(
                    arena[f][:, dst_c:dst_c + 512], tmpB[:, 0:512],
                    modc[:, 24 + f:24 + f + 1], None, ALU.add),
                    waits=[a3, t_modc])
        tok_x_free = tok_y0[7]

    tok_p1 = barrier()

    def attn_phase(l, lo_keys, lo_q, in_base, out_base, prevb):
        CUR_TAG[0] = f'a{l}-pre'
        tw = {}
        for i in range(4):
            tw[("tab", i)] = GD(lambda g, i=i: g.dma_start(
                tabs[i][:, :], ap("tabs")[i, :, :]), waits=prevb)
        tadd = [None] * 4
        for i in range(4):
            tlo = GD(lambda g, i=i: g.dma_start(
                tablo[:, :], ap("tabs")[4 + i, :, :]),
                waits=prevb + [tadd[i - 1] if i else None])
            tadd[i] = Vm(lambda v, i=i: v.tensor_add(
                tabs[i][:, :], tabs[i][:, :], tablo[:, :]),
                waits=[tw[("tab", i)], tlo])
        for i in range(8):
            tw[("wqkv", i)] = GD(lambda g, i=i, l=l: g.dma_start(
                wqkv_sb[i][:, :], ap("wqkv")[l, i * 128:(i + 1) * 128, :]),
                waits=prevb)
        for i in range(4):
            tw[("wo", i)] = GD(lambda g, i=i, l=l: g.dma_start(
                wo_sb[i][:, :], ap("wo")[l, i * 128:(i + 1) * 128, :]),
                waits=prevb)
        for nm, dd in [("rotm", rotm_sb), ("m_up", m_up_sb), ("m_lo", m_lo_sb)]:
            tw[nm] = GD(lambda g, nm=nm, dd=dd: g.dma_start(dd[:, :], ap(nm)),
                        waits=prevb)
        for nm, dd in [("qsc", qs8_sb), ("qspc", qs8p_sb), ("ksc", ks_sb),
                       ("kspc", ksp_sb)]:
            tw[nm] = Ym(lambda y, nm=nm, dd=dd, l=l: y.dma_start(
                dd[:, :], ap(nm)[l]), waits=prevb)
        t_vneg = Vm(lambda v: v.tensor_scalar(
            vnegrow[:, :], vrow[0:1, 0:VNEG_W], -1.0, BIG, ALU.add, ALU.mult),
            waits=[tl["validr"]] + prevb)
        t_vnegc = Vm(lambda v: v.tensor_scalar(
            vnegc[:, :], vcol[:, :], -1.0, BIG, ALU.add, ALU.mult),
            waits=[tl["validc"]] + prevb)
        t8a = Vm(lambda v: v.tensor_scalar(qs8_sb[:, :], qs8_sb[:, :], QK_SCALE,
                                           None, ALU.mult), waits=[tw["qsc"]])
        t8b = Vm(lambda v: v.tensor_scalar(qs8p_sb[:, :], qs8p_sb[:, :],
                                           QK_SCALE, None, ALU.mult),
                 waits=[tw["qspc"]])
        ttab = [None] * 4
        for i, ctok in enumerate([t8a, t8b, tw["ksc"], tw["kspc"]]):
            col = [qs8_sb, qs8p_sb, ks_sb, ksp_sb][i]
            ttab[i] = Vm(lambda v, i=i, col=col: v.tensor_scalar(
                tabs[i][:, :], tabs[i][:, :], col[:, :], None, ALU.mult),
                waits=[tadd[i], ctok])

        tok_out = [None] * 8
        tok_kr = [[None] * 4, [None] * 4]
        tok_vr = [[None] * 4, [None] * 4]
        tok_kfree = [None, None]
        t_qhat = [None] * 4
        t_sB = None

        def h1s(dc, c0, w, u):
            off = (u - in_base) - (1 if dc >= 4 else 0) + c0
            return arena[dc][:, off:off + w]

        def stage_a(sb_u):
            CUR_TAG[0] = f'a{l}-sa'
            ps = (sb_u // 512) % 2
            tq_l = {}
            prev_x3 = [None]
            prev_sq = [None]
            prev_final = [None]
            for grp in range(2):
                for ept in range(4):
                    e = grp * 4 + ept
                    bk_ = e % 2
                    mm = None
                    for dc in range(8):
                        w = [tw[("wqkv", dc)], mm]
                        if dc == 0:
                            w += bacq([bk_]) + [t_sB]
                        mm = Tm(lambda t, e=e, dc=dc, bk_=bk_, sb_u=sb_u:
                                t.matmul(
                            pb[bk_][:, 0:512],
                            wqkv_sb[dc][:, e * 128:(e + 1) * 128],
                            h1s(dc, 0, 512, sb_u),
                            start=(dc == 0), stop=(dc == 7)), waits=w)
                    bwr([bk_], mm)
                    c1 = brd([bk_], Sm(lambda s, bk_=bk_: s.activation(
                        qsbuf[:, :], pb[bk_][:, 0:512], AF.Copy),
                        waits=[mm, prev_sq[0]]))
                    rmm = Tm(lambda t: t.matmul(
                        pb[4][:, 0:512], rotm_sb[:, :], qsbuf[:, :],
                        start=True, stop=True),
                        waits=[c1, tw["rotm"]] + bacq([4]))
                    bwr([4], rmm)
                    sq = Gm(lambda g: g.tensor_mul(tmpC[:, :], qsbuf[:, :],
                                                   qsbuf[:, :]), waits=[c1])
                    prev_sq[0] = sq
                    ssm = Tm(lambda t: t.matmul(
                        pb[5][0:2, 0:512], bd2_sb[:, :], tmpC[:, :],
                        start=True, stop=True),
                        waits=[sq, tl["bd2"]] + bacq([5]))
                    bwr([5], ssm)
                    iv = brd([5], Sm(lambda s: s.activation(
                        rows4[0:2, :], pb[5][0:2, 0:512], AF.Ln,
                        bias=eps24[:, :]), waits=[ssm, t_eps24]))
                    iv = Sm(lambda s: s.activation(rows4[0:2, :], rows4[0:2, :],
                                                   AF.Exp, scale=-0.5),
                            waits=[iv])
                    ivB = Tm(lambda t: t.matmul(
                        pb[5][:, 0:512], sel2_sb[:, :], rows4[0:2, :],
                        start=True, stop=True),
                        waits=[iv, tl["sel2"]] + bacq([5]))
                    bwr([5], ivB)
                    ct, st_ = 2 * grp, 2 * grp + 1
                    tc0 = sb_u - TAB_LO
                    x1 = brd([bk_], Vm(lambda v, bk_=bk_, ct=ct, tc0=tc0:
                                       v.tensor_mul(
                        tmpA[:, :], pb[bk_][:, 0:512],
                        tabs[ct][:, tc0:tc0 + 512]),
                        waits=[mm, ttab[ct], c1, prev_final[0], prev_x3[0]]))
                    x2 = brd([4], Vm(lambda v, st_=st_, tc0=tc0: v.tensor_mul(
                        tmpB[:, :], pb[4][:, 0:512],
                        tabs[st_][:, tc0:tc0 + 512]),
                        waits=[rmm, ttab[st_], prev_x3[0]]))
                    x3 = Gm(lambda g: g.tensor_add(tmpA[:, :], tmpA[:, :],
                                                   tmpB[:, :]), waits=[x1, x2])
                    prev_x3[0] = x3
                    if grp == 0:
                        dst, wfree = qhat[ept], t_sB
                    else:
                        dst, wfree = khat[ps][ept], tok_kfree[ps]
                    tqv = brd([5], Vm(lambda v, dst=dst: v.tensor_mul(
                        dst[:, :], tmpA[:, :], pb[5][:, 0:512]),
                        waits=[x3, ivB, wfree]))
                    tq_l[e] = tqv
                    prev_final[0] = tqv
                    if grp == 0:
                        t_qhat[ept] = tqv
                    else:
                        tok_kr[ps][ept] = tqv
            tvv = None
            for st in range(4):
                bk_ = st % 2
                mm = None
                for dc in range(8):
                    w = [tw[("wqkv", dc)], mm]
                    if dc == 0:
                        w += bacq([bk_]) + [tq_l[7], tvv]
                    mm = Tm(lambda t, st=st, dc=dc, bk_=bk_, sb_u=sb_u: t.matmul(
                        pb[bk_][:, 0:512], h1s(dc, st * 128, 128, sb_u),
                        wqkv_sb[dc][:, 1024:1536],
                        start=(dc == 0), stop=(dc == 7)), waits=w)
                bwr([bk_], mm)
                tvv = brd([bk_], Vm(lambda v, st=st, bk_=bk_, ps=ps:
                                    v.tensor_copy(
                    vtm[ps][st][:, :], pb[bk_][:, 0:512]),
                    waits=[mm, tok_kfree[ps]]))
                tok_vr[ps][st] = tvv
            return tvv

        def stage_b2(qp, t_prev_wo):
            CUR_TAG[0] = f'a{l}-sb'
            qc2 = qp % 512
            tev = [None] * 8
            tav = [None] * 8
            toev = t_prev_wo

            def pieces_for(qt):
                out = []
                j = 0
                while j < 640:
                    kabs = qt - 512 + j
                    slot = (kabs // 512) % 2
                    kc = kabs % 512
                    w = min(512 - kc, 640 - j)
                    if j < 512 and j + w > 512:
                        w = 512 - j
                    if kabs < 3072 < kabs + w:
                        w = 3072 - kabs
                    out.append((0 if j < 512 else 1, j if j < 512 else j - 512,
                                w, slot, kc, kabs))
                    j += w
                return out

            for h in range(HEADS):
                ept, half = h // 2, h % 2
                r0 = 64 * half
                for ti in range(2):
                    qt = qp + 128 * ti
                    qc = qt % 512
                    w01 = bacq([0, 1])
                    mm = None
                    for (bk_, c0, w, slot, kc, kabs) in pieces_for(qt):
                        mm = Tm(lambda t, ept=ept, r0=r0, bk_=bk_, c0=c0, w=w,
                                slot=slot, kc=kc, qc=qc, kabs=kabs: t.matmul(
                            pb[bk_][:, c0:c0 + w],
                            qhat[ept][r0:r0 + 64, qc:qc + 128],
                            khat[slot][ept][r0:r0 + 64, kc:kc + w],
                            start=True, stop=(kabs + w > 3072)),
                            waits=[tok_kr[slot][ept], t_qhat[ept], mm] +
                                  (w01 if mm is None else []))
                        if kabs + w <= 3072:
                            mm = Tm(lambda t, bk_=bk_, c0=c0, w=w, kabs=kabs:
                                    t.matmul(
                                pb[bk_][:, c0:c0 + w], onescb[0:1, 0:128],
                                vnegrow[0:1, kabs:kabs + w],
                                start=False, stop=True),
                                waits=[t_vneg, mm, t_onesb])
                    bwr([0], mm)
                    bwr([1], mm)
                    mq1 = brd([0], Vm(lambda v: v.tensor_add(
                        pb[0][:, 0:128], pb[0][:, 0:128], m_lo_sb[:, :]),
                        waits=[mm, tw["m_lo"]]))
                    mq2 = brd([1], Vm(lambda v: v.tensor_add(
                        pb[1][:, 0:128], pb[1][:, 0:128], m_up_sb[:, :]),
                        waits=[mm, tw["m_up"], mq1]))
                    nc_ = 2 * h + ti
                    n1 = brd([0], Vm(lambda v, nc_=nc_: v.tensor_reduce(
                        nmax[:, nc_:nc_ + 1], pb[0][:, 0:512], AX.X, ALU.max,
                        negate=True), waits=[mq1, mq2]))
                    n2 = brd([1], Vm(lambda v: v.tensor_reduce(
                        tmpA[:, 0:1], pb[1][:, 0:128], AX.X, ALU.max,
                        negate=True), waits=[mq2, n1, prev_nm[0]]))
                    prev_nm[0] = Vm(lambda v, nc_=nc_: v.tensor_scalar(
                        nmax[:, nc_:nc_ + 1], nmax[:, nc_:nc_ + 1],
                        tmpA[:, 0:1], None, ALU.min), waits=[n2])
                    if qt < L2F_LO:
                        prev_nm[0] = Vm(lambda v, nc_=nc_, qt=qt:
                                        v.tensor_mul(
                            nmax[:, nc_:nc_ + 1], nmax[:, nc_:nc_ + 1],
                            vcol[:, qt // 128:qt // 128 + 1]),
                            waits=[prev_nm[0], tl["validc"]])
                ttrA = Tm(lambda t, h=h: t.transpose(
                    pb[1][0:1, 0:128], nmax[:, 2 * h:2 * h + 1],
                    identsb[:, :]),
                    waits=[prev_nm[0], tl["ident"]] + bacq([1]))
                ttrB = Tm(lambda t, h=h: t.transpose(
                    pb[1][0:1, 128:256], nmax[:, 2 * h + 1:2 * h + 2],
                    identsb[:, :]), waits=[prev_nm[0], ttrA])
                bwr([1], ttrB)
                trw = brd([1], Vm(lambda v: v.tensor_copy(
                    nrow4[0:1, 0:256], pb[1][0:1, 0:256]),
                    waits=[ttrA, ttrB, prev_bc[0]]))
                trw = Vm(lambda v: v.tensor_copy(
                    nrow4[0:1, 256:512], nrow4[0:1, 0:256]), waits=[trw])
                pbh = pbuf[h % 2]
                smm = [None] * 3
                wst = bacq([5, 6, 7])
                for b in range(6):
                    kb = qp - 512 + b * 128
                    slot = (kb // 512) % 2
                    kc = kb % 512
                    bs = 5 + b // 2
                    c0 = 256 * (b % 2)
                    smm[b // 2] = Tm(lambda t, ept=ept, r0=r0, slot=slot,
                                     kc=kc, bs=bs, c0=c0, qc2=qc2, b=b:
                                     t.matmul(
                        pb[bs][:, c0:c0 + 256],
                        khat[slot][ept][r0:r0 + 64, kc:kc + 128],
                        qhat[ept][r0:r0 + 64, qc2:qc2 + 256],
                        start=(b % 2 == 0), stop=False, skip_group_check=True),
                        waits=[tok_kr[slot][ept], t_qhat[ept], smm[b // 2]] +
                              (wst if b == 0 else []))
                tbcm = [None] * 3
                for g2 in range(3):
                    tbcm[g2] = Tm(lambda t, g2=g2: t.matmul(
                        pb[5 + g2][:, 0:512], onescb[0:1, 0:128],
                        nrow4[0:1, 0:512], start=False, stop=True,
                        skip_group_check=True),
                        waits=[trw, smm[g2], t_onesb,
                               tbcm[g2 - 1] if g2 else None])
                    bwr([5 + g2], tbcm[g2])
                prev_bc[0] = tbcm[2]
                mk = brd([5], Vm(lambda v: v.tensor_add(
                    pb[5][:, 0:128], pb[5][:, 0:128], m_up_sb[:, :]),
                    waits=[tbcm[0], tw["m_up"]]))
                mk = brd([5], Vm(lambda v: v.tensor_scalar(
                    pb[5][:, 128:256], pb[5][:, 128:256], -BIG, None,
                    ALU.add), waits=[mk]))
                mk = brd([5], Vm(lambda v: v.tensor_add(
                    pb[5][:, 384:512], pb[5][:, 384:512], m_up_sb[:, :]),
                    waits=[mk]))
                mk2 = brd([7], Vm(lambda v: v.tensor_add(
                    pb[7][:, 0:128], pb[7][:, 0:128], m_lo_sb[:, :]),
                    waits=[tbcm[2], tw["m_lo"], mk]))
                mk2 = brd([7], Vm(lambda v: v.tensor_scalar(
                    pb[7][:, 256:384], pb[7][:, 256:384], -BIG, None,
                    ALU.add), waits=[mk2]))
                mk2 = brd([7], Vm(lambda v: v.tensor_add(
                    pb[7][:, 384:512], pb[7][:, 384:512], m_lo_sb[:, :]),
                    waits=[mk2]))
                we = [tav[h - 2] if h >= 2 else None]
                bank_ready = [mk, mk, tbcm[1], tbcm[1], mk2, mk2]
                eb = [None] * 6
                ep = None
                for b in range(6):
                    kb = qp - 512 + b * 128
                    bs = 5 + b // 2
                    c0 = 256 * (b % 2)
                    eb[b] = brd([bs], Sm(lambda s, pbh=pbh, b=b, bs=bs, c0=c0,
                                         kb=kb: s.activation(
                        pbh[:, b * 256:(b + 1) * 256], pb[bs][:, c0:c0 + 256],
                        AF.Exp, bias=vnegc[:, kb // 128:kb // 128 + 1]),
                        waits=[bank_ready[b], t_vnegc, ep] + we))
                    ep = eb[b]
                tev[h] = eb[5]
                wd = bacq([4])
                tdn = None
                for b in range(6):
                    tdn = Tm(lambda t, b=b, pbh=pbh: t.matmul(
                        pb[4][0:1, 0:256], ones128b[:, :],
                        pbh[:, b * 256:(b + 1) * 256],
                        start=(b == 0), stop=(b == 5)),
                        waits=[eb[b], tdn, t_ones128b] +
                              (wd if b == 0 else []))
                bwr([4], tdn)
                rc = brd([4], Vm(lambda v: v.tensor_scalar(
                    denrow[0:1, 0:256], pb[4][0:1, 0:256], 1e-20, None,
                    ALU.add), waits=[tdn, prev_tbc[0]]))
                rc = Vm(lambda v: v.reciprocal(
                    denrow[0:1, 0:256], denrow[0:1, 0:256]), waits=[rc])
                if qp < L2F_LO:
                    rc = Vm(lambda v, qp=qp: v.tensor_mul(
                        denrow[0:1, 0:256], denrow[0:1, 0:256],
                        vrow[0:1, qp:qp + 256]), waits=[rc, tl["validr"]])
                avb = 2 + (h % 2)
                avc = 256 * ((h // 2) % 2)
                tp = (0, 64) if half else None
                wa = bacq([avb])
                amm = None
                for b in range(6):
                    kb = qp - 512 + b * 128
                    slot = (kb // 512) % 2
                    stt_ = (kb % 512) // 128
                    amm = Tm(lambda t, slot=slot, stt_=stt_, h=h, avb=avb,
                             avc=avc, tp=tp, pbh=pbh, b=b, r0=r0: t.matmul(
                        pb[avb][r0:r0 + 64, avc:avc + 256],
                        vtm[slot][stt_][:, h * 64:h * 64 + 64],
                        pbh[:, b * 256:(b + 1) * 256],
                        start=(b == 0), stop=(b == 5), tile_position=tp),
                        waits=[tok_vr[slot][stt_], eb[b], amm] +
                              (wa if b == 0 else []))
                tav[h] = amm
                bwr([avb], amm)
                tbc = Tm(lambda t: t.matmul(
                    pb[4][:, 256:512], onesc[0:1, :], denrow[0:1, 0:256],
                    start=True, stop=True), waits=[rc, tdn, amm])
                bwr([4], tbc)
                prev_tbc[0] = tbc
                toev = brd([avb], Vm(lambda v, ept=ept, r0=r0, avb=avb,
                                     avc=avc, qc2=qc2: v.tensor_copy(
                    obuf[ept][r0:r0 + 64, qc2:qc2 + 256],
                    pb[avb][r0:r0 + 64, avc:avc + 256]),
                    waits=[amm, toev]))
                toev = brd([4], Vm(lambda v, ept=ept, r0=r0, qc2=qc2:
                                   v.tensor_mul(
                    obuf[ept][r0:r0 + 64, qc2:qc2 + 256],
                    obuf[ept][r0:r0 + 64, qc2:qc2 + 256],
                    pb[4][r0:r0 + 64, 256:512]), waits=[tbc, toev]))
            return toev

        prev_nm = [None]
        prev_bc = [None]
        prev_tbc = [None]

        for sb_u in range(lo_keys, T, 512):
            ta = stage_a(sb_u)
            if sb_u < lo_q or SUBB == 0:
                t_sB = ta
                continue
            tos = None
            for k2 in range(2):
                tos = stage_b2(sb_u + k2 * 256,
                               tok_out[7] if k2 == 0 else None)
            t_sB = tos
            if SUBB < 3 or SUBB == 23:
                continue
            CUR_TAG[0] = f'a{l}-wo'
            for f in range(8):
                bk_ = 6 + f % 2
                mm = None
                for ic in range(4):
                    w = [tos, tw[("wo", ic)], mm]
                    if ic == 0:
                        w += bacq([bk_])
                    mm = Tm(lambda t, f=f, ic=ic, bk_=bk_: t.matmul(
                        pb[bk_][:, 0:512], wo_sb[ic][:, f * 128:(f + 1) * 128],
                        obuf[ic][:, 0:512], start=(ic == 0), stop=(ic == 3)),
                        waits=w)
                bwr([bk_], mm)
                oc = sb_u - out_base
                icc = sb_u - in_base
                tok_out[f] = brd([bk_], Vm(lambda v, f=f, oc=oc, icc=icc,
                                           bk_=bk_: v.tensor_add(
                    arena[f][:, oc:oc + 512], arena[f][:, icc:icc + 512],
                    pb[bk_][:, 0:512]), waits=[mm, tok_out[f]]))
            ps = (sb_u // 512) % 2
            tok_kfree[1 - ps] = tos
        return barrier()

    bar = tok_p1
    if NPHASES >= 2:
        bar = attn_phase(0, 1024, L1A_LO, AB, 1536, tok_p1)

    def ffn_phase(l, lo, in_base, out_base, prevb, final):
        tw1 = []
        for i in range(8):
            tw1.append(GD(lambda g, i=i, l=l: g.dma_start(
                w1_sb[i][:, :], ap("w1")[l, i * 128:(i + 1) * 128, :]),
                waits=prevb))
        tb = {}
        for n, dd in [("b1a", b1a_sb), ("b1g", b1g_sb), ("b2c", b2_sb),
                      ("lgc", lg_sb), ("lbc", lb_sb)]:
            tb[n] = Ym(lambda y, n=n, dd=dd, l=l: y.dma_start(
                dd[:, :], ap(n)[l]), waits=prevb)
        tz = Vm(lambda v: v.memset(gated[NHC - 1][:, :], 0.0), waits=prevb)

        blocks = []
        u = lo
        while u < T:
            w = min(512, T - u)
            blocks.append((u, w))
            u += w

        tok_y = [None] * 8
        tok_w2 = None
        tok_xa = None
        t_pass2_last = None
        def xs(dc, c0, w, u0):
            off = (u0 - in_base) - (1 if dc >= 4 else 0) + c0
            return arena[dc][:, off:off + w]

        for (u0, W) in blocks:
            CUR_TAG[0] = f'f{l}-st'
            w5 = bacq([5])
            w4f = bacq([4])
            tsum = None
            for f in range(8):
                tsum = Tm(lambda t, f=f, W=W, u0=u0: t.matmul(
                    pb[5][0:1, 0:W], ones128b[:, :], xs(f, 0, W, u0),
                    start=(f == 0), stop=(f == 7)),
                    waits=[t_ones128b, tsum, tok_y[7]] +
                          (w5 + w4f + (prevb if u0 == lo else [])
                           if f == 0 else []))
            bwr([5], tsum)
            tsq = None
            for f in range(8):
                s1 = Sm(lambda s, f=f, W=W, u0=u0: s.activation(
                    tmpCb[:, 0:W], xs(f, 0, W, u0), AF.Square),
                    waits=[tsq, tok_y[7]] +
                          (prevb if u0 == lo and f == 0 else []))
                tsq = Tm(lambda t, f=f, W=W: t.matmul(
                    pb[4][0:1, 0:W], ones128b[:, :], tmpCb[:, 0:W],
                    start=(f == 0), stop=(f == 7)),
                    waits=[s1, t_ones128b] + ([tsum] if f == 0 else []))
            bwr([4], tsq)
            r1 = brd([5], Vm(lambda v, W=W: v.tensor_scalar(
                rowaux[0:1, 0:W], pb[5][0:1, 0:W], 1.0 / D, None, ALU.mult),
                waits=[tsum, tsq]))
            r2 = brd([4], Vm(lambda v, W=W: v.tensor_scalar(
                rows4[0:1, 0:W], pb[4][0:1, 0:W], 1.0 / D, None, ALU.mult),
                waits=[tsq]))
            r3 = brd([4], Vm(lambda v, W=W: v.tensor_mul(
                pb[4][0:1, 0:W], rowaux[0:1, 0:W], rowaux[0:1, 0:W]),
                waits=[r1, r2]))
            r4 = brd([4], Vm(lambda v, W=W: v.tensor_sub(
                rows4[0:1, 0:W], rows4[0:1, 0:W], pb[4][0:1, 0:W]), waits=[r3]))
            r5 = Sm(lambda s, W=W: s.activation(rows4[0:1, 0:W], rows4[0:1, 0:W],
                                                AF.Ln, bias=eps5[0:1, :]),
                    waits=[r4, t_eps5])
            r5 = Sm(lambda s, W=W: s.activation(rows4[0:1, 0:W], rows4[0:1, 0:W],
                                                AF.Exp, scale=-0.5), waits=[r5])
            rb = Vm(lambda v, W=W: v.tensor_scalar(
                rowaux[0:1, 0:W], rowaux[0:1, 0:W], -1.0, None, ALU.mult),
                waits=[r5])
            rb = Vm(lambda v, W=W: v.tensor_mul(rowaux[0:1, 0:W],
                                                rowaux[0:1, 0:W],
                                                rows4[0:1, 0:W]), waits=[rb])
            tB1 = Tm(lambda t, W=W: t.matmul(
                pb[6][:, 0:W], onesc[0:1, :], rows4[0:1, 0:W],
                start=True, stop=True), waits=[r5, t_ones] + bacq([6]))
            bwr([6], tB1)
            tB2 = Tm(lambda t, W=W: t.matmul(
                pb[7][:, 0:W], onesc[0:1, :], rowaux[0:1, 0:W],
                start=True, stop=True), waits=[rb] + bacq([7]))
            bwr([7], tB2)
            tlnh = [None] * 8
            for f in range(8):
                a1 = brd([6], Vm(lambda v, f=f, W=W, u0=u0: v.tensor_mul(
                    tmpB[:, 0:W], xs(f, 0, W, u0), pb[6][:, 0:W]),
                    waits=[tB1, tlnh[f - 1] if f else None]))
                a2 = brd([7], Vm(lambda v, W=W: v.tensor_add(
                    tmpB[:, 0:W], tmpB[:, 0:W], pb[7][:, 0:W]),
                    waits=[a1, tB2]))
                tlnh[f] = Vm(lambda v, f=f, W=W: v.tensor_scalar(
                    lnh[f][:, 0:W], tmpB[:, 0:W], lg_sb[:, f:f + 1],
                    lb_sb[:, f:f + 1], ALU.mult, ALU.add),
                    waits=[a2, tb["lgc"], tb["lbc"], t_pass2_last])
            tvB = None
            if (not final) and u0 < L2F_LO:
                tvBm = Tm(lambda t, u0=u0, W=W: t.matmul(
                    pb[4][:, 0:W], onescb[0:1, 0:128], vrow[0:1, u0:u0 + W],
                    start=True, stop=True),
                    waits=[tlnh[7], tl["validr"], t_onesb] + bacq([4]))
                bwr([4], tvBm)
                tvB = brd([4], Vm(lambda v, W=W: v.tensor_copy(
                    validB[:, 0:W], pb[4][:, 0:W]), waits=[tvBm]))
            CUR_TAG[0] = f'f{l}-w1'
            tgd = [None] * NHC
            for c in range(NHC):
                aw = min(128, HID - c * 128)
                pa, pg = [(0, 1), (2, 3), (6, 7)][c % 3]
                mma = None
                wA = bacq([pa])
                for dc in range(8):
                    mma = Tm(lambda t, c=c, dc=dc, aw=aw, W=W, pa=pa: t.matmul(
                        pb[pa][0:aw, 0:W], w1_sb[dc][:, c * 128:c * 128 + aw],
                        lnh[dc][:, 0:W], start=(dc == 0), stop=(dc == 7)),
                        waits=[tlnh[dc], tw1[dc], mma] +
                              (wA if dc == 0 else []))
                bwr([pa], mma)
                mmg = None
                wG = bacq([pg])
                for dc in range(8):
                    mmg = Tm(lambda t, c=c, dc=dc, aw=aw, W=W, pg=pg: t.matmul(
                        pb[pg][0:aw, 0:W],
                        w1_sb[dc][:, HID + c * 128:HID + c * 128 + aw],
                        lnh[dc][:, 0:W], start=(dc == 0), stop=(dc == 7)),
                        waits=[tlnh[dc], mmg] + (wG if dc == 0 else []))
                bwr([pg], mmg)
                ge = brd([pg], Sm(lambda s, c=c, aw=aw, W=W, pg=pg: s.activation(
                    gated[c][0:aw, 0:W], pb[pg][0:aw, 0:W], AF.Gelu,
                    bias=b1g_sb[0:aw, c:c + 1]),
                    waits=[mmg, tb["b1g"], tz]))
                tgd[c] = brd([pa], Vm(lambda v, c=c, aw=aw, W=W, pa=pa:
                                      v.scalar_tensor_tensor(
                    gated[c][0:aw, 0:W], pb[pa][0:aw, 0:W],
                    b1a_sb[0:aw, c:c + 1], gated[c][0:aw, 0:W],
                    ALU.add, ALU.mult), waits=[ge, tb["b1a"]]))
            CUR_TAG[0] = f'f{l}-w2'
            mmo = [None] * 8
            wall = bacq(list(range(8)))
            for c in range(NHC):
                tok_w2A = GD(lambda g, c=c, l=l: g.dma_start(
                    w2r[:, 0:512], ap("w2p")[l, c * 128:(c + 1) * 128, 0:512]),
                    waits=[tok_w2, tgd[c], mmo[3]])
                tok_w2 = GD(lambda g, c=c, l=l: g.dma_start(
                    w2r[:, 512:1024],
                    ap("w2p")[l, c * 128:(c + 1) * 128, 512:1024]),
                    waits=[tok_w2A, mmo[7]])
                for f in range(8):
                    mmo[f] = Tm(lambda t, c=c, f=f, W=W: t.matmul(
                        pb[f][:, 0:W], w2r[:, f * 128:(f + 1) * 128],
                        gated[c][:, 0:W], start=(c == 0), stop=(c == NHC - 1)),
                        waits=[tgd[c], tok_w2A if f < 4 else tok_w2, mmo[f]] +
                              (wall if c == 0 and f == 0 else []))
            for f in range(8):
                bwr([f], mmo[f])
            CUR_TAG[0] = f'f{l}-ep'
            if not final:
                for f in range(8):
                    e1 = brd([f], Vm(lambda v, f=f, u0=u0, W=W:
                                     v.scalar_tensor_tensor(
                        tmpB[:, 0:W], pb[f][:, 0:W], b2_sb[:, f:f + 1],
                        arena[f][:, u0 - in_base:u0 - in_base + W],
                        ALU.add, ALU.add),
                        waits=[mmo[f], tb["b2c"], tok_y[f],
                               tok_y[f - 1] if f else tok_y[7]]))
                    if u0 < L2F_LO:
                        tok_y[f] = Vm(lambda v, f=f, u0=u0, W=W: v.tensor_mul(
                            arena[f][:, u0 - out_base:u0 - out_base + W],
                            tmpB[:, 0:W], validB[:, 0:W]), waits=[e1, tvB])
                    else:
                        tok_y[f] = Vm(lambda v, f=f, u0=u0, W=W: v.tensor_copy(
                            arena[f][:, u0 - out_base:u0 - out_base + W],
                            tmpB[:, 0:W]), waits=[e1])
                t_pass2_last = tok_y[7]
            else:
                tst = [None, None]
                for f in range(8):
                    buf = tmpB if f % 2 == 0 else tmpC
                    tok_xa = Ym(lambda y, f=f, u0=u0, W=W: y.dma_start(
                        xaring[:, 0:W],
                        xa_d.ap()[f, :, u0 - L2F_LO:u0 - L2F_LO + W]),
                        waits=[tok_xa, tok_y[f - 1] if f else tok_y[7]] +
                              (prevb if u0 == lo and f == 0 else []))
                    e1 = brd([f], Vm(lambda v, f=f, u0=u0, W=W, buf=buf:
                                     v.scalar_tensor_tensor(
                        buf[:, 0:W], pb[f][:, 0:W], b2_sb[:, f:f + 1],
                        arena[f][:, u0 - in_base:u0 - in_base + W],
                        ALU.add, ALU.add),
                        waits=[mmo[f], tb["b2c"], tst[f % 2]]))
                    ty2 = Vm(lambda v, f=f, W=W, buf=buf:
                             v.scalar_tensor_tensor(
                        buf[:, 0:W], buf[:, 0:W], gbc[:, f:f + 1],
                        xaring[:, 0:W], ALU.mult, ALU.add),
                        waits=[e1, t_gb, tok_xa])
                    tok_y[f] = ty2
                    tst[f % 2] = Ym(lambda y, f=f, u0=u0, W=W, buf=buf:
                                    y.dma_start(
                        out_d.ap()[f, :, u0 - L2F_LO:u0 - L2F_LO + W],
                        buf[:, 0:W]), waits=[ty2])
                    tok_y[f] = tst[f % 2]
                t_pass2_last = tok_y[7]
        return barrier()

    if NPHASES >= 3:
        bar = ffn_phase(0, L1F_LO, 1536, 1920, bar, False)
    if NPHASES >= 4:
        bar = attn_phase(1, 2048, L2A_LO, 1920, 2560, bar)
    if NPHASES >= 5:
        bar = ffn_phase(1, L2F_LO, 2560, 0, bar, True)
    final_waits = barrier()

    with (
        nc.semaphore("sem_t") as sem_t,
        nc.semaphore("sem_v") as sem_v,
        nc.semaphore("sem_s") as sem_s,
        nc.semaphore("sem_g") as sem_g,
        nc.semaphore("sem_gd") as sem_gd,
        nc.semaphore("sem_y") as sem_y,
        nc.Block() as block,
    ):
        sems = {"t": sem_t, "v": sem_v, "s": sem_s, "g": sem_g, "gd": sem_gd,
                "y": sem_y}

        def emit_stream(eng_obj, eng_key):
            last = {}
            for (ek, emit, waits, inc) in ops:
                if ek != eng_key:
                    continue
                for (sname, val, peng) in waits:
                    if last.get(sname, 0) >= val:
                        continue
                    eng_obj.wait_ge(sems[sname], val)
                    last[sname] = val
                ins = emit(eng_obj)
                if inc is not None:
                    ins.then_inc(sems[inc[0]], inc[2])
            if eng_key == "Y":
                for (sname, val, _) in final_waits:
                    if sname == "y" and last.get("y", 0) < val:
                        eng_obj.wait_ge(sems["y"], val)

        @block.tensor
        def _(t):
            emit_stream(t, "T")

        @block.vector
        def _(v):
            emit_stream(v, "V")

        @block.scalar
        def _(s):
            emit_stream(s, "S")

        @block.gpsimd
        def _(g):
            emit_stream(g, "G")

        @block.sync
        def _(y):
            emit_stream(y, "Y")

    return nc


_TABCACHE = {}


def _host_tables(lo):
    if lo in _TABCACHE:
        return _TABCACHE[lo]
    import jax
    import jax.numpy as jnp
    with jax.default_device(jax.devices("cpu")[0]):
        half = jnp.arange(0, 64, 2, dtype=jnp.float32)
        invf = 1.0 / 10000.0 ** (half / 64)
        t = jnp.arange(lo + TAB_LO, lo + TAB_LO + TABW, dtype=jnp.float32)
        fr = t[:, None] * invf[None, :]
        base = (half + 0.4 * 64) / (1.4 * 64)
        power = (t - 4096) / float(256)
        sc32 = base[None, :] ** power[:, None]
        cos32, sin32 = jnp.cos(fr), jnp.sin(fr)
    cos = np.tile(np.asarray(cos32, np.float64), (1, 2))
    sin = np.tile(np.asarray(sin32, np.float64), (1, 2))
    sc = np.tile(np.asarray(sc32, np.float64), (1, 2))
    prods = [(cos * sc).T, (sin * sc).T, (cos / sc).T, (sin / sc).T]
    tabs_ = np.zeros((8, 128, TABW), np.float32)
    for i, p in enumerate(prods):
        hi = p.astype(np.float32)
        lo_ = (p - hi.astype(np.float64)).astype(np.float32)
        tabs_[i] = np.tile(hi, (2, 1))
        tabs_[4 + i] = np.tile(lo_, (2, 1))
    _TABCACHE[lo] = tabs_
    return tabs_


def kernel(L0_attn_x_seq, L0_attn_Wq, L0_attn_bq, L0_attn_Wk, L0_attn_bk,
           L0_attn_Wv, L0_attn_bv, L0_attn_gamma, L0_ffn_wqkv, L0_ffn_q_scale,
           L0_ffn_k_scale, L0_ffn_wo, L0_ffn_ln_g, L0_ffn_ln_b, L0_ffn_w1,
           L0_ffn_b1, L0_ffn_w2, L0_ffn_b2, L0_c, L0_wmod):
    f32 = np.float32
    x = np.asarray(L0_attn_x_seq, f32)
    nc = build_nc()

    def colz(v):
        return np.ascontiguousarray(np.asarray(v, f32).reshape(8, 128).T)

    w2p = np.zeros((2, HIDP, D), f32)
    w2p[:, :HID] = np.asarray(L0_ffn_w2, f32)
    b1 = np.asarray(L0_ffn_b1, f32)
    b1a = np.zeros((2, 128, NHC), f32)
    b1g = np.zeros((2, 128, NHC), f32)
    for l in range(2):
        pa = np.zeros(HIDP, f32)
        pa[:HID] = b1[l, :HID]
        pg = np.zeros(HIDP, f32)
        pg[:HID] = b1[l, HID:]
        b1a[l] = pa.reshape(NHC, 128).T
        b1g[l] = pg.reshape(NHC, 128).T
    qs = np.asarray(L0_ffn_q_scale, f32)
    ks = np.asarray(L0_ffn_k_scale, f32)

    def tile2(v):
        return np.ascontiguousarray(np.tile(v, 2).reshape(128, 1))

    def perm(v):
        return np.concatenate([v[32:], v[:32]])

    rotm = np.zeros((128, 128), f32)
    for d in range(128):
        h0 = (d // 64) * 64
        dl = d % 64
        rotm[h0 + (dl + 32) % 64, d] = -1.0 if dl < 32 else 1.0
    jj, ii = np.meshgrid(np.arange(128), np.arange(128), indexing="ij")
    m_up = np.where(jj >= ii, 0.0, -BIG).astype(f32)
    m_lo = np.where(jj <= ii, 0.0, -BIG).astype(f32)
    tri01 = (jj <= ii).astype(f32)
    bd2 = np.zeros((128, 2), f32)
    bd2[:64, 0] = 1.0
    bd2[64:, 1] = 1.0
    sel2 = np.zeros((2, 128), f32)
    sel2[0, :64] = 1.0
    sel2[1, 64:] = 1.0

    shared = {
        "wq": np.ascontiguousarray(np.asarray(L0_attn_Wq, f32).T),
        "wk": np.ascontiguousarray(np.asarray(L0_attn_Wk, f32).T),
        "wv": np.ascontiguousarray(np.asarray(L0_attn_Wv, f32).T),
        "bqc": colz(L0_attn_bq), "bkc": colz(L0_attn_bk),
        "bvr": np.asarray(L0_attn_bv, f32).reshape(1, D),
        "wqkv": np.ascontiguousarray(np.asarray(L0_ffn_wqkv, f32)),
        "wo": np.ascontiguousarray(np.asarray(L0_ffn_wo, f32)),
        "w1": np.ascontiguousarray(np.asarray(L0_ffn_w1, f32)),
        "w2p": w2p, "b1a": b1a, "b1g": b1g,
        "b2c": np.stack([colz(np.asarray(L0_ffn_b2, f32)[l]) for l in range(2)]),
        "lgc": np.stack([colz(np.asarray(L0_ffn_ln_g, f32)[l]) for l in range(2)]),
        "lbc": np.stack([colz(np.asarray(L0_ffn_ln_b, f32)[l]) for l in range(2)]),
        "qsc": np.stack([tile2(qs[l]) for l in range(2)]),
        "qspc": np.stack([tile2(perm(qs[l])) for l in range(2)]),
        "ksc": np.stack([tile2(ks[l]) for l in range(2)]),
        "kspc": np.stack([tile2(perm(ks[l])) for l in range(2)]),
        "wmod": np.ascontiguousarray(np.asarray(L0_wmod, f32)),
        "gam": np.asarray(L0_attn_gamma, f32).reshape(1, 1),
        "ident": np.eye(128, dtype=f32),
        "rotm": rotm, "m_up": m_up, "m_lo": m_lo, "tri01": tri01,
        "bd2": bd2, "sel2": sel2,
    }

    in_maps = []
    for c in range(8):
        b, qdx = c // 4, c % 4
        start = OWN * qdx
        lo = start - 3072
        xh = np.zeros((T, D), f32)
        s0 = max(0, lo)
        xh[s0 - lo:] = x[b, s0:start + OWN]
        upos = np.arange(lo, start + OWN)
        valid = (upos >= 0).astype(f32)
        m = dict(shared)
        m["xh"] = np.ascontiguousarray(xh.T.reshape(8, 128, T))
        m["crow"] = np.asarray(L0_c, f32)[b].reshape(1, D)
        m["tabs"] = _host_tables(lo)
        m["validr"] = np.ascontiguousarray(
            np.tile(valid[:VROW_W].reshape(1, VROW_W), (64, 1)))
        m["validc"] = np.ascontiguousarray(valid.reshape(T // 128, 128).T)
        in_maps.append(m)

    res = run_bass_kernel_spmd(nc, in_maps, core_ids=list(range(8)))
    out = np.zeros((2, 8192, D), f32)
    for c in range(8):
        b, qdx = c // 4, c % 4
        r = np.asarray(res.results[c]["out"]).reshape(D, OWN)
        out[b, OWN * qdx:OWN * (qdx + 1)] = r.T
    return out


# version 14 (final)
# speedup vs baseline: 1.2050x; 1.0064x over previous
import numpy as np
import concourse.bass as bass
import concourse.mybir as mybir
from concourse.bass_utils import run_bass_kernel_spmd

F32 = mybir.dt.float32
BF16 = mybir.dt.bfloat16
AF = mybir.ActivationFunctionType
ALU = mybir.AluOpType
AX = mybir.AxisListType

D = 1024
OWN = 2048
T = 5120
CH = 512
HEADS = 8
HID = 2730
NHC = 22
HIDP = NHC * 128
QK_SCALE = 8.0
EPS_RET = 1e-6
BIG = 1.0e30

RET_LO, L1A_LO, L1F_LO, L2A_LO, L2F_LO = 512, 1536, 1920, 2560, 3072
T_TAGS = []
CUR_TAG = ["init"]
TAB_LO, TABW = 1024, 4096
AB = 1016
ARENA_W = 4112
VROW_W = 3456
VNEG_W = 3200
NPHASES = 5
SUBB = 3


def build_nc():
    nc = bass.Bass("TRN2", target_bir_lowering=False)

    dr = {}
    for n, shp in [
        ("xh", [8, 128, T]), ("wq", [D, D]), ("wk", [D, D]), ("wv", [D, D]),
        ("bqc", [128, 8]), ("bkc", [128, 8]), ("bvr", [1, D]),
        ("wqkv", [2, D, 1536]), ("wo", [2, 512, D]),
        ("w1", [2, D, 2 * HID]), ("w2p", [2, HIDP, D]),
        ("b1a", [2, 128, NHC]), ("b1g", [2, 128, NHC]), ("b2c", [2, 128, 8]),
        ("lgc", [2, 128, 8]), ("lbc", [2, 128, 8]),
        ("qsc", [2, 128, 1]), ("qspc", [2, 128, 1]),
        ("ksc", [2, 128, 1]), ("kspc", [2, 128, 1]),
        ("crow", [1, D]), ("wmod", [D, 6 * D]), ("gam", [1, 1]),
        ("tabs", [8, 128, TABW]),
        ("validr", [64, VROW_W]), ("validc", [128, T // 128]),
        ("ident", [128, 128]), ("rotm", [128, 128]),
        ("m_up", [128, 128]), ("m_lo", [128, 128]), ("tri01", [128, 128]),
        ("bd2", [128, 2]), ("sel2", [2, 128]),
    ]:
        dr[n] = nc.dram_tensor(n, shp, F32, kind="ExternalInput")
    out_d = nc.dram_tensor("out", [8, 128, OWN], F32, kind="ExternalOutput")
    xa_d = nc.dram_tensor("xa_scr", [8, 128, OWN], F32, kind="Internal")

    def ap(n):
        return dr[n].ap()

    SB_BASE = 17408

    def at(name, shape, dtype, off):
        return nc.alloc_sbuf_tensor_at(name, shape, dtype, offset=SB_BASE + off)

    arena = [at(f"arena{f}", [128, ARENA_W], BF16, f * ARENA_W * 2)
             for f in range(8)]
    B0 = 8 * ARENA_W * 2
    wqT = [at(f"wqT{i}", [128, 1024], BF16, B0 + i * 2048) for i in range(8)]
    wkT = [at(f"wkT{i}", [128, 1024], BF16, B0 + 16384 + i * 2048) for i in range(8)]
    wvT = [at(f"wvT{i}", [128, 1024], BF16, B0 + 32768 + i * 2048) for i in range(8)]
    kring = [[at(f"krk{c}_{i}", [128, 512], BF16, B0 + 49152 + (c * 8 + i) * 1024)
              for i in range(8)] for c in range(2)]
    vring = [[at(f"vrk{c}_{i}", [128, 1024], BF16, B0 + 65536 + (c * 4 + i) * 2048)
              for i in range(4)] for c in range(2)]
    bvb = [at(f"bvb{i}", [128, 512], BF16, B0 + 81920 + i * 1024) for i in range(2)]
    gpow = at("gpow", [1, 512], F32, B0 + 83968)
    tabs = [at(f"tab{i}", [128, TABW], BF16, B0 + i * TABW * 2) for i in range(4)]
    wqkv_sb = [at(f"wqkv{i}", [128, 1536], BF16, B0 + 32768 + i * 3072)
               for i in range(8)]
    wo_sb = [at(f"wo{i}", [128, 1024], BF16, B0 + 57344 + i * 2048) for i in range(4)]
    khat = [[at(f"khat{c}_{i}", [128, 512], BF16, B0 + 65536 + (c * 4 + i) * 1024)
             for i in range(4)] for c in range(2)]
    vtm = [[at(f"vtm{c}_{i}", [128, 512], BF16, B0 + 73728 + (c * 4 + i) * 1024)
            for i in range(4)] for c in range(2)]
    w1_sb = [at(f"w1_{i}", [128, 5460], BF16, B0 + i * 10944) for i in range(8)]

    C0 = B0 + 87552
    _cc = [C0]

    def cal(name, shape, dtype):
        t = at(name, shape, dtype, _cc[0])
        per = int(np.prod(shape[1:])) * (2 if dtype == BF16 else 4)
        _cc[0] += (per + 31) // 32 * 32
        return t

    tmpA = cal("tmpA", [128, 512], F32)
    tmpB = cal("tmpB", [128, 512], F32)
    tmpC = cal("tmpC", [128, 512], F32)
    tmpCb = at("tmpCb", [128, 512], BF16, 8 * ARENA_W * 2 + 87552 + 4096)
    rows4 = cal("rows4", [64, 512], F32)
    rowaux = cal("rowaux", [64, 512], F32)
    denrow = cal("denrow", [64, 512], F32)
    vrow = cal("vrow", [64, VROW_W], BF16)
    identsb = cal("identsb", [128, 128], F32)
    onesc = cal("onesc", [64, 128], F32)
    ones128 = cal("ones128", [128, 1], F32)
    tri01_sb = cal("tri01", [128, 128], BF16)
    modc = cal("modc", [128, 48], F32)
    gac = cal("gac", [128, 8], F32)
    gbc = cal("gbc", [128, 8], F32)
    ca1c = cal("ca1c", [128, 8], F32)
    cb1c = cal("cb1c", [128, 8], F32)
    tgat = cal("tgat", [128, 8], F32)
    sgam = cal("sgam", [1, 1], F32)
    lng = cal("lng", [1, 1], F32)
    lngc = cal("lngc", [128, 1], F32)
    gnegc = cal("gnegc", [128, 4], F32)
    gprevc = cal("gprevc", [128, 4], F32)
    iotac = cal("iotac", [128, 4], F32)
    gmk8 = cal("gmk8", [128, 8], F32)
    gmk = [gmk8[:, i:i + 1] for i in range(8)]
    vcol = cal("vcol", [128, T // 128], F32)
    bq_sb = cal("bq", [128, 8], F32)
    bk_sb = cal("bk", [128, 8], F32)
    crow_sb = cal("crowsb", [128, 8], F32)
    b1a_sb = cal("b1a", [128, NHC], F32)
    b1g_sb = cal("b1g", [128, NHC], F32)
    b2_sb = cal("b2", [128, 8], F32)
    lg_sb = cal("lgcs", [128, 8], F32)
    lb_sb = cal("lbcs", [128, 8], F32)
    qs8_sb = cal("qs8", [128, 1], F32)
    qs8p_sb = cal("qs8p", [128, 1], F32)
    ks_sb = cal("ks", [128, 1], F32)
    ksp_sb = cal("ksp", [128, 1], F32)
    bd2_sb = cal("bd2", [128, 2], F32)
    sel2_sb = cal("sel2", [2, 128], F32)
    outp = cal("outp", [128, 128], F32)
    eps5 = cal("eps5", [2, 1], F32)
    eps24 = cal("eps24", [2, 1], F32)
    onescb = cal("onescb", [64, 256], BF16)
    ones128b = cal("ones128b", [128, 1], BF16)
    crow_bf = cal("crowbf", [128, 8], BF16)
    PB = 176832
    assert _cc[0] <= PB, _cc[0]
    xfm = [at(f"xfm{i}", [128, 512], F32, PB + i * 2048) for i in range(8)]
    hfm = [at(f"hfm{i}", [128, 512], BF16, PB + 16384 + i * 1024) for i in range(8)]
    Abuf = [at(f"Ab{i}", [128, 512], BF16, PB + 16384 + i * 1024) for i in range(8)]
    qfm = [at(f"qfm{i}", [128, 512], BF16, PB + 24576 + i * 1024) for i in range(8)]
    xstage = at("xstage", [128, 1024], F32, PB + 16384)
    wstage = at("wstage", [128, 1024], F32, PB)
    qhat = [at(f"qhat{i}", [128, 512], BF16, PB + i * 1024) for i in range(4)]
    obuf = [at(f"obuf{i}", [128, 512], BF16, PB + 4096 + i * 1024) for i in range(4)]
    qsbuf = at("qsbuf", [128, 512], BF16, PB + 8192)
    pbuf = [at(f"pbuf{i}", [128, 1536], BF16, PB + 9216 + i * 3072)
            for i in range(2)]
    vnegrow = at("vnegrow", [1, VNEG_W], BF16, PB + 15360)
    m_up_sb = at("m_up", [128, 128], BF16, PB + 21760)
    m_lo_sb = at("m_lo", [128, 128], BF16, PB + 22016)
    rotm_sb = at("rotm", [128, 128], BF16, PB + 22272)
    nrow4 = at("nrow4", [1, 512], BF16, PB + 30720)
    nmax = at("nmax", [128, 16], F32, PB + 31744)
    vnegc = at("vnegc", [128, 40], F32, PB + 31808)
    bd2b = at("bd2b", [128, 2], BF16, PB + 31968)
    tablo = at("tablo", [128, TABW], BF16, PB + 22528)
    gated = [at(f"gat{i}", [128, 512], BF16, PB + i * 1024) for i in range(NHC)]
    lnh = [at(f"lnh{i}", [128, 512], BF16, PB + 22528 + i * 1024) for i in range(8)]
    w2r = at("w2r", [128, 1024], BF16, PB + 30720)
    gelbuf = at("gelbuf", [128, 512], BF16, PB + 32768)
    validB = at("validB", [128, 512], BF16, PB + 33792)
    xaring = tmpA
    assert 17408 + PB + 34816 <= 229344, PB

    pb = [nc.alloc_psum_tensor(f"pb{i}", [128, 512], F32) for i in range(8)]

    ops = []
    counts = {"t": 0, "v": 0, "s": 0, "g": 0, "gd": 0, "y": 0}

    def _add(eng, sem, amt, emit, waits):
        prev = counts[sem]
        counts[sem] += amt
        ws = [w for w in waits if w is not None]
        if sem in ("gd", "y") and prev > 0:
            ws.append((sem, prev, "~"))
        ops.append((eng, emit, ws, (sem, counts[sem], amt)))
        return (sem, counts[sem], eng)

    def Tm(emit, waits=()):
        T_TAGS.append(CUR_TAG[0])
        return _add("T", "t", 1, emit, waits)

    def Vm(emit, waits=()):
        return _add("V", "v", 1, emit, waits)

    def Sm(emit, waits=()):
        return _add("S", "s", 1, emit, waits)

    def Gm(emit, waits=()):
        return _add("G", "g", 1, emit, waits)

    def GD(emit, waits=()):
        return _add("G", "gd", 16, emit, waits)

    def Ym(emit, waits=()):
        return _add("Y", "y", 16, emit, waits)

    def barrier():
        return [(s, counts[s], "~") for s in counts if counts[s] > 0]

    bankR = [[] for _ in range(8)]
    bankW = [None] * 8

    def bacq(bs):
        w = []
        for b in bs:
            w += bankR[b]
            bankR[b] = []
        return w

    def bwr(bs, tok):
        for b in bs:
            bankW[b] = tok

    def brd(bs, tok):
        for b in bs:
            bankR[b].append(tok)
        return tok

    tl = {}
    for n, dst in [("validc", vcol), ("ident", identsb),
                   ("bd2", bd2_sb), ("sel2", sel2_sb),
                   ("gam", sgam), ("bqc", bq_sb), ("bkc", bk_sb)]:
        tl[n] = Ym(lambda y, d=dst, n=n: y.dma_start(d[:, :], ap(n)))
    for n, dst in [("validr", vrow), ("tri01", tri01_sb)]:
        tl[n] = GD(lambda g, d=dst, n=n: g.dma_start(d[:, :], ap(n)))
    tl["crow"] = Ym(lambda y: y.dma_start(
        crow_sb[:, :], ap("crow").rearrange("o (j p) -> (o p) j", p=128),
        allow_slow_non_contiguous=True))
    t_ones = Vm(lambda v: v.memset(onesc[:, :], 1.0))
    t_ones128 = Vm(lambda v: v.memset(ones128[:, :], 1.0))
    t_onesb = Vm(lambda v: v.memset(onescb[:, :], 1.0))
    t_eps5 = Vm(lambda v: v.memset(eps5[:, :], 1e-5))
    t_eps24 = Vm(lambda v: v.memset(eps24[:, :], 1e-24))
    t_ones128b = Vm(lambda v: v.memset(ones128b[:, :], 1.0))
    t_iota = Gm(lambda g: g.iota(iotac[:, :], [[128, 4]], channel_multiplier=1,
                                 allow_small_or_imprecise_dtypes=True))
    t_iotar = Gm(lambda g: g.iota(gpow[:, :], [[1, 512]], channel_multiplier=0,
                                  allow_small_or_imprecise_dtypes=True))

    t_lng = Sm(lambda s: s.activation(lng[:, :], sgam[:, :], AF.Ln),
               waits=[tl["gam"]])
    tk_ = Tm(lambda t: t.matmul(pb[5][:, 0:1], onesc[0:1, :], lng[:, :],
                                start=True, stop=True),
             waits=[t_lng, t_ones] + bacq([5]))
    bwr([5], tk_)
    t_lngc = brd([5], Vm(lambda v: v.tensor_copy(lngc[:, :], pb[5][:, 0:1]),
                         waits=[tk_]))
    t1 = Vm(lambda v: v.tensor_scalar(gpow[:, :], gpow[:, :], lng[0:1, 0:1], None,
                                      ALU.mult), waits=[t_iotar, t_lng])
    t_gpow = Sm(lambda s: s.activation(gpow[:, :], gpow[:, :], AF.Exp), waits=[t1])
    t1 = Vm(lambda v: v.tensor_scalar(gnegc[:, :], iotac[:, :], lngc[:, :], -1.0,
                                      ALU.mult, ALU.mult), waits=[t_iota, t_lngc])
    t_gneg = Sm(lambda s: s.activation(gnegc[:, :], gnegc[:, :], AF.Exp), waits=[t1])
    t1 = Vm(lambda v: v.tensor_scalar(tmpA[:, 0:1], lngc[:, :], 512.0, None,
                                      ALU.mult), waits=[t_lngc])
    t2 = Vm(lambda v: v.tensor_scalar(gprevc[:, :], iotac[:, :], lngc[:, :], -1.0,
                                      ALU.mult, ALU.mult), waits=[t_iota, t_lngc])
    t3 = Vm(lambda v: v.tensor_scalar(gprevc[:, :], gprevc[:, :], tmpA[:, 0:1],
                                      None, ALU.add), waits=[t1, t2])
    t_gprev = Sm(lambda s: s.activation(gprevc[:, :], gprevc[:, :], AF.Exp),
                 waits=[t3])
    t_b1 = Ym(lambda y: y.dma_start(tmpA[0:1, 0:512], ap("bvr")[:, 0:512]),
              waits=[t1, t3])
    t_b2 = Ym(lambda y: y.dma_start(tmpB[0:1, 0:512], ap("bvr")[:, 512:1024]))
    tm1 = Tm(lambda t: t.matmul(pb[5][:, 0:512], onesc[0:1, :], tmpA[0:1, 0:512],
                                start=True, stop=True),
             waits=[t_b1, t_ones] + bacq([5]))
    bwr([5], tm1)
    t_bvb0 = brd([5], Vm(lambda v: v.tensor_copy(bvb[0][:, :], pb[5][:, 0:512]),
                         waits=[tm1]))
    tm2 = Tm(lambda t: t.matmul(pb[5][:, 0:512], onesc[0:1, :], tmpB[0:1, 0:512],
                                start=True, stop=True),
             waits=[t_b2] + bacq([5]))
    bwr([5], tm2)
    t_bvb1 = brd([5], Vm(lambda v: v.tensor_copy(bvb[1][:, :], pb[5][:, 0:512]),
                         waits=[tm2]))

    t_silu = Sm(lambda s: s.activation(crow_bf[:, :], crow_sb[:, :], AF.Silu),
                waits=[tl["crow"]])
    prev_ld = None
    mm_last = None
    first_mod = True
    for cg in range(12):
        lo = cg * 512
        lds = []
        for kc in range(8):
            lds.append(GD(lambda g, kc=kc, lo=lo: g.dma_start(
                kring[0][kc][:, :], ap("wmod")[kc * 128:(kc + 1) * 128, lo:lo + 512]),
                waits=[prev_ld]))
        for j in range(4):
            vf = cg * 4 + j
            for kc in range(8):
                w = [lds[kc], t_silu, mm_last]
                if first_mod:
                    w += bacq([4])
                    first_mod = False
                mm_last = Tm(lambda t, kc=kc, j=j, vf=vf: t.matmul(
                    pb[4][:, vf:vf + 1], kring[0][kc][:, j * 128:(j + 1) * 128],
                    crow_bf[:, kc:kc + 1], start=(kc == 0), stop=(kc == 7)),
                    waits=w)
        prev_ld = mm_last
    bwr([4], mm_last)
    t_modc = brd([4], Vm(lambda v: v.tensor_copy(modc[:, :], pb[4][:, 0:48]),
                         waits=[mm_last]))
    t_ca1 = Vm(lambda v: v.tensor_scalar(ca1c[:, :], modc[:, 8:16], 1.0, None,
                                         ALU.add), waits=[t_modc])
    t_cb1 = Vm(lambda v: v.tensor_scalar(cb1c[:, :], modc[:, 32:40], 1.0, None,
                                         ALU.add), waits=[t_modc])

    def gtrans(dst, c0, w0):
        a1 = Sm(lambda s: s.activation(tgat[:, :], modc[:, c0:c0 + 8], AF.Abs),
                waits=[t_modc, w0])
        a2 = Vm(lambda v: v.tensor_scalar(tgat[:, :], tgat[:, :], 0.25, None,
                                          ALU.max), waits=[a1])
        a3 = Sm(lambda s: s.activation(dst[:, :], modc[:, c0:c0 + 8], AF.Sign),
                waits=[a2])
        return Vm(lambda v: v.tensor_mul(dst[:, :], dst[:, :], tgat[:, :]),
                  waits=[a3])
    t_ga = gtrans(gac, 16, None)
    t_gb = gtrans(gbc, 40, t_ga)

    tok_wT = t_modc
    for wname, wdst in [("wq", wqT), ("wk", wkT), ("wv", wvT)]:
        for c in range(8):
            tok_wT = GD(lambda g, wn=wname, c=c, wd=wdst: g.dma_start(
                wd[c][:, :], ap(wn)[c * 128:(c + 1) * 128, :]),
                waits=[tok_wT])

    def ln_stats2(x_src, x_waits, ones_lhs, ones_tok, extra_first=()):
        w5 = bacq([5])
        w4 = bacq([4])
        tsum = None
        for f in range(8):
            w = [x_waits[f], ones_tok, tsum] + \
                (list(extra_first) + w5 + w4 if f == 0 else [])
            tsum = Tm(lambda t, f=f: t.matmul(
                pb[5][0:1, 0:512], ones_lhs[:, :], x_src[f],
                start=(f == 0), stop=(f == 7)), waits=w)
        bwr([5], tsum)
        tsq = None
        for f in range(8):
            s1 = Sm(lambda s, f=f: s.activation(tmpCb[:, :], x_src[f],
                                                AF.Square),
                    waits=[x_waits[f], tsq])
            tsq = Tm(lambda t, f=f: t.matmul(
                pb[4][0:1, 0:512], ones128b[:, :], tmpCb[:, :],
                start=(f == 0), stop=(f == 7)),
                waits=[s1, t_ones128b] + ([tsum] if f == 0 else []))
        bwr([4], tsq)
        r1 = brd([5], Vm(lambda v: v.tensor_scalar(
            rowaux[0:1, :], pb[5][0:1, 0:512], 1.0 / D, None, ALU.mult),
            waits=[tsum, tsq]))
        r2 = brd([4], Vm(lambda v: v.tensor_scalar(
            rows4[0:1, :], pb[4][0:1, 0:512], 1.0 / D, None, ALU.mult),
            waits=[tsq]))
        r3 = brd([4], Vm(lambda v: v.tensor_mul(
            pb[4][0:1, 0:512], rowaux[0:1, :], rowaux[0:1, :]),
            waits=[r1, r2]))
        r4 = brd([4], Vm(lambda v: v.tensor_sub(
            rows4[0:1, :], rows4[0:1, :], pb[4][0:1, 0:512]), waits=[r3]))
        r5 = Sm(lambda s: s.activation(rows4[0:1, :], rows4[0:1, :], AF.Ln,
                                       bias=eps5[0:1, :]), waits=[r4, t_eps5])
        return Sm(lambda s: s.activation(rows4[0:1, :], rows4[0:1, :], AF.Exp,
                                         scale=-0.5), waits=[r5])

    def ln_broadcasts(t_inv, u0, do_valid):
        tiv = t_inv
        if do_valid:
            tiv = Vm(lambda v, u0=u0: v.tensor_mul(
                rows4[0:1, :], rows4[0:1, :], vrow[0:1, u0:u0 + 512]),
                waits=[t_inv, tl["validr"]])
        tb_ = Vm(lambda v: v.tensor_scalar(rowaux[0:1, :], rowaux[0:1, :], -1.0,
                                           None, ALU.mult), waits=[tiv])
        tb_ = Vm(lambda v: v.tensor_mul(rowaux[0:1, :], rowaux[0:1, :],
                                        rows4[0:1, :]), waits=[tb_])
        tB1 = Tm(lambda t: t.matmul(pb[6][:, 0:512], onesc[0:1, :],
                                    rows4[0:1, :], start=True, stop=True),
                 waits=[tiv, t_ones] + bacq([6]))
        bwr([6], tB1)
        tB2 = Tm(lambda t: t.matmul(pb[7][:, 0:512], onesc[0:1, :],
                                    rowaux[0:1, :], start=True, stop=True),
                 waits=[tb_] + bacq([7]))
        bwr([7], tB2)
        tVB = None
        if do_valid:
            tVB = Tm(lambda t, u0=u0: t.matmul(
                pb[4][:, 0:512], onescb[0:1, 0:128], vrow[0:1, u0:u0 + 512],
                start=True, stop=True), waits=[tiv, t_onesb] + bacq([4]))
            bwr([4], tVB)
        return tB1, tB2, tVB

    tok_y0 = [None] * 8
    tok_x_free = None
    tok_kv_free = [None, None]
    tok_k_ready = [[None] * 8, [None] * 8]
    tok_v_ready = [[None] * 4, [None] * 4]
    tok_xa_st = None
    t_abuf_free = None

    for ci in range(10 if NPHASES >= 1 else 0):
        CUR_TAG[0] = 'p1'
        u0 = ci * CH
        par = ci % 2
        pp = 1 - par
        txf = [None] * 8
        for f in range(8):
            txf[f] = Ym(lambda y, u0=u0, f=f: y.dma_start(
                xfm[f][:, 0:512], ap("xh")[f, :, u0:u0 + 512]),
                waits=[tok_x_free, txf[f - 1] if f else None,
                       tok_wT if ci == 0 and f == 0 else None])
        xtoks = list(txf)

        xsrc = [xfm[f][:, 0:512] for f in range(8)]
        t_inv = ln_stats2(xsrc, xtoks, ones128, t_ones128)
        tB1, tB2, _ = ln_broadcasts(t_inv, u0, False)
        th = [None] * 8
        for f in range(8):
            a1 = brd([6], Vm(lambda v, f=f: v.tensor_mul(
                tmpB[:, :], xfm[f][:, :], pb[6][:, 0:512]),
                waits=[tB1, xtoks[f], th[f - 1] if f else None]))
            a2 = brd([7], Vm(lambda v: v.tensor_add(
                tmpB[:, :], tmpB[:, :], pb[7][:, 0:512]), waits=[a1, tB2]))
            th[f] = Vm(lambda v, f=f: v.tensor_scalar(
                hfm[f][:, :], tmpB[:, :], ca1c[:, f:f + 1], modc[:, f:f + 1],
                ALU.mult, ALU.add), waits=[a2, t_ca1, t_modc])

        tq = [None] * 8
        tkk = [None] * 8
        for e in range(8):
            bk_ = e % 2
            mm = None
            for dc in range(8):
                w = [th[dc], mm]
                if dc == 0:
                    w += [tok_wT] + bacq([bk_])
                mm = Tm(lambda t, e=e, dc=dc, bk_=bk_: t.matmul(
                    pb[bk_][:, 0:512], wqT[dc][:, e * 128:(e + 1) * 128],
                    hfm[dc][:, :], start=(dc == 0), stop=(dc == 7)), waits=w)
            bwr([bk_], mm)
            tq[e] = brd([bk_], Sm(lambda s, e=e, bk_=bk_: s.activation(
                qfm[e][:, :], pb[bk_][:, 0:512], AF.Relu,
                bias=bq_sb[:, e:e + 1]), waits=[mm, tl["bqc"]]))
        for e in range(8):
            bk_ = e % 2
            mm = None
            for dc in range(8):
                w = [th[dc], mm]
                if dc == 0:
                    w += bacq([bk_])
                mm = Tm(lambda t, e=e, dc=dc, bk_=bk_: t.matmul(
                    pb[bk_][:, 0:512], wkT[dc][:, e * 128:(e + 1) * 128],
                    hfm[dc][:, :], start=(dc == 0), stop=(dc == 7)), waits=w)
            bwr([bk_], mm)
            tkk[e] = brd([bk_], Sm(lambda s, e=e, bk_=bk_, par=par: s.activation(
                kring[par][e][:, :], pb[bk_][:, 0:512], AF.Relu,
                bias=bk_sb[:, e:e + 1]),
                waits=[mm, tl["bkc"], tok_kv_free[par]]))
            tok_k_ready[par][e] = tkk[e]
        tv = [None] * 4
        for st in range(4):
            for half in range(2):
                bk_ = half
                mm = None
                for dc in range(8):
                    w = [th[dc], mm]
                    if dc == 0:
                        w += bacq([bk_])
                    mm = Tm(lambda t, st=st, half=half, dc=dc, bk_=bk_: t.matmul(
                        pb[bk_][:, 0:512], hfm[dc][:, st * 128:(st + 1) * 128],
                        wvT[dc][:, half * 512:(half + 1) * 512],
                        start=(dc == 0), stop=(dc == 7)), waits=w)
                bwr([bk_], mm)
                tv[st] = brd([bk_], Vm(lambda v, st=st, half=half, bk_=bk_,
                                       par=par: v.tensor_add(
                    vring[par][st][:, half * 512:(half + 1) * 512],
                    pb[bk_][:, 0:512], bvb[half][:, :]),
                    waits=[mm, tok_kv_free[par], t_bvb0, t_bvb1, tv[st]]))
            tok_v_ready[par][st] = tv[st]

        if ci == 0:
            tok_x_free = th[7]
            t_abuf_free = tv[3]
            continue

        tgm = [None] * 8
        for jp in range(8):
            jt = jp % 4
            uj = (u0 - CH if jp < 4 else u0) + jt * 128
            gcol = gprevc if jp < 4 else gnegc
            tgm[jp] = Vm(lambda v, jp=jp, jt=jt, uj=uj, gcol=gcol: v.tensor_mul(
                gmk[jp], gcol[:, jt:jt + 1],
                vcol[:, uj // 128:uj // 128 + 1]),
                waits=[t_gneg, t_gprev, tl["validc"]])
        tA = [None] * 8
        for jp in range(8):
            sp = pp if jp < 4 else par
            jt = jp % 4
            bk_ = 2 + jp % 2
            mm = None
            for ec in range(8):
                w = [tq[ec], tok_k_ready[sp][ec], mm]
                if ec == 0:
                    w += bacq([bk_])
                mm = Tm(lambda t, jt=jt, ec=ec, bk_=bk_, sp=sp: t.matmul(
                    pb[bk_][:, 0:512], kring[sp][ec][:, jt * 128:(jt + 1) * 128],
                    qfm[ec][:, :], start=(ec == 0), stop=(ec == 7)), waits=w)
            bwr([bk_], mm)
            ilo = 0 if jp < 4 else jt * 128
            ev = brd([bk_], Vm(lambda v, jp=jp, bk_=bk_: v.tensor_scalar(
                Abuf[jp][:, 0:512], pb[bk_][:, 0:512], gmk[jp], None,
                ALU.mult), waits=[mm, tgm[jp]]))
            if jp >= 4:
                ev = Vm(lambda v, jp=jp, jt=jt: v.tensor_mul(
                    Abuf[jp][:, jt * 128:(jt + 1) * 128],
                    Abuf[jp][:, jt * 128:(jt + 1) * 128], tri01_sb[:, :]),
                    waits=[ev, tl["tri01"]])
            tA[jp] = ev
        tden = None
        w0 = bacq([4])
        for jp in range(8):
            ilo = 0 if jp < 4 else (jp % 4) * 128
            tden = Tm(lambda t, jp=jp, ilo=ilo: t.matmul(
                pb[4][0:1, ilo:512], ones128b[:, :], Abuf[jp][:, ilo:512],
                start=(jp == 0), stop=(jp == 7)),
                waits=[tA[jp], tden, t_ones128b] + (w0 if jp == 0 else []))
        bwr([4], tden)
        rr = brd([4], Vm(lambda v: v.tensor_mul(
            denrow[0:1, :], pb[4][0:1, 0:512], gpow[:, :]),
            waits=[tden, t_gpow]))
        rr = Vm(lambda v: v.tensor_scalar(denrow[0:1, :], denrow[0:1, :],
                                          EPS_RET, None, ALU.add), waits=[rr])
        rr = Vm(lambda v: v.reciprocal(denrow[0:1, :], denrow[0:1, :]), waits=[rr])
        rr = Vm(lambda v: v.tensor_mul(denrow[0:1, :], denrow[0:1, :],
                                       gpow[:, :]), waits=[rr])
        rbt = Tm(lambda t: t.matmul(pb[4][:, 0:512], onesc[0:1, :], denrow[0:1, :],
                                    start=True, stop=True),
                 waits=[rr] + bacq([4]))
        bwr([4], rbt)
        tAr = [None] * 8
        for jp in range(8):
            ilo = 0 if jp < 4 else (jp % 4) * 128
            tAr[jp] = brd([4], Vm(lambda v, jp=jp, ilo=ilo: v.tensor_mul(
                Abuf[jp][:, ilo:512], Abuf[jp][:, ilo:512], pb[4][:, ilo:512]),
                waits=[rbt, tA[jp]]))
        t_xa = [None] * 8
        for dv in range(8):
            bk_ = 2 + dv % 2
            mm = None
            for jp in range(8):
                sp = pp if jp < 4 else par
                jt = jp % 4
                ilo = 0 if jp < 4 else jt * 128
                w = [tAr[jp], tok_v_ready[sp][jt], mm]
                if jp == 0:
                    w += bacq([bk_])
                mm = Tm(lambda t, dv=dv, jt=jt, ilo=ilo, bk_=bk_, sp=sp, jp=jp:
                        t.matmul(
                    pb[bk_][:, ilo:512],
                    vring[sp][jt][:, dv * 128:(dv + 1) * 128],
                    Abuf[jp][:, ilo:512], start=(jp == 0), stop=(jp == 7)),
                    waits=w)
            bwr([bk_], mm)
            t_xa[dv] = brd([bk_], Vm(lambda v, dv=dv, bk_=bk_:
                                     v.scalar_tensor_tensor(
                xfm[dv][:, :], pb[bk_][:, 0:512], gac[:, dv:dv + 1],
                xfm[dv][:, :], ALU.mult, ALU.add),
                waits=[mm, t_ga, xtoks[dv]]))
        tok_kv_free[pp] = t_xa[7]
        t_abuf_free = t_xa[7]

        if u0 >= L2F_LO:
            for f in range(8):
                tok_xa_st = Ym(lambda y, f=f, u0=u0: y.dma_start(
                    xa_d.ap()[f, :, u0 - L2F_LO:u0 - L2F_LO + 512],
                    xfm[f][:, 0:512]), waits=[t_xa[f], tok_xa_st])

        t_inv2 = ln_stats2(xsrc, t_xa, ones128, t_ones128)
        do_v = u0 < L2F_LO
        tB1b, tB2b, tVBb = ln_broadcasts(t_inv2, u0, do_v)
        lo_w = 504 if ci == 1 else 0
        for f in range(8):
            a1 = brd([6], Vm(lambda v, f=f, lo_w=lo_w: v.tensor_mul(
                tmpB[:, lo_w:512], xfm[f][:, lo_w:512], pb[6][:, lo_w:512]),
                waits=[tB1b, t_xa[f], tok_y0[f - 1] if f else None]))
            a2 = brd([7], Vm(lambda v, lo_w=lo_w: v.tensor_add(
                tmpB[:, lo_w:512], tmpB[:, lo_w:512], pb[7][:, lo_w:512]),
                waits=[a1, tB2b]))
            a3 = Vm(lambda v, f=f, lo_w=lo_w: v.tensor_scalar(
                tmpB[:, lo_w:512], tmpB[:, lo_w:512], cb1c[:, f:f + 1], None,
                ALU.mult), waits=[a2, t_cb1])
            dst_c = u0 - AB + lo_w
            if do_v:
                tok_y0[f] = brd([4], Vm(lambda v, f=f, lo_w=lo_w, dst_c=dst_c:
                                        v.scalar_tensor_tensor(
                    arena[f][:, dst_c:dst_c + 512 - lo_w], pb[4][:, lo_w:512],
                    modc[:, 24 + f:24 + f + 1], tmpB[:, lo_w:512],
                    ALU.mult, ALU.add), waits=[a3, tVBb, t_modc]))
            else:
                tok_y0[f] = Vm(lambda v, f=f, dst_c=dst_c: v.tensor_scalar(
                    arena[f][:, dst_c:dst_c + 512], tmpB[:, 0:512],
                    modc[:, 24 + f:24 + f + 1], None, ALU.add),
                    waits=[a3, t_modc])
        tok_x_free = tok_y0[7]

    tok_p1 = barrier()

    def attn_phase(l, lo_keys, lo_q, in_base, out_base, prevb):
        CUR_TAG[0] = f'a{l}-pre'
        tw = {}
        for i in range(4):
            tw[("tab", i)] = GD(lambda g, i=i: g.dma_start(
                tabs[i][:, :], ap("tabs")[i, :, :]), waits=prevb)
        tadd = [None] * 4
        for i in range(4):
            tlo = GD(lambda g, i=i: g.dma_start(
                tablo[:, :], ap("tabs")[4 + i, :, :]),
                waits=prevb + [tadd[i - 1] if i else None])
            tadd[i] = Vm(lambda v, i=i: v.tensor_add(
                tabs[i][:, :], tabs[i][:, :], tablo[:, :]),
                waits=[tw[("tab", i)], tlo])
        for i in range(8):
            tw[("wqkv", i)] = GD(lambda g, i=i, l=l: g.dma_start(
                wqkv_sb[i][:, :], ap("wqkv")[l, i * 128:(i + 1) * 128, :]),
                waits=prevb)
        for i in range(4):
            tw[("wo", i)] = GD(lambda g, i=i, l=l: g.dma_start(
                wo_sb[i][:, :], ap("wo")[l, i * 128:(i + 1) * 128, :]),
                waits=prevb)
        for nm, dd in [("rotm", rotm_sb), ("m_up", m_up_sb), ("m_lo", m_lo_sb)]:
            tw[nm] = GD(lambda g, nm=nm, dd=dd: g.dma_start(dd[:, :], ap(nm)),
                        waits=prevb)
        for nm, dd in [("qsc", qs8_sb), ("qspc", qs8p_sb), ("ksc", ks_sb),
                       ("kspc", ksp_sb)]:
            tw[nm] = Ym(lambda y, nm=nm, dd=dd, l=l: y.dma_start(
                dd[:, :], ap(nm)[l]), waits=prevb)
        t_vneg = Vm(lambda v: v.tensor_scalar(
            vnegrow[:, :], vrow[0:1, 0:VNEG_W], -1.0, BIG, ALU.add, ALU.mult),
            waits=[tl["validr"]] + prevb)
        t_vnegc = Vm(lambda v: v.tensor_scalar(
            vnegc[:, :], vcol[:, :], -1.0, BIG, ALU.add, ALU.mult),
            waits=[tl["validc"]] + prevb)
        tw["bd2b"] = GD(lambda g: g.dma_start(bd2b[:, :], ap("bd2")),
                        waits=prevb)
        t8a = Vm(lambda v: v.tensor_scalar(qs8_sb[:, :], qs8_sb[:, :], QK_SCALE,
                                           None, ALU.mult), waits=[tw["qsc"]])
        t8b = Vm(lambda v: v.tensor_scalar(qs8p_sb[:, :], qs8p_sb[:, :],
                                           QK_SCALE, None, ALU.mult),
                 waits=[tw["qspc"]])
        ttab = [None] * 4
        for i, ctok in enumerate([t8a, t8b, tw["ksc"], tw["kspc"]]):
            col = [qs8_sb, qs8p_sb, ks_sb, ksp_sb][i]
            ttab[i] = Vm(lambda v, i=i, col=col: v.tensor_scalar(
                tabs[i][:, :], tabs[i][:, :], col[:, :], None, ALU.mult),
                waits=[tadd[i], ctok])

        tok_out = [None] * 8
        tok_kr = [[None] * 4, [None] * 4]
        tok_vr = [[None] * 4, [None] * 4]
        tok_kfree = [None, None]
        t_qhat = [None] * 4
        t_sB = None

        def h1s(dc, c0, w, u):
            off = (u - in_base) - (1 if dc >= 4 else 0) + c0
            return arena[dc][:, off:off + w]

        def stage_a(sb_u):
            CUR_TAG[0] = f'a{l}-sa'
            ps = (sb_u // 512) % 2
            tq_l = {}
            prev_x3 = [None]
            prev_sq = [None]
            prev_final = [None]
            for grp in range(2):
                for ept in range(4):
                    e = grp * 4 + ept
                    bk_ = e % 2
                    mm = None
                    for dc in range(8):
                        w = [tw[("wqkv", dc)], mm]
                        if dc == 0:
                            w += bacq([bk_]) + [t_sB]
                        mm = Tm(lambda t, e=e, dc=dc, bk_=bk_, sb_u=sb_u:
                                t.matmul(
                            pb[bk_][:, 0:512],
                            wqkv_sb[dc][:, e * 128:(e + 1) * 128],
                            h1s(dc, 0, 512, sb_u),
                            start=(dc == 0), stop=(dc == 7)), waits=w)
                    bwr([bk_], mm)
                    c1 = brd([bk_], Sm(lambda s, bk_=bk_: s.activation(
                        qsbuf[:, :], pb[bk_][:, 0:512], AF.Copy),
                        waits=[mm, prev_sq[0]]))
                    rmm = Tm(lambda t: t.matmul(
                        pb[4][:, 0:512], rotm_sb[:, :], qsbuf[:, :],
                        start=True, stop=True),
                        waits=[c1, tw["rotm"]] + bacq([4]))
                    bwr([4], rmm)
                    sq = Gm(lambda g: g.tensor_mul(tmpCb[:, :], qsbuf[:, :],
                                                   qsbuf[:, :]), waits=[c1])
                    prev_sq[0] = sq
                    ssm = Tm(lambda t: t.matmul(
                        pb[5][0:2, 0:512], bd2b[:, :], tmpCb[:, :],
                        start=True, stop=True),
                        waits=[sq, tw["bd2b"]] + bacq([5]))
                    bwr([5], ssm)
                    iv = brd([5], Sm(lambda s: s.activation(
                        rows4[0:2, :], pb[5][0:2, 0:512], AF.Ln,
                        bias=eps24[:, :]), waits=[ssm, t_eps24]))
                    iv = Sm(lambda s: s.activation(rows4[0:2, :], rows4[0:2, :],
                                                   AF.Exp, scale=-0.5),
                            waits=[iv])
                    ivB = Tm(lambda t: t.matmul(
                        pb[5][:, 0:512], sel2_sb[:, :], rows4[0:2, :],
                        start=True, stop=True),
                        waits=[iv, tl["sel2"]] + bacq([5]))
                    bwr([5], ivB)
                    ct, st_ = 2 * grp, 2 * grp + 1
                    tc0 = sb_u - TAB_LO
                    x1 = brd([bk_], Vm(lambda v, bk_=bk_, ct=ct, tc0=tc0:
                                       v.tensor_mul(
                        tmpA[:, :], pb[bk_][:, 0:512],
                        tabs[ct][:, tc0:tc0 + 512]),
                        waits=[mm, ttab[ct], c1, prev_final[0], prev_x3[0]]))
                    x2 = brd([4], Vm(lambda v, st_=st_, tc0=tc0: v.tensor_mul(
                        tmpB[:, :], pb[4][:, 0:512],
                        tabs[st_][:, tc0:tc0 + 512]),
                        waits=[rmm, ttab[st_], prev_x3[0]]))
                    x3 = Gm(lambda g: g.tensor_add(tmpA[:, :], tmpA[:, :],
                                                   tmpB[:, :]), waits=[x1, x2])
                    prev_x3[0] = x3
                    if grp == 0:
                        dst, wfree = qhat[ept], t_sB
                    else:
                        dst, wfree = khat[ps][ept], tok_kfree[ps]
                    tqv = brd([5], Vm(lambda v, dst=dst: v.tensor_mul(
                        dst[:, :], tmpA[:, :], pb[5][:, 0:512]),
                        waits=[x3, ivB, wfree]))
                    tq_l[e] = tqv
                    prev_final[0] = tqv
                    if grp == 0:
                        t_qhat[ept] = tqv
                    else:
                        tok_kr[ps][ept] = tqv
            tvv = None
            for st in range(4):
                bk_ = st % 2
                mm = None
                for dc in range(8):
                    w = [tw[("wqkv", dc)], mm]
                    if dc == 0:
                        w += bacq([bk_]) + [tq_l[7], tvv]
                    mm = Tm(lambda t, st=st, dc=dc, bk_=bk_, sb_u=sb_u: t.matmul(
                        pb[bk_][:, 0:512], h1s(dc, st * 128, 128, sb_u),
                        wqkv_sb[dc][:, 1024:1536],
                        start=(dc == 0), stop=(dc == 7)), waits=w)
                bwr([bk_], mm)
                tvv = brd([bk_], Vm(lambda v, st=st, bk_=bk_, ps=ps:
                                    v.tensor_copy(
                    vtm[ps][st][:, :], pb[bk_][:, 0:512]),
                    waits=[mm, tok_kfree[ps]]))
                tok_vr[ps][st] = tvv
            return tvv

        def stage_b2(qp, t_prev_wo):
            CUR_TAG[0] = f'a{l}-sb'
            qc2 = qp % 512
            tev = [None] * 8
            tav = [None] * 8
            toev = t_prev_wo

            def pieces_for(qt):
                out = []
                j = 0
                while j < 640:
                    kabs = qt - 512 + j
                    slot = (kabs // 512) % 2
                    kc = kabs % 512
                    w = min(512 - kc, 640 - j)
                    if j < 512 and j + w > 512:
                        w = 512 - j
                    if kabs < 3072 < kabs + w:
                        w = 3072 - kabs
                    out.append((0 if j < 512 else 1, j if j < 512 else j - 512,
                                w, slot, kc, kabs))
                    j += w
                return out

            for h in range(HEADS):
                ept, half = h // 2, h % 2
                r0 = 64 * half
                for ti in range(2):
                    qt = qp + 128 * ti
                    qc = qt % 512
                    w01 = bacq([0, 1])
                    mm = None
                    for (bk_, c0, w, slot, kc, kabs) in pieces_for(qt):
                        mm = Tm(lambda t, ept=ept, r0=r0, bk_=bk_, c0=c0, w=w,
                                slot=slot, kc=kc, qc=qc, kabs=kabs: t.matmul(
                            pb[bk_][:, c0:c0 + w],
                            qhat[ept][r0:r0 + 64, qc:qc + 128],
                            khat[slot][ept][r0:r0 + 64, kc:kc + w],
                            start=True, stop=(kabs + w > 3072)),
                            waits=[tok_kr[slot][ept], t_qhat[ept], mm] +
                                  (w01 if mm is None else []))
                        if kabs + w <= 3072:
                            mm = Tm(lambda t, bk_=bk_, c0=c0, w=w, kabs=kabs:
                                    t.matmul(
                                pb[bk_][:, c0:c0 + w], onescb[0:1, 0:128],
                                vnegrow[0:1, kabs:kabs + w],
                                start=False, stop=True),
                                waits=[t_vneg, mm, t_onesb])
                    bwr([0], mm)
                    bwr([1], mm)
                    mq1 = brd([0], Vm(lambda v: v.tensor_add(
                        pb[0][:, 0:128], pb[0][:, 0:128], m_lo_sb[:, :]),
                        waits=[mm, tw["m_lo"]]))
                    mq2 = brd([1], Vm(lambda v: v.tensor_add(
                        pb[1][:, 0:128], pb[1][:, 0:128], m_up_sb[:, :]),
                        waits=[mm, tw["m_up"], mq1]))
                    nc_ = 2 * h + ti
                    n1 = brd([0], Vm(lambda v, nc_=nc_: v.tensor_reduce(
                        nmax[:, nc_:nc_ + 1], pb[0][:, 0:512], AX.X, ALU.max,
                        negate=True), waits=[mq1, mq2]))
                    n2 = brd([1], Vm(lambda v: v.tensor_reduce(
                        tmpA[:, 0:1], pb[1][:, 0:128], AX.X, ALU.max,
                        negate=True), waits=[mq2, n1, prev_nm[0]]))
                    prev_nm[0] = Vm(lambda v, nc_=nc_: v.tensor_scalar(
                        nmax[:, nc_:nc_ + 1], nmax[:, nc_:nc_ + 1],
                        tmpA[:, 0:1], None, ALU.min), waits=[n2])
                    if qt < L2F_LO:
                        prev_nm[0] = Vm(lambda v, nc_=nc_, qt=qt:
                                        v.tensor_mul(
                            nmax[:, nc_:nc_ + 1], nmax[:, nc_:nc_ + 1],
                            vcol[:, qt // 128:qt // 128 + 1]),
                            waits=[prev_nm[0], tl["validc"]])
                ttrA = Tm(lambda t, h=h: t.transpose(
                    pb[1][0:1, 0:128], nmax[:, 2 * h:2 * h + 1],
                    identsb[:, :]),
                    waits=[prev_nm[0], tl["ident"]] + bacq([1]))
                ttrB = Tm(lambda t, h=h: t.transpose(
                    pb[1][0:1, 128:256], nmax[:, 2 * h + 1:2 * h + 2],
                    identsb[:, :]), waits=[prev_nm[0], ttrA])
                bwr([1], ttrB)
                trw = brd([1], Vm(lambda v: v.tensor_copy(
                    nrow4[0:1, 0:256], pb[1][0:1, 0:256]),
                    waits=[ttrA, ttrB, prev_bc[0]]))
                trw = Vm(lambda v: v.tensor_copy(
                    nrow4[0:1, 256:512], nrow4[0:1, 0:256]), waits=[trw])
                pbh = pbuf[h % 2]
                smm = [None] * 3
                wst = bacq([5, 6, 7])
                for b in range(6):
                    kb = qp - 512 + b * 128
                    slot = (kb // 512) % 2
                    kc = kb % 512
                    bs = 5 + b // 2
                    c0 = 256 * (b % 2)
                    smm[b // 2] = Tm(lambda t, ept=ept, r0=r0, slot=slot,
                                     kc=kc, bs=bs, c0=c0, qc2=qc2, b=b:
                                     t.matmul(
                        pb[bs][:, c0:c0 + 256],
                        khat[slot][ept][r0:r0 + 64, kc:kc + 128],
                        qhat[ept][r0:r0 + 64, qc2:qc2 + 256],
                        start=(b % 2 == 0), stop=False, skip_group_check=True),
                        waits=[tok_kr[slot][ept], t_qhat[ept], smm[b // 2]] +
                              (wst if b == 0 else []))
                tbcm = [None] * 3
                for g2 in range(3):
                    tbcm[g2] = Tm(lambda t, g2=g2: t.matmul(
                        pb[5 + g2][:, 0:512], onescb[0:1, 0:128],
                        nrow4[0:1, 0:512], start=False, stop=True,
                        skip_group_check=True),
                        waits=[trw, smm[g2], t_onesb,
                               tbcm[g2 - 1] if g2 else None])
                    bwr([5 + g2], tbcm[g2])
                prev_bc[0] = tbcm[2]
                mk = brd([5], Vm(lambda v: v.tensor_add(
                    pb[5][:, 0:128], pb[5][:, 0:128], m_up_sb[:, :]),
                    waits=[tbcm[0], tw["m_up"]]))
                mk = brd([5], Vm(lambda v: v.tensor_scalar(
                    pb[5][:, 128:256], pb[5][:, 128:256], -BIG, None,
                    ALU.add), waits=[mk]))
                mk = brd([5], Vm(lambda v: v.tensor_add(
                    pb[5][:, 384:512], pb[5][:, 384:512], m_up_sb[:, :]),
                    waits=[mk]))
                mk2 = brd([7], Vm(lambda v: v.tensor_add(
                    pb[7][:, 0:128], pb[7][:, 0:128], m_lo_sb[:, :]),
                    waits=[tbcm[2], tw["m_lo"], mk]))
                mk2 = brd([7], Vm(lambda v: v.tensor_scalar(
                    pb[7][:, 256:384], pb[7][:, 256:384], -BIG, None,
                    ALU.add), waits=[mk2]))
                mk2 = brd([7], Vm(lambda v: v.tensor_add(
                    pb[7][:, 384:512], pb[7][:, 384:512], m_lo_sb[:, :]),
                    waits=[mk2]))
                we = [tav[h - 2] if h >= 2 else None]
                bank_ready = [mk, mk, tbcm[1], tbcm[1], mk2, mk2]
                eb = [None] * 6
                ep = None
                for b in range(6):
                    kb = qp - 512 + b * 128
                    bs = 5 + b // 2
                    c0 = 256 * (b % 2)
                    eb[b] = brd([bs], Sm(lambda s, pbh=pbh, b=b, bs=bs, c0=c0,
                                         kb=kb: s.activation(
                        pbh[:, b * 256:(b + 1) * 256], pb[bs][:, c0:c0 + 256],
                        AF.Exp, bias=vnegc[:, kb // 128:kb // 128 + 1]),
                        waits=[bank_ready[b], t_vnegc, ep] + we))
                    ep = eb[b]
                tev[h] = eb[5]
                wd = bacq([4])
                tdn = None
                for b in range(6):
                    tdn = Tm(lambda t, b=b, pbh=pbh: t.matmul(
                        pb[4][0:1, 0:256], ones128b[:, :],
                        pbh[:, b * 256:(b + 1) * 256],
                        start=(b == 0), stop=(b == 5)),
                        waits=[eb[b], tdn, t_ones128b] +
                              (wd if b == 0 else []))
                bwr([4], tdn)
                rc = brd([4], Vm(lambda v: v.tensor_scalar(
                    denrow[0:1, 0:256], pb[4][0:1, 0:256], 1e-20, None,
                    ALU.add), waits=[tdn, prev_tbc[0]]))
                rc = Vm(lambda v: v.reciprocal(
                    denrow[0:1, 0:256], denrow[0:1, 0:256]), waits=[rc])
                if qp < L2F_LO:
                    rc = Vm(lambda v, qp=qp: v.tensor_mul(
                        denrow[0:1, 0:256], denrow[0:1, 0:256],
                        vrow[0:1, qp:qp + 256]), waits=[rc, tl["validr"]])
                avb = 2 + (h % 2)
                avc = 256 * ((h // 2) % 2)
                tp = (0, 64) if half else None
                wa = bacq([avb])
                amm = None
                for b in range(6):
                    kb = qp - 512 + b * 128
                    slot = (kb // 512) % 2
                    stt_ = (kb % 512) // 128
                    amm = Tm(lambda t, slot=slot, stt_=stt_, h=h, avb=avb,
                             avc=avc, tp=tp, pbh=pbh, b=b, r0=r0: t.matmul(
                        pb[avb][r0:r0 + 64, avc:avc + 256],
                        vtm[slot][stt_][:, h * 64:h * 64 + 64],
                        pbh[:, b * 256:(b + 1) * 256],
                        start=(b == 0), stop=(b == 5), tile_position=tp),
                        waits=[tok_vr[slot][stt_], eb[b], amm] +
                              (wa if b == 0 else []))
                tav[h] = amm
                bwr([avb], amm)
                tbc = Tm(lambda t: t.matmul(
                    pb[4][:, 256:512], onesc[0:1, :], denrow[0:1, 0:256],
                    start=True, stop=True), waits=[rc, tdn, amm])
                bwr([4], tbc)
                prev_tbc[0] = tbc
                toev = brd([avb], Vm(lambda v, ept=ept, r0=r0, avb=avb,
                                     avc=avc, qc2=qc2: v.tensor_copy(
                    obuf[ept][r0:r0 + 64, qc2:qc2 + 256],
                    pb[avb][r0:r0 + 64, avc:avc + 256]),
                    waits=[amm, toev]))
                toev = brd([4], Vm(lambda v, ept=ept, r0=r0, qc2=qc2:
                                   v.tensor_mul(
                    obuf[ept][r0:r0 + 64, qc2:qc2 + 256],
                    obuf[ept][r0:r0 + 64, qc2:qc2 + 256],
                    pb[4][r0:r0 + 64, 256:512]), waits=[tbc, toev]))
            return toev

        prev_nm = [None]
        prev_bc = [None]
        prev_tbc = [None]

        for sb_u in range(lo_keys, T, 512):
            ta = stage_a(sb_u)
            if sb_u < lo_q or SUBB == 0:
                t_sB = ta
                continue
            tos = None
            for k2 in range(2):
                tos = stage_b2(sb_u + k2 * 256,
                               tok_out[7] if k2 == 0 else None)
            t_sB = tos
            if SUBB < 3 or SUBB == 23:
                continue
            CUR_TAG[0] = f'a{l}-wo'
            for f in range(8):
                bk_ = 6 + f % 2
                mm = None
                for ic in range(4):
                    w = [tos, tw[("wo", ic)], mm]
                    if ic == 0:
                        w += bacq([bk_])
                    mm = Tm(lambda t, f=f, ic=ic, bk_=bk_: t.matmul(
                        pb[bk_][:, 0:512], wo_sb[ic][:, f * 128:(f + 1) * 128],
                        obuf[ic][:, 0:512], start=(ic == 0), stop=(ic == 3)),
                        waits=w)
                bwr([bk_], mm)
                oc = sb_u - out_base
                icc = sb_u - in_base
                tok_out[f] = brd([bk_], Vm(lambda v, f=f, oc=oc, icc=icc,
                                           bk_=bk_: v.tensor_add(
                    arena[f][:, oc:oc + 512], arena[f][:, icc:icc + 512],
                    pb[bk_][:, 0:512]), waits=[mm, tok_out[f]]))
            ps = (sb_u // 512) % 2
            tok_kfree[1 - ps] = tos
        return barrier()

    bar = tok_p1
    if NPHASES >= 2:
        bar = attn_phase(0, 1024, L1A_LO, AB, 1536, tok_p1)

    def ffn_phase(l, lo, in_base, out_base, prevb, final):
        tw1 = []
        for i in range(8):
            tw1.append(GD(lambda g, i=i, l=l: g.dma_start(
                w1_sb[i][:, :], ap("w1")[l, i * 128:(i + 1) * 128, :]),
                waits=prevb))
        tb = {}
        for n, dd in [("b1a", b1a_sb), ("b1g", b1g_sb), ("b2c", b2_sb),
                      ("lgc", lg_sb), ("lbc", lb_sb)]:
            tb[n] = Ym(lambda y, n=n, dd=dd, l=l: y.dma_start(
                dd[:, :], ap(n)[l]), waits=prevb)
        tz = Vm(lambda v: v.memset(gated[NHC - 1][:, :], 0.0), waits=prevb)

        blocks = []
        u = lo
        while u < T:
            w = min(512, T - u)
            blocks.append((u, w))
            u += w

        tok_y = [None] * 8
        tok_w2 = None
        tok_xa = None
        t_pass2_last = None
        def xs(dc, c0, w, u0):
            off = (u0 - in_base) - (1 if dc >= 4 else 0) + c0
            return arena[dc][:, off:off + w]

        for (u0, W) in blocks:
            CUR_TAG[0] = f'f{l}-st'
            w5 = bacq([5])
            w4f = bacq([4])
            tsum = None
            for f in range(8):
                tsum = Tm(lambda t, f=f, W=W, u0=u0: t.matmul(
                    pb[5][0:1, 0:W], ones128b[:, :], xs(f, 0, W, u0),
                    start=(f == 0), stop=(f == 7)),
                    waits=[t_ones128b, tsum, tok_y[7]] +
                          (w5 + w4f + (prevb if u0 == lo else [])
                           if f == 0 else []))
            bwr([5], tsum)
            tsq = None
            for f in range(8):
                s1 = Sm(lambda s, f=f, W=W, u0=u0: s.activation(
                    tmpCb[:, 0:W], xs(f, 0, W, u0), AF.Square),
                    waits=[tsq, tok_y[7]] +
                          (prevb if u0 == lo and f == 0 else []))
                tsq = Tm(lambda t, f=f, W=W: t.matmul(
                    pb[4][0:1, 0:W], ones128b[:, :], tmpCb[:, 0:W],
                    start=(f == 0), stop=(f == 7)),
                    waits=[s1, t_ones128b] + ([tsum] if f == 0 else []))
            bwr([4], tsq)
            r1 = brd([5], Vm(lambda v, W=W: v.tensor_scalar(
                rowaux[0:1, 0:W], pb[5][0:1, 0:W], 1.0 / D, None, ALU.mult),
                waits=[tsum, tsq]))
            r2 = brd([4], Vm(lambda v, W=W: v.tensor_scalar(
                rows4[0:1, 0:W], pb[4][0:1, 0:W], 1.0 / D, None, ALU.mult),
                waits=[tsq]))
            r3 = brd([4], Vm(lambda v, W=W: v.tensor_mul(
                pb[4][0:1, 0:W], rowaux[0:1, 0:W], rowaux[0:1, 0:W]),
                waits=[r1, r2]))
            r4 = brd([4], Vm(lambda v, W=W: v.tensor_sub(
                rows4[0:1, 0:W], rows4[0:1, 0:W], pb[4][0:1, 0:W]), waits=[r3]))
            r5 = Sm(lambda s, W=W: s.activation(rows4[0:1, 0:W], rows4[0:1, 0:W],
                                                AF.Ln, bias=eps5[0:1, :]),
                    waits=[r4, t_eps5])
            r5 = Sm(lambda s, W=W: s.activation(rows4[0:1, 0:W], rows4[0:1, 0:W],
                                                AF.Exp, scale=-0.5), waits=[r5])
            rb = Vm(lambda v, W=W: v.tensor_scalar(
                rowaux[0:1, 0:W], rowaux[0:1, 0:W], -1.0, None, ALU.mult),
                waits=[r5])
            rb = Vm(lambda v, W=W: v.tensor_mul(rowaux[0:1, 0:W],
                                                rowaux[0:1, 0:W],
                                                rows4[0:1, 0:W]), waits=[rb])
            tB1 = Tm(lambda t, W=W: t.matmul(
                pb[6][:, 0:W], onesc[0:1, :], rows4[0:1, 0:W],
                start=True, stop=True), waits=[r5, t_ones] + bacq([6]))
            bwr([6], tB1)
            tB2 = Tm(lambda t, W=W: t.matmul(
                pb[7][:, 0:W], onesc[0:1, :], rowaux[0:1, 0:W],
                start=True, stop=True), waits=[rb] + bacq([7]))
            bwr([7], tB2)
            tlnh = [None] * 8
            for f in range(8):
                a1 = brd([6], Vm(lambda v, f=f, W=W, u0=u0: v.tensor_mul(
                    tmpB[:, 0:W], xs(f, 0, W, u0), pb[6][:, 0:W]),
                    waits=[tB1, tlnh[f - 1] if f else None]))
                a2 = brd([7], Vm(lambda v, W=W: v.tensor_add(
                    tmpB[:, 0:W], tmpB[:, 0:W], pb[7][:, 0:W]),
                    waits=[a1, tB2]))
                tlnh[f] = Vm(lambda v, f=f, W=W: v.tensor_scalar(
                    lnh[f][:, 0:W], tmpB[:, 0:W], lg_sb[:, f:f + 1],
                    lb_sb[:, f:f + 1], ALU.mult, ALU.add),
                    waits=[a2, tb["lgc"], tb["lbc"], t_pass2_last])
            tvB = None
            if (not final) and u0 < L2F_LO:
                tvBm = Tm(lambda t, u0=u0, W=W: t.matmul(
                    pb[4][:, 0:W], onescb[0:1, 0:128], vrow[0:1, u0:u0 + W],
                    start=True, stop=True),
                    waits=[tlnh[7], tl["validr"], t_onesb] + bacq([4]))
                bwr([4], tvBm)
                tvB = brd([4], Vm(lambda v, W=W: v.tensor_copy(
                    validB[:, 0:W], pb[4][:, 0:W]), waits=[tvBm]))
            CUR_TAG[0] = f'f{l}-w1'
            tgd = [None] * NHC
            for c in range(NHC):
                aw = min(128, HID - c * 128)
                pa, pg = [(0, 1), (2, 3), (6, 7)][c % 3]
                mma = None
                wA = bacq([pa])
                for dc in range(8):
                    mma = Tm(lambda t, c=c, dc=dc, aw=aw, W=W, pa=pa: t.matmul(
                        pb[pa][0:aw, 0:W], w1_sb[dc][:, c * 128:c * 128 + aw],
                        lnh[dc][:, 0:W], start=(dc == 0), stop=(dc == 7)),
                        waits=[tlnh[dc], tw1[dc], mma] +
                              (wA if dc == 0 else []))
                bwr([pa], mma)
                mmg = None
                wG = bacq([pg])
                for dc in range(8):
                    mmg = Tm(lambda t, c=c, dc=dc, aw=aw, W=W, pg=pg: t.matmul(
                        pb[pg][0:aw, 0:W],
                        w1_sb[dc][:, HID + c * 128:HID + c * 128 + aw],
                        lnh[dc][:, 0:W], start=(dc == 0), stop=(dc == 7)),
                        waits=[tlnh[dc], mmg] + (wG if dc == 0 else []))
                bwr([pg], mmg)
                ge = brd([pg], Sm(lambda s, c=c, aw=aw, W=W, pg=pg: s.activation(
                    gated[c][0:aw, 0:W], pb[pg][0:aw, 0:W], AF.Gelu,
                    bias=b1g_sb[0:aw, c:c + 1]),
                    waits=[mmg, tb["b1g"], tz]))
                tgd[c] = brd([pa], Vm(lambda v, c=c, aw=aw, W=W, pa=pa:
                                      v.scalar_tensor_tensor(
                    gated[c][0:aw, 0:W], pb[pa][0:aw, 0:W],
                    b1a_sb[0:aw, c:c + 1], gated[c][0:aw, 0:W],
                    ALU.add, ALU.mult), waits=[ge, tb["b1a"]]))
            CUR_TAG[0] = f'f{l}-w2'
            mmo = [None] * 8
            wall = bacq(list(range(8)))
            for c in range(NHC):
                tok_w2A = GD(lambda g, c=c, l=l: g.dma_start(
                    w2r[:, 0:512], ap("w2p")[l, c * 128:(c + 1) * 128, 0:512]),
                    waits=[tok_w2, tgd[c], mmo[3]])
                tok_w2 = GD(lambda g, c=c, l=l: g.dma_start(
                    w2r[:, 512:1024],
                    ap("w2p")[l, c * 128:(c + 1) * 128, 512:1024]),
                    waits=[tok_w2A, mmo[7]])
                for f in range(8):
                    mmo[f] = Tm(lambda t, c=c, f=f, W=W: t.matmul(
                        pb[f][:, 0:W], w2r[:, f * 128:(f + 1) * 128],
                        gated[c][:, 0:W], start=(c == 0), stop=(c == NHC - 1)),
                        waits=[tgd[c], tok_w2A if f < 4 else tok_w2, mmo[f]] +
                              (wall if c == 0 and f == 0 else []))
            for f in range(8):
                bwr([f], mmo[f])
            CUR_TAG[0] = f'f{l}-ep'
            if not final:
                for f in range(8):
                    e1 = brd([f], Vm(lambda v, f=f, u0=u0, W=W:
                                     v.scalar_tensor_tensor(
                        tmpB[:, 0:W], pb[f][:, 0:W], b2_sb[:, f:f + 1],
                        arena[f][:, u0 - in_base:u0 - in_base + W],
                        ALU.add, ALU.add),
                        waits=[mmo[f], tb["b2c"], tok_y[f],
                               tok_y[f - 1] if f else tok_y[7]]))
                    if u0 < L2F_LO:
                        tok_y[f] = Vm(lambda v, f=f, u0=u0, W=W: v.tensor_mul(
                            arena[f][:, u0 - out_base:u0 - out_base + W],
                            tmpB[:, 0:W], validB[:, 0:W]), waits=[e1, tvB])
                    else:
                        tok_y[f] = Vm(lambda v, f=f, u0=u0, W=W: v.tensor_copy(
                            arena[f][:, u0 - out_base:u0 - out_base + W],
                            tmpB[:, 0:W]), waits=[e1])
                t_pass2_last = tok_y[7]
            else:
                tst = [None, None]
                for f in range(8):
                    buf = tmpB if f % 2 == 0 else tmpC
                    tok_xa = Ym(lambda y, f=f, u0=u0, W=W: y.dma_start(
                        xaring[:, 0:W],
                        xa_d.ap()[f, :, u0 - L2F_LO:u0 - L2F_LO + W]),
                        waits=[tok_xa, tok_y[f - 1] if f else tok_y[7]] +
                              (prevb if u0 == lo and f == 0 else []))
                    e1 = brd([f], Vm(lambda v, f=f, u0=u0, W=W, buf=buf:
                                     v.scalar_tensor_tensor(
                        buf[:, 0:W], pb[f][:, 0:W], b2_sb[:, f:f + 1],
                        arena[f][:, u0 - in_base:u0 - in_base + W],
                        ALU.add, ALU.add),
                        waits=[mmo[f], tb["b2c"], tst[f % 2]]))
                    ty2 = Vm(lambda v, f=f, W=W, buf=buf:
                             v.scalar_tensor_tensor(
                        buf[:, 0:W], buf[:, 0:W], gbc[:, f:f + 1],
                        xaring[:, 0:W], ALU.mult, ALU.add),
                        waits=[e1, t_gb, tok_xa])
                    tok_y[f] = ty2
                    tst[f % 2] = Ym(lambda y, f=f, u0=u0, W=W, buf=buf:
                                    y.dma_start(
                        out_d.ap()[f, :, u0 - L2F_LO:u0 - L2F_LO + W],
                        buf[:, 0:W]), waits=[ty2])
                    tok_y[f] = tst[f % 2]
                t_pass2_last = tok_y[7]
        return barrier()

    if NPHASES >= 3:
        bar = ffn_phase(0, L1F_LO, 1536, 1920, bar, False)
    if NPHASES >= 4:
        bar = attn_phase(1, 2048, L2A_LO, 1920, 2560, bar)
    if NPHASES >= 5:
        bar = ffn_phase(1, L2F_LO, 2560, 0, bar, True)
    final_waits = barrier()

    with (
        nc.semaphore("sem_t") as sem_t,
        nc.semaphore("sem_v") as sem_v,
        nc.semaphore("sem_s") as sem_s,
        nc.semaphore("sem_g") as sem_g,
        nc.semaphore("sem_gd") as sem_gd,
        nc.semaphore("sem_y") as sem_y,
        nc.Block() as block,
    ):
        sems = {"t": sem_t, "v": sem_v, "s": sem_s, "g": sem_g, "gd": sem_gd,
                "y": sem_y}

        def emit_stream(eng_obj, eng_key):
            last = {}
            for (ek, emit, waits, inc) in ops:
                if ek != eng_key:
                    continue
                for (sname, val, peng) in waits:
                    if last.get(sname, 0) >= val:
                        continue
                    eng_obj.wait_ge(sems[sname], val)
                    last[sname] = val
                ins = emit(eng_obj)
                if inc is not None:
                    ins.then_inc(sems[inc[0]], inc[2])
            if eng_key == "Y":
                for (sname, val, _) in final_waits:
                    if sname == "y" and last.get("y", 0) < val:
                        eng_obj.wait_ge(sems["y"], val)

        @block.tensor
        def _(t):
            emit_stream(t, "T")

        @block.vector
        def _(v):
            emit_stream(v, "V")

        @block.scalar
        def _(s):
            emit_stream(s, "S")

        @block.gpsimd
        def _(g):
            emit_stream(g, "G")

        @block.sync
        def _(y):
            emit_stream(y, "Y")

    return nc


_TABCACHE = {}


def _host_tables(lo):
    if lo in _TABCACHE:
        return _TABCACHE[lo]
    import jax
    import jax.numpy as jnp
    with jax.default_device(jax.devices("cpu")[0]):
        half = jnp.arange(0, 64, 2, dtype=jnp.float32)
        invf = 1.0 / 10000.0 ** (half / 64)
        t = jnp.arange(lo + TAB_LO, lo + TAB_LO + TABW, dtype=jnp.float32)
        fr = t[:, None] * invf[None, :]
        base = (half + 0.4 * 64) / (1.4 * 64)
        power = (t - 4096) / float(256)
        sc32 = base[None, :] ** power[:, None]
        cos32, sin32 = jnp.cos(fr), jnp.sin(fr)
    cos = np.tile(np.asarray(cos32, np.float64), (1, 2))
    sin = np.tile(np.asarray(sin32, np.float64), (1, 2))
    sc = np.tile(np.asarray(sc32, np.float64), (1, 2))
    prods = [(cos * sc).T, (sin * sc).T, (cos / sc).T, (sin / sc).T]
    tabs_ = np.zeros((8, 128, TABW), np.float32)
    for i, p in enumerate(prods):
        hi = p.astype(np.float32)
        lo_ = (p - hi.astype(np.float64)).astype(np.float32)
        tabs_[i] = np.tile(hi, (2, 1))
        tabs_[4 + i] = np.tile(lo_, (2, 1))
    _TABCACHE[lo] = tabs_
    return tabs_


def kernel(L0_attn_x_seq, L0_attn_Wq, L0_attn_bq, L0_attn_Wk, L0_attn_bk,
           L0_attn_Wv, L0_attn_bv, L0_attn_gamma, L0_ffn_wqkv, L0_ffn_q_scale,
           L0_ffn_k_scale, L0_ffn_wo, L0_ffn_ln_g, L0_ffn_ln_b, L0_ffn_w1,
           L0_ffn_b1, L0_ffn_w2, L0_ffn_b2, L0_c, L0_wmod):
    f32 = np.float32
    x = np.asarray(L0_attn_x_seq, f32)
    nc = build_nc()

    def colz(v):
        return np.ascontiguousarray(np.asarray(v, f32).reshape(8, 128).T)

    w2p = np.zeros((2, HIDP, D), f32)
    w2p[:, :HID] = np.asarray(L0_ffn_w2, f32)
    b1 = np.asarray(L0_ffn_b1, f32)
    b1a = np.zeros((2, 128, NHC), f32)
    b1g = np.zeros((2, 128, NHC), f32)
    for l in range(2):
        pa = np.zeros(HIDP, f32)
        pa[:HID] = b1[l, :HID]
        pg = np.zeros(HIDP, f32)
        pg[:HID] = b1[l, HID:]
        b1a[l] = pa.reshape(NHC, 128).T
        b1g[l] = pg.reshape(NHC, 128).T
    qs = np.asarray(L0_ffn_q_scale, f32)
    ks = np.asarray(L0_ffn_k_scale, f32)

    def tile2(v):
        return np.ascontiguousarray(np.tile(v, 2).reshape(128, 1))

    def perm(v):
        return np.concatenate([v[32:], v[:32]])

    rotm = np.zeros((128, 128), f32)
    for d in range(128):
        h0 = (d // 64) * 64
        dl = d % 64
        rotm[h0 + (dl + 32) % 64, d] = -1.0 if dl < 32 else 1.0
    jj, ii = np.meshgrid(np.arange(128), np.arange(128), indexing="ij")
    m_up = np.where(jj >= ii, 0.0, -BIG).astype(f32)
    m_lo = np.where(jj <= ii, 0.0, -BIG).astype(f32)
    tri01 = (jj <= ii).astype(f32)
    bd2 = np.zeros((128, 2), f32)
    bd2[:64, 0] = 1.0
    bd2[64:, 1] = 1.0
    sel2 = np.zeros((2, 128), f32)
    sel2[0, :64] = 1.0
    sel2[1, 64:] = 1.0

    shared = {
        "wq": np.ascontiguousarray(np.asarray(L0_attn_Wq, f32).T),
        "wk": np.ascontiguousarray(np.asarray(L0_attn_Wk, f32).T),
        "wv": np.ascontiguousarray(np.asarray(L0_attn_Wv, f32).T),
        "bqc": colz(L0_attn_bq), "bkc": colz(L0_attn_bk),
        "bvr": np.asarray(L0_attn_bv, f32).reshape(1, D),
        "wqkv": np.ascontiguousarray(np.asarray(L0_ffn_wqkv, f32)),
        "wo": np.ascontiguousarray(np.asarray(L0_ffn_wo, f32)),
        "w1": np.ascontiguousarray(np.asarray(L0_ffn_w1, f32)),
        "w2p": w2p, "b1a": b1a, "b1g": b1g,
        "b2c": np.stack([colz(np.asarray(L0_ffn_b2, f32)[l]) for l in range(2)]),
        "lgc": np.stack([colz(np.asarray(L0_ffn_ln_g, f32)[l]) for l in range(2)]),
        "lbc": np.stack([colz(np.asarray(L0_ffn_ln_b, f32)[l]) for l in range(2)]),
        "qsc": np.stack([tile2(qs[l]) for l in range(2)]),
        "qspc": np.stack([tile2(perm(qs[l])) for l in range(2)]),
        "ksc": np.stack([tile2(ks[l]) for l in range(2)]),
        "kspc": np.stack([tile2(perm(ks[l])) for l in range(2)]),
        "wmod": np.ascontiguousarray(np.asarray(L0_wmod, f32)),
        "gam": np.asarray(L0_attn_gamma, f32).reshape(1, 1),
        "ident": np.eye(128, dtype=f32),
        "rotm": rotm, "m_up": m_up, "m_lo": m_lo, "tri01": tri01,
        "bd2": bd2, "sel2": sel2,
    }

    in_maps = []
    for c in range(8):
        b, qdx = c // 4, c % 4
        start = OWN * qdx
        lo = start - 3072
        xh = np.zeros((T, D), f32)
        s0 = max(0, lo)
        xh[s0 - lo:] = x[b, s0:start + OWN]
        upos = np.arange(lo, start + OWN)
        valid = (upos >= 0).astype(f32)
        m = dict(shared)
        m["xh"] = np.ascontiguousarray(xh.T.reshape(8, 128, T))
        m["crow"] = np.asarray(L0_c, f32)[b].reshape(1, D)
        m["tabs"] = _host_tables(lo)
        m["validr"] = np.ascontiguousarray(
            np.tile(valid[:VROW_W].reshape(1, VROW_W), (64, 1)))
        m["validc"] = np.ascontiguousarray(valid.reshape(T // 128, 128).T)
        in_maps.append(m)

    res = run_bass_kernel_spmd(nc, in_maps, core_ids=list(range(8)))
    out = np.zeros((2, 8192, D), f32)
    for c in range(8):
        b, qdx = c // 4, c % 4
        r = np.asarray(res.results[c]["out"]).reshape(D, OWN)
        out[b, OWN * qdx:OWN * (qdx + 1)] = r.T
    return out
